# version 02
# baseline (speedup 1.0000x reference)
import contextlib
import numpy as np
import ml_dtypes
import concourse.bass as bass
import concourse.mybir as mybir
from concourse.bass_utils import run_bass_kernel_spmd
from concourse.replica_groups import maybe_share_collective_output_space

F32 = mybir.dt.float32
BF16 = mybir.dt.bfloat16
AF = mybir.ActivationFunctionType
ALU = mybir.AluOpType
AX = mybir.AxisListType
BFNP = ml_dtypes.bfloat16

NCORES = 8
DM = 1024
DI = 2048
DS = 64
DH = 2048
NE = 8
DMC = DM // 128
DIC = DI // 128
LN_EPS = 1e-5


def _nm(t):
    n = getattr(t, "name", None)
    if isinstance(n, str):
        return n
    try:
        return t.name()
    except Exception:
        return str(t)


class _Stop(Exception):
    pass


def build_nc(tok=2048, upto=99):
    TT = 512
    NT = tok // TT
    NG = tok // 128

    nc = bass.Bass("TRN2", target_bir_lowering=False)

    xT = nc.dram_tensor("xT", [DM, tok], F32, kind="ExternalInput")
    W_in = nc.dram_tensor("W_in", [DM, 2 * DI], F32, kind="ExternalInput")
    b_in = nc.dram_tensor("b_in", [32, 128], F32, kind="ExternalInput")
    W_xp = nc.dram_tensor("W_xp", [DI, DS], F32, kind="ExternalInput")
    W_dt = nc.dram_tensor("W_dt", [DI, DM], F32, kind="ExternalInput")
    b_dt = nc.dram_tensor("b_dt", [8, 128], F32, kind="ExternalInput")
    W_out = nc.dram_tensor("W_out", [DI, DM], F32, kind="ExternalInput")
    b_out = nc.dram_tensor("b_out", [8, 128], F32, kind="ExternalInput")
    bxa = nc.dram_tensor("bxa", [2, 64], F32, kind="ExternalInput")
    Dv = nc.dram_tensor("Dv", [8, 128], F32, kind="ExternalInput")
    gate_W = nc.dram_tensor("gate_W", [DM, NE], F32, kind="ExternalInput")
    W1 = nc.dram_tensor("W1", [NE, DM, DH], F32, kind="ExternalInput")
    b1 = nc.dram_tensor("b1", [NE, 16, 128], F32, kind="ExternalInput")
    W2 = nc.dram_tensor("W2", [NE, DH, DM], F32, kind="ExternalInput")
    b2 = nc.dram_tensor("b2", [NE, 8, 128], F32, kind="ExternalInput")
    maskI = nc.dram_tensor("maskI", [64, 8], F32, kind="ExternalInput")
    idm = nc.dram_tensor("idm", [128, 128], F32, kind="ExternalInput")
    sel8 = nc.dram_tensor("sel8", [8, 1024], BF16, kind="ExternalInput")
    outT = nc.dram_tensor("outT", [DM, tok], F32, kind="ExternalOutput")

    xi_hbm = nc.dram_tensor("xi_hbm", [NT, 128, DIC * TT], BF16, kind="Internal")
    sr_hbm = nc.dram_tensor("sr_hbm", [NT, 128, DIC * TT], BF16, kind="Internal")
    xn_hbm = nc.dram_tensor("xn_hbm", [NT, 128, DMC * TT], BF16, kind="Internal")
    NTB = tok // 256
    h_hbm = nc.dram_tensor("h_hbm", [NTB, DMC, 128, 256], F32, kind="Internal")
    ssum_hbm = nc.dram_tensor("ssum_hbm", [128, tok], F32, kind="Internal")
    cw_hbm = nc.dram_tensor("cw_hbm", [8, tok], BF16, kind="Internal")
    xg_hbm = nc.dram_tensor("xg_hbm", [DMC, 128, tok], BF16, kind="Internal")
    rg = [list(range(NCORES))]
    space = maybe_share_collective_output_space("AllGather", rg)
    ag_in = nc.dram_tensor("ag_in", [2, 64], F32, kind="Internal")
    ag_out = nc.dram_tensor("ag_out", [16, 64], F32, kind="Internal",
                            addr_space=space)

    ENGS = ("tensor", "vector", "scalar", "sync", "gpsimd")
    ESEM = {"tensor": "t", "vector": "v", "scalar": "s"}
    streams = {e: [] for e in ENGS}
    cnt = {}
    wm = {e: {} for e in ENGS}
    SEM = {}
    last_w = {}
    readers = {}
    pending = {e: [] for e in ENGS}

    def _reg_read(name, ev):
        d = readers.setdefault(name, {})
        d[ev[0]] = max(d.get(ev[0], 0), ev[1])

    def _reg_write(name, ev):
        last_w[name] = ev
        readers[name] = {}

    def on(eng, f, r=(), w=(), waits=(), sem=None, amt=1, defer=False):
        rn = [_nm(t) for t in r]
        wn = [_nm(t) for t in w]
        auto = {}
        for n in rn:
            if n in last_w:
                s_, v_ = last_w[n]
                auto[s_] = max(auto.get(s_, 0), v_)
        for n in wn:
            if n in last_w:
                s_, v_ = last_w[n]
                auto[s_] = max(auto.get(s_, 0), v_)
            for s_, v_ in readers.get(n, {}).items():
                auto[s_] = max(auto.get(s_, 0), v_)
        for s_, v_ in waits:
            if v_ > 0:
                auto[s_] = max(auto.get(s_, 0), v_)
        if sem is None and not defer:
            sem = ESEM.get(eng)
            if sem is None:
                raise RuntimeError(f"need explicit sem for {eng}")
        wl = sorted(auto.items())

        def op(E):
            for s_, v_ in wl:
                if wm[eng].get(s_, 0) < v_:
                    E.wait_ge(SEM[s_], v_)
                    wm[eng][s_] = v_
            ins = f(E)
            if not defer:
                ins.then_inc(SEM[sem], amt)
        streams[eng].append(op)
        if defer:
            pending[eng].extend([("r", n) for n in rn] + [("w", n) for n in wn])
            return None
        cnt[sem] = cnt.get(sem, 0) + amt
        ev = (sem, cnt[sem])
        for kind, n in pending[eng]:
            (_reg_read if kind == "r" else _reg_write)(n, ev)
        pending[eng] = []
        for n in rn:
            _reg_read(n, ev)
        for n in wn:
            _reg_write(n, ev)
        return ev

    def now(s):
        return cnt.get(s, 0)

    def dma(eng, out_ap, in_ap, r, w, waits=()):
        q = "sy" if eng == "sync" else "gq"
        chain = [(q, cnt.get(q, 0))]
        return on(eng, lambda E: E.dma_start(out=out_ap, in_=in_ap),
                  r=r, w=w, waits=list(waits) + chain, sem=q, amt=16)

    def cview(sb, nch):
        return sb[:].rearrange("p (c j) -> p c j", c=nch)

    def wload(eng, dst_sb, src2d, nch, parts, r, w):
        s3 = src2d.rearrange("(c p) j -> p c j", p=128)
        d3 = cview(dst_sb, nch)
        step = nch // parts
        for i in range(parts):
            dma(eng, d3[:, i * step:(i + 1) * step, :],
                s3[:, i * step:(i + 1) * step, :], r=r, w=w)

    def barrier():
        on("vector", lambda E: E.memzero(FLUSH[0:1, 0:4]), w=[FLUSH])
        on("scalar", lambda E: E.memzero(FLUSH[0:1, 4:8]), w=[FLUSH])
        snap = {s: now(s) for s in ("t", "v", "s", "sy", "gq", "cc")}
        own = {"tensor": "t", "vector": "v", "scalar": "s"}
        for eng in ENGS:
            wl = [(s, v) for s, v in snap.items() if v > 0 and own.get(eng) != s]
            on(eng, lambda E: E.nop(), waits=wl, defer=True)

    with contextlib.ExitStack() as STK:
        for s in ("t", "v", "s", "sy", "gq", "cc"):
            SEM[s] = STK.enter_context(nc.semaphore("sem_" + s))
            cnt[s] = 0

        L = lambda *a: STK.enter_context(nc.sbuf_tensor(*a, side="left"))
        FLUSH = L("FLUSH", [1, 8], F32)
        ID = L("ID", [128, 128], F32)
        ONESB = L("ONESB", [128, 128], BF16)
        MASK = L("MASK", [64, 8], F32)
        BINT = L("BINT", [128, 32], F32)
        BDTT = L("BDTT", [128, 8], F32)
        BOUT = L("BOUT", [128, 8], F32)
        DT = L("DT", [128, 8], F32)
        BXAT = L("BXAT", [64, 2], F32)
        ASC = L("ASC", [64, 1], F32)
        GWF = L("GWF", [128, 8 * 8], F32)
        BSTG1 = L("BSTG1", [32, 128], F32)
        BSTG2 = L("BSTG2", [8, 128], F32)
        BSTG3 = L("BSTG3", [8, 128], F32)
        BSTG4 = L("BSTG4", [8, 128], F32)
        BSTG5 = L("BSTG5", [2, 64], F32)

        dma("gpsimd", ID[:], idm[:], r=[idm], w=[ID])
        dma("gpsimd", MASK[:], maskI[:], r=[maskI], w=[MASK])
        dma("gpsimd", BSTG1[:], b_in[:], r=[b_in], w=[BSTG1])
        dma("gpsimd", BSTG2[:], b_dt[:], r=[b_dt], w=[BSTG2])
        dma("gpsimd", BSTG3[:], b_out[:], r=[b_out], w=[BSTG3])
        dma("gpsimd", BSTG4[:], Dv[:], r=[Dv], w=[BSTG4])
        dma("gpsimd", BSTG5[:], bxa[:], r=[bxa], w=[BSTG5])
        dma("sync", cview(GWF, DMC),
            gate_W.ap().rearrange("(c p) e -> p c e", p=128), r=[gate_W], w=[GWF])
        on("vector", lambda E: E.memset(ONESB[:], 1.0), w=[ONESB])

        with contextlib.ExitStack() as P0:
            PT = P0.enter_context(nc.psum_tensor("PT0", [128, 32], F32))
            for (stg, dst) in ((BSTG1, BINT), (BSTG2, BDTT), (BSTG3, BOUT),
                               (BSTG4, DT), (BSTG5, BXAT)):
                p, q = stg.shape[0], stg.shape[1]
                on("tensor",
                   (lambda stg, p, q: lambda E: E.transpose(
                       PT[0:q, 0:p], stg[:], ID[0:p, 0:p]))(stg, p, q),
                   r=[stg, ID], w=[PT])
                on("vector",
                   (lambda dst, p, q: lambda E: E.tensor_copy(
                       dst[0:q, 0:p], PT[0:q, 0:p]))(dst, p, q),
                   r=[PT], w=[dst])
            on("scalar", lambda E: E.activation(ASC[:], BXAT[:, 1:2], AF.Exp),
               r=[BXAT], w=[ASC])
            on("vector", lambda E: E.tensor_scalar(
                ASC[:], ASC[:], -1.0 / DM, None, ALU.mult), r=[ASC], w=[ASC])
            barrier()

        try:
            if upto < 1:
                raise _Stop()
            with contextlib.ExitStack() as PA:
                RA = lambda *a: PA.enter_context(nc.sbuf_tensor(*a, side="right"))
                WIN = RA("WIN", [128, DMC * 4096], BF16)
                XTf = [RA("XTf0", [128, DMC * TT], F32), RA("XTf1", [128, DMC * TT], F32)]
                XB = RA("XB", [128, DMC * TT], BF16)
                SQ = RA("SQ", [128, DMC * TT], BF16)
                XN = RA("XN", [128, DMC * TT], BF16)
                XIT = RA("XIT", [128, DIC * TT], BF16)
                SRT = RA("SRT", [128, DIC * TT], BF16)
                MT = RA("MT", [128, TT], F32)
                VT = RA("VT", [128, TT], F32)
                T1 = RA("T1", [128, TT], F32)
                RT = RA("RT", [128, TT], F32)
                PS_S = PA.enter_context(nc.psum_tensor("PS_S", [128, TT], F32))
                PS_Q = PA.enter_context(nc.psum_tensor("PS_Q", [128, TT], F32))
                PX = [PA.enter_context(nc.psum_tensor("PXa", [128, TT], F32)),
                      PA.enter_context(nc.psum_tensor("PXb", [128, TT], F32))]

                wload("gpsimd", WIN, W_in.ap(), DMC, 2, r=[W_in], w=[WIN])

                for ti in range(NT):
                    XTt = XTf[ti % 2]
                    dma("sync", cview(XTt, DMC),
                        xT[:, ti * TT:(ti + 1) * TT].rearrange("(c p) t -> p c t", p=128),
                        r=[xT], w=[XTt])
                    for c in range(DMC):
                        on("vector", (lambda c, XTt: lambda E: E.tensor_copy(
                            XB[:, c * TT:(c + 1) * TT], XTt[:, c * TT:(c + 1) * TT]))(c, XTt),
                            r=[XTt], w=[XB])
                    for c in range(DMC):
                        on("scalar", (lambda c: lambda E: E.activation(
                            SQ[:, c * TT:(c + 1) * TT], XB[:, c * TT:(c + 1) * TT],
                            AF.Square))(c), r=[XB], w=[SQ])
                    for c in range(DMC):
                        on("tensor", (lambda c: lambda E: E.matmul(
                            PS_S[:], ONESB[:], XB[:, c * TT:(c + 1) * TT],
                            start=(c == 0), stop=(c == DMC - 1)))(c),
                            r=[ONESB, XB], w=[PS_S], defer=(c < DMC - 1))
                    for c in range(DMC):
                        on("tensor", (lambda c: lambda E: E.matmul(
                            PS_Q[:], ONESB[:], SQ[:, c * TT:(c + 1) * TT],
                            start=(c == 0), stop=(c == DMC - 1)))(c),
                            r=[ONESB, SQ], w=[PS_Q], defer=(c < DMC - 1))
                    on("vector", lambda E: E.tensor_scalar(
                        MT[:], PS_S[:], 1.0 / DM, None, ALU.mult), r=[PS_S], w=[MT])
                    on("vector", lambda E: E.tensor_scalar(
                        VT[:], PS_Q[:], 1.0 / DM, None, ALU.mult), r=[PS_Q], w=[VT])
                    on("vector", lambda E: E.tensor_mul(T1[:], MT[:], MT[:]),
                       r=[MT], w=[T1])
                    on("vector", lambda E: E.tensor_sub(VT[:], VT[:], T1[:]),
                       r=[VT, T1], w=[VT])
                    on("vector", lambda E: E.tensor_scalar(
                        VT[:], VT[:], LN_EPS, None, ALU.add), r=[VT], w=[VT])
                    on("vector", lambda E: E.reciprocal(T1[:], VT[:]), r=[VT], w=[T1])
                    on("scalar", lambda E: E.activation(RT[:], T1[:], AF.Sqrt),
                       r=[T1], w=[RT])
                    for c in range(DMC):
                        on("vector", (lambda c, XTt: lambda E: E.tensor_sub(
                            VT[:], XTt[:, c * TT:(c + 1) * TT], MT[:]))(c, XTt),
                            r=[XTt, MT], w=[VT])
                        on("vector", (lambda c: lambda E: E.tensor_mul(
                            XN[:, c * TT:(c + 1) * TT], VT[:], RT[:]))(c),
                            r=[VT, RT], w=[XN])
                    for j in range(32):
                        pb = j % 2
                        for c in range(DMC):
                            on("tensor", (lambda j, c, pb: lambda E: E.matmul(
                                PX[pb][:],
                                WIN[:, c * 4096 + j * 128:c * 4096 + (j + 1) * 128],
                                XN[:, c * TT:(c + 1) * TT],
                                start=(c == 0), stop=(c == DMC - 1)))(j, c, pb),
                                r=[WIN, XN], w=[PX[pb]], defer=(c < DMC - 1))
                        dst = XIT if j < 16 else SRT
                        jj = j if j < 16 else j - 16
                        on("scalar", (lambda j, jj, dst, pb: lambda E: E.activation(
                            dst[:, jj * TT:(jj + 1) * TT], PX[pb][:], AF.Silu,
                            bias=BINT[:, j:j + 1]))(j, jj, dst, pb),
                            r=[PX[pb], BINT], w=[dst])
                    dma("sync", xi_hbm[ti], XIT[:], r=[XIT], w=[xi_hbm])
                    dma("sync", sr_hbm[ti], SRT[:], r=[SRT], w=[sr_hbm])
                    dma("sync", xn_hbm[ti], XN[:], r=[XN], w=[xn_hbm])
                barrier()

            if upto < 2:
                raise _Stop()
            with contextlib.ExitStack() as PB:
                RB = lambda *a: PB.enter_context(nc.sbuf_tensor(*a, side="right"))
                WDT = RB("WDT", [128, DIC * 1024], BF16)
                WXP = RB("WXP", [128, DIC * 64], BF16)
                BX = RB("BX", [64, tok], F32)
                AFULL = RB("AFULL", [64, tok], F32)
                XIJ = [RB("XIJ0", [128, DIC * TT], BF16), RB("XIJ1", [128, DIC * TT], BF16)]
                SPB = RB("SPB", [128, DMC * TT], BF16)
                TA = RB("TA", [128, TT], F32)
                TB = RB("TB", [128, TT], F32)
                TC = RB("TC", [128, TT], F32)
                TD = RB("TD", [128, TT], F32)
                PD = [PB.enter_context(nc.psum_tensor("PDa", [128, TT], F32)),
                      PB.enter_context(nc.psum_tensor("PDb", [128, TT], F32))]
                PMD = PB.enter_context(nc.psum_tensor("PMD", [128, TT], F32))
                PBX = PB.enter_context(nc.psum_tensor("PBX", [64, TT], F32))
                PSS = PB.enter_context(nc.psum_tensor("PSS", [128, TT], F32))
                PSC = PB.enter_context(nc.psum_tensor("PSC", [128, 128], F32))

                A2L = (upto - 210) if 210 <= upto < 220 else 9
                wload("gpsimd", WDT, W_dt.ap(), DIC, 4, r=[W_dt], w=[WDT])
                wload("gpsimd", WXP, W_xp.ap(), DIC, 1, r=[W_xp], w=[WXP])

                for ti in range(NT):
                    XJ = XIJ[ti % 2]
                    dma("sync", XJ[:], xi_hbm[ti], r=[xi_hbm], w=[XJ])
                    for j8 in range(DMC if A2L >= 2 else 0):
                        pb = j8 % 2
                        for k in range(DIC):
                            on("tensor", (lambda j8, k, pb, XJ: lambda E: E.matmul(
                                PD[pb][:],
                                WDT[:, k * 1024 + j8 * 128:k * 1024 + (j8 + 1) * 128],
                                XJ[:, k * TT:(k + 1) * TT],
                                start=(k == 0), stop=(k == DIC - 1)))(j8, k, pb, XJ),
                                r=[WDT, XJ], w=[PD[pb]], defer=(k < DIC - 1))
                        on("scalar", (lambda j8, pb: lambda E: E.activation(
                            TA[:], PD[pb][:], AF.Abs, bias=BDTT[:, j8:j8 + 1]))(j8, pb),
                            r=[PD[pb], BDTT], w=[TA])
                        on("scalar", lambda E: E.activation(
                            TB[:], TA[:], AF.Exp, scale=-1.0), r=[TA], w=[TB])
                        on("scalar", lambda E: E.activation(
                            TC[:], TB[:], AF.Ln, bias=1.0), r=[TB], w=[TC])
                        on("scalar", (lambda j8, pb: lambda E: E.activation(
                            TD[:], PD[pb][:], AF.Relu,
                            bias=BDTT[:, j8:j8 + 1]))(j8, pb),
                            r=[PD[pb], BDTT], w=[TD])
                        on("vector", (lambda j8: lambda E: E.tensor_add(
                            SPB[:, j8 * TT:(j8 + 1) * TT], TD[:], TC[:]))(j8),
                            r=[TD, TC], w=[SPB])
                        on("tensor", (lambda j8: lambda E: E.matmul(
                            PMD[:], ONESB[:], SPB[:, j8 * TT:(j8 + 1) * TT],
                            start=(j8 == 0), stop=(j8 == DMC - 1)))(j8),
                            r=[ONESB, SPB], w=[PMD], defer=(j8 < DMC - 1))
                    if A2L >= 3:
                        on("scalar", (lambda ti: lambda E: E.activation(
                            AFULL[:, ti * TT:(ti + 1) * TT], PMD[0:64, :], AF.Exp,
                            scale=ASC[:, 0:1]))(ti),
                            r=[PMD, ASC], w=[AFULL])
                    for k in range(DIC if A2L >= 4 else 0):
                        on("tensor", (lambda k, XJ: lambda E: E.matmul(
                            PBX[:], WXP[:, 64 * k:64 * (k + 1)],
                            XJ[:, k * TT:(k + 1) * TT],
                            start=(k == 0), stop=(k == DIC - 1)))(k, XJ),
                            r=[WXP, XJ], w=[PBX], defer=(k < DIC - 1))
                    if A2L >= 4:
                        on("vector", (lambda ti: lambda E: E.tensor_scalar(
                            BX[:, ti * TT:(ti + 1) * TT], PBX[:], BXAT[:, 0:1], None,
                            ALU.add))(ti),
                            r=[PBX, BXAT], w=[BX])

                if upto < 22:
                    raise _Stop()
                ZER = RB("ZER", [64, tok], F32)
                STATE = RB("STATE", [64, tok], F32)
                CP = RB("CP", [64, tok], F32)
                STCB = RB("STCB", [64, tok], BF16)
                SUM2 = RB("SUM2", [64, 2], F32)
                SUM2T = RB("SUM2T", [2, 64], F32)
                AGSB = RB("AGSB", [16, 64], F32)
                AGT = RB("AGT", [64, 16], F32)
                SIN = RB("SIN", [64, 1], F32)
                TF = RB("TF", [64, 1], F32)
                SSB = RB("SSB", [128, tok], F32)

                on("vector", lambda E: E.memset(ZER[:], 0.0), w=[ZER])
                on("vector", lambda E: E.tensor_tensor_scan(
                    STATE[:], AFULL[:], BX[:], 0.0, ALU.mult, ALU.add),
                    r=[AFULL, BX], w=[STATE])
                on("vector", lambda E: E.tensor_tensor_scan(
                    CP[:], AFULL[:], ZER[:], 1.0, ALU.mult, ALU.add),
                    r=[AFULL, ZER], w=[CP])
                on("vector", lambda E: E.tensor_copy(SUM2[:, 0:1], CP[:, tok - 1:tok]),
                   r=[CP], w=[SUM2])
                on("vector", lambda E: E.tensor_copy(SUM2[:, 1:2], STATE[:, tok - 1:tok]),
                   r=[STATE], w=[SUM2])
                on("tensor", lambda E: E.transpose(
                    PSC[0:2, 0:64], SUM2[:], ID[0:64, 0:64]), r=[SUM2, ID], w=[PSC])
                on("vector", lambda E: E.tensor_copy(SUM2T[:], PSC[0:2, 0:64]),
                   r=[PSC], w=[SUM2T])
                if upto < 23:
                    raise _Stop()
                dma("gpsimd", ag_in[:], SUM2T[:], r=[SUM2T], w=[ag_in])
                on("gpsimd", lambda E: E.collective_compute(
                    "AllGather", ALU.bypass, replica_groups=rg,
                    ins=[ag_in[:]], outs=[ag_out[:]]),
                    r=[ag_in], w=[ag_out], sem="cc", amt=1)
                dma("gpsimd", AGSB[:], ag_out[:], r=[ag_out], w=[AGSB])
                on("tensor", lambda E: E.transpose(
                    PSC[0:64, 0:16], AGSB[:], ID[0:16, 0:16]), r=[AGSB, ID], w=[PSC])
                on("vector", lambda E: E.tensor_copy(AGT[:], PSC[0:64, 0:16]),
                   r=[PSC], w=[AGT])
                if upto < 24:
                    raise _Stop()
                on("vector", lambda E: E.memset(SIN[:], 0.0), w=[SIN])
                for k in range(NCORES):
                    on("vector", (lambda k: lambda E: E.scalar_tensor_tensor(
                        TF[:], SIN[:], AGT[:, 2 * k:2 * k + 1],
                        AGT[:, 2 * k + 1:2 * k + 2], ALU.mult, ALU.add))(k),
                        r=[SIN, AGT], w=[TF])
                    on("vector", lambda E: E.tensor_sub(TF[:], TF[:], SIN[:]),
                       r=[TF, SIN], w=[TF])
                    on("vector", (lambda k: lambda E: E.tensor_mul(
                        TF[:], TF[:], MASK[:, k:k + 1]))(k), r=[TF, MASK], w=[TF])
                    on("vector", lambda E: E.tensor_add(SIN[:], SIN[:], TF[:]),
                       r=[SIN, TF], w=[SIN])
                on("vector", lambda E: E.scalar_tensor_tensor(
                    STATE[:], CP[:], SIN[:, 0:1], STATE[:], ALU.mult, ALU.add),
                    r=[CP, SIN, STATE], w=[STATE])
                on("vector", lambda E: E.tensor_copy(STCB[:], STATE[:]),
                   r=[STATE], w=[STCB])
                if upto < 25:
                    raise _Stop()
                for i in range(NT):
                    on("tensor", (lambda i: lambda E: E.matmul(
                        PSS[:], ONESB[0:64, :], STCB[:, i * TT:(i + 1) * TT],
                        start=True, stop=True))(i),
                        r=[ONESB, STCB], w=[PSS])
                    on("vector", (lambda i: lambda E: E.tensor_copy(
                        SSB[:, i * TT:(i + 1) * TT], PSS[:]))(i), r=[PSS], w=[SSB])
                dma("sync", ssum_hbm[:], SSB[:], r=[SSB], w=[ssum_hbm])
                barrier()

            if upto < 3:
                raise _Stop()
            ACC = L("ACC", [128, DMC * tok], F32)
            TTB = 256

            with contextlib.ExitStack() as PC:
                RC = lambda *a: PC.enter_context(nc.sbuf_tensor(*a, side="right"))
                WOUT = RC("WOUT", [128, DIC * 1024], BF16)
                XIJb = RC("XIJb", [128, DIC * TTB], BF16)
                SRJ = RC("SRJ", [128, DIC * TTB], BF16)
                XTb = RC("XTb", [128, DMC * TTB], F32)
                XNb = RC("XNb", [128, DMC * TTB], BF16)
                MIX = RC("MIX", [128, DMC * TTB], F32)
                MIXB = RC("MIXB", [128, DMC * TTB], BF16)
                HT = RC("HT", [128, DMC * TTB], F32)
                HBB = RC("HBB", [128, DMC * TTB], BF16)
                SQB = RC("SQB", [128, DMC * TTB], BF16)
                XGT = SQB
                SSBb = RC("SSBb", [128, TTB], F32)
                LGF = RC("LGF", [8, tok], F32)
                MT2 = RC("MT2", [128, TTB], F32)
                VT2 = RC("VT2", [128, TTB], F32)
                T12 = RC("T12", [128, TTB], F32)
                RT2 = RC("RT2", [128, TTB], F32)
                M3T = RC("M3T", [128, TTB], F32)
                R3T = RC("R3T", [128, TTB], F32)
                PY = [PC.enter_context(nc.psum_tensor("PYa", [128, TTB], F32)),
                      PC.enter_context(nc.psum_tensor("PYb", [128, TTB], F32))]
                PS2 = PC.enter_context(nc.psum_tensor("PS2", [128, TTB], F32))
                PQ2 = PC.enter_context(nc.psum_tensor("PQ2", [128, TTB], F32))
                PLG = PC.enter_context(nc.psum_tensor("PLG", [8, TTB], F32))

                wload("gpsimd", WOUT, W_out.ap(), DIC, 4, r=[W_out], w=[WOUT])

                def lnstats(src_bf, n_chunks, mdst, rdst):
                    for c in range(n_chunks):
                        on("scalar", (lambda c: lambda E: E.activation(
                            SQB[:, c * TTB:(c + 1) * TTB],
                            src_bf[:, c * TTB:(c + 1) * TTB],
                            AF.Square))(c), r=[src_bf], w=[SQB])
                    for c in range(n_chunks):
                        on("tensor", (lambda c: lambda E: E.matmul(
                            PS2[:], ONESB[:], src_bf[:, c * TTB:(c + 1) * TTB],
                            start=(c == 0), stop=(c == n_chunks - 1)))(c),
                            r=[ONESB, src_bf], w=[PS2], defer=(c < n_chunks - 1))
                    for c in range(n_chunks):
                        on("tensor", (lambda c: lambda E: E.matmul(
                            PQ2[:], ONESB[:], SQB[:, c * TTB:(c + 1) * TTB],
                            start=(c == 0), stop=(c == n_chunks - 1)))(c),
                            r=[ONESB, SQB], w=[PQ2], defer=(c < n_chunks - 1))
                    on("vector", (lambda: lambda E: E.tensor_scalar(
                        mdst[:], PS2[:], 1.0 / (n_chunks * 128), None, ALU.mult))(),
                        r=[PS2], w=[mdst])
                    on("vector", (lambda: lambda E: E.tensor_scalar(
                        VT2[:], PQ2[:], 1.0 / (n_chunks * 128), None, ALU.mult))(),
                        r=[PQ2], w=[VT2])
                    on("vector", (lambda: lambda E: E.tensor_mul(
                        T12[:], mdst[:], mdst[:]))(), r=[mdst], w=[T12])
                    on("vector", lambda E: E.tensor_sub(VT2[:], VT2[:], T12[:]),
                       r=[VT2, T12], w=[VT2])
                    on("vector", lambda E: E.tensor_scalar(
                        VT2[:], VT2[:], LN_EPS, None, ALU.add), r=[VT2], w=[VT2])
                    on("vector", lambda E: E.reciprocal(T12[:], VT2[:]),
                       r=[VT2], w=[T12])
                    on("scalar", (lambda: lambda E: E.activation(
                        rdst[:], T12[:], AF.Sqrt))(), r=[T12], w=[rdst])

                for tb in range(NTB):
                    ti, hf = tb // 2, tb % 2
                    dma("sync", XIJb[:].rearrange("p (k t) -> p k t", k=DIC),
                        xi_hbm[ti].rearrange("p (k t) -> p k t", k=DIC)
                        [:, :, hf * TTB:(hf + 1) * TTB],
                        r=[xi_hbm], w=[XIJb])
                    dma("sync", SRJ[:].rearrange("p (k t) -> p k t", k=DIC),
                        sr_hbm[ti].rearrange("p (k t) -> p k t", k=DIC)
                        [:, :, hf * TTB:(hf + 1) * TTB],
                        r=[sr_hbm], w=[SRJ])
                    dma("sync", cview(XTb, DMC),
                        xT[:, tb * TTB:(tb + 1) * TTB].rearrange(
                            "(c p) t -> p c t", p=128),
                        r=[xT], w=[XTb])
                    dma("sync", SSBb[:], ssum_hbm[:, tb * TTB:(tb + 1) * TTB],
                        r=[ssum_hbm], w=[SSBb])
                    dma("sync", XNb[:].rearrange("p (c t) -> p c t", c=DMC),
                        xn_hbm[ti].rearrange("p (c t) -> p c t", c=DMC)
                        [:, :, hf * TTB:(hf + 1) * TTB],
                        r=[xn_hbm], w=[XNb])
                    for k in range(DIC):
                        on("vector", (lambda k: lambda E: E.tensor_mul(
                            VT2[:], SSBb[:], SRJ[:, k * TTB:(k + 1) * TTB]))(k),
                            r=[SSBb, SRJ], w=[VT2])
                        on("vector", (lambda k: lambda E: E.tensor_mul(
                            XIJb[:, k * TTB:(k + 1) * TTB], VT2[:],
                            XIJb[:, k * TTB:(k + 1) * TTB]))(k),
                            r=[VT2, XIJb], w=[XIJb])
                    for j8 in range(DMC if A2L >= 2 else 0):
                        pb = j8 % 2
                        for k in range(DIC):
                            on("tensor", (lambda j8, k, pb: lambda E: E.matmul(
                                PY[pb][:],
                                WOUT[:, k * 1024 + j8 * 128:k * 1024 + (j8 + 1) * 128],
                                XIJb[:, k * TTB:(k + 1) * TTB],
                                start=(k == 0), stop=(k == DIC - 1)))(j8, k, pb),
                                r=[WOUT, XIJb], w=[PY[pb]], defer=(k < DIC - 1))
                        on("vector", (lambda j8, pb: lambda E: E.tensor_scalar(
                            T12[:], PY[pb][:], BOUT[:, j8:j8 + 1], None, ALU.add))(j8, pb),
                            r=[PY[pb], BOUT], w=[T12])
                        on("vector", (lambda j8: lambda E: E.scalar_tensor_tensor(
                            MIX[:, j8 * TTB:(j8 + 1) * TTB],
                            XNb[:, j8 * TTB:(j8 + 1) * TTB],
                            DT[:, j8:j8 + 1], T12[:], ALU.mult, ALU.add))(j8),
                            r=[XNb, DT, T12], w=[MIX])
                    for c in range(DMC):
                        on("vector", (lambda c: lambda E: E.tensor_copy(
                            MIXB[:, c * TTB:(c + 1) * TTB],
                            MIX[:, c * TTB:(c + 1) * TTB]))(c), r=[MIX], w=[MIXB])
                    lnstats(MIXB, DMC, MT2, RT2)
                    for c in range(DMC):
                        on("vector", (lambda c: lambda E: E.tensor_sub(
                            VT2[:], MIX[:, c * TTB:(c + 1) * TTB], MT2[:]))(c),
                            r=[MIX, MT2], w=[VT2])
                        on("vector", lambda E: E.tensor_mul(VT2[:], VT2[:], RT2[:]),
                           r=[VT2, RT2], w=[VT2])
                        on("vector", (lambda c: lambda E: E.tensor_add(
                            HT[:, c * TTB:(c + 1) * TTB], VT2[:],
                            XTb[:, c * TTB:(c + 1) * TTB]))(c),
                            r=[VT2, XTb], w=[HT])
                        on("vector", (lambda c: lambda E: E.tensor_copy(
                            HBB[:, c * TTB:(c + 1) * TTB],
                            HT[:, c * TTB:(c + 1) * TTB]))(c), r=[HT], w=[HBB])
                    dma("sync", h_hbm[tb].rearrange("c p t -> p c t"), cview(HT, DMC),
                        r=[HT], w=[h_hbm])
                    lnstats(HBB, DMC, M3T, R3T)
                    for c in range(DMC):
                        on("vector", (lambda c: lambda E: E.tensor_sub(
                            VT2[:], HT[:, c * TTB:(c + 1) * TTB], M3T[:]))(c),
                            r=[HT, M3T], w=[VT2])
                        on("vector", (lambda c, tb: lambda E: E.tensor_mul(
                            ACC[:, c * tok + tb * TTB:c * tok + (tb + 1) * TTB],
                            VT2[:], R3T[:]))(c, tb), r=[VT2, R3T], w=[ACC])
                        on("vector", (lambda c, tb: lambda E: E.tensor_copy(
                            XGT[:, c * TTB:(c + 1) * TTB],
                            ACC[:, c * tok + tb * TTB:c * tok + (tb + 1) * TTB]))(c, tb),
                            r=[ACC], w=[XGT])
                    dma("sync",
                        xg_hbm[:, :, tb * TTB:(tb + 1) * TTB].rearrange("c p t -> p c t"),
                        cview(XGT, DMC), r=[XGT], w=[xg_hbm])
                    for c in range(DMC):
                        on("tensor", (lambda c, tb: lambda E: E.matmul(
                            PLG[:], GWF[:, 8 * c:8 * (c + 1)],
                            ACC[:, c * tok + tb * TTB:c * tok + (tb + 1) * TTB],
                            start=(c == 0), stop=(c == DMC - 1)))(c, tb),
                            r=[GWF, ACC], w=[PLG], defer=(c < DMC - 1))
                    on("vector", (lambda tb: lambda E: E.tensor_copy(
                        LGF[:, tb * TTB:(tb + 1) * TTB], PLG[:]))(tb), r=[PLG], w=[LGF])

                CWFMB = RC("CWFMB", [8, tok], BF16)
                CWG = RC("CWG", [128, NG * 8], F32)
                EG = RC("EG", [128, NG * 8], F32)
                GG = RC("GG", [128, NG * 8], F32)
                K1 = RC("K1", [128, NG * 8], F32)
                GS = RC("GS", [128, NG * 8], F32)
                M1 = RC("M1", [128, NG], F32)
                M1N = RC("M1N", [128, NG], F32)
                S1 = RC("S1", [128, NG], F32)
                R1 = RC("R1", [128, NG], F32)
                M2 = RC("M2", [128, NG], F32)
                M3g = RC("M3g", [128, NG], F32)
                S2 = RC("S2", [128, NG], F32)
                R2 = RC("R2", [128, NG], F32)
                PGT = PC.enter_context(nc.psum_tensor("PGT", [128, NG * 8], F32))
                PGB = [PC.enter_context(nc.psum_tensor("PGB0", [8, 128], F32)),
                       PC.enter_context(nc.psum_tensor("PGB1", [8, 128], F32))]

                for g in range(NG):
                    on("tensor", (lambda g: lambda E: E.transpose(
                        PGT[:, 8 * g:8 * (g + 1)], LGF[:, 128 * g:128 * (g + 1)],
                        ID[0:8, 0:8]))(g),
                        r=[LGF, ID], w=[PGT], defer=(g < NG - 1))
                on("vector", lambda E: E.tensor_copy(CWG[:], PGT[:]), r=[PGT], w=[CWG])
                for g in range(NG):
                    on("vector", (lambda g: lambda E: E.reduce_max(
                        M1[:, g:g + 1], CWG[:, 8 * g:8 * (g + 1)], AX.X))(g),
                        r=[CWG], w=[M1])
                on("vector", lambda E: E.tensor_scalar(
                    M1N[:], M1[:], -1.0, None, ALU.mult), r=[M1], w=[M1N])
                for g in range(NG):
                    on("scalar", (lambda g: lambda E: E.activation(
                        EG[:, 8 * g:8 * (g + 1)], CWG[:, 8 * g:8 * (g + 1)], AF.Exp,
                        bias=M1N[:, g:g + 1]))(g), r=[CWG, M1N], w=[EG])
                for g in range(NG):
                    on("vector", (lambda g: lambda E: E.reduce_sum(
                        S1[:, g:g + 1], EG[:, 8 * g:8 * (g + 1)], AX.X))(g),
                        r=[EG], w=[S1])
                on("vector", lambda E: E.reciprocal(R1[:], S1[:]), r=[S1], w=[R1])
                for g in range(NG):
                    on("vector", (lambda g: lambda E: E.tensor_scalar(
                        GG[:, 8 * g:8 * (g + 1)], EG[:, 8 * g:8 * (g + 1)],
                        R1[:, g:g + 1], None, ALU.mult))(g), r=[EG, R1], w=[GG])
                for g in range(NG):
                    on("vector", (lambda g: lambda E: E.reduce_max(
                        M2[:, g:g + 1], GG[:, 8 * g:8 * (g + 1)], AX.X))(g),
                        r=[GG], w=[M2])
                for g in range(NG):
                    on("vector", (lambda g: lambda E: E.tensor_scalar(
                        K1[:, 8 * g:8 * (g + 1)], GG[:, 8 * g:8 * (g + 1)],
                        M2[:, g:g + 1], None, ALU.is_ge))(g), r=[GG, M2], w=[K1])
                on("vector", lambda E: E.scalar_tensor_tensor(
                    K1[:], K1[:], -1e9, GG[:], ALU.mult, ALU.add), r=[K1, GG], w=[K1])
                for g in range(NG):
                    on("vector", (lambda g: lambda E: E.reduce_max(
                        M3g[:, g:g + 1], K1[:, 8 * g:8 * (g + 1)], AX.X))(g),
                        r=[K1], w=[M3g])
                for g in range(NG):
                    on("vector", (lambda g: lambda E: E.tensor_scalar(
                        K1[:, 8 * g:8 * (g + 1)], GG[:, 8 * g:8 * (g + 1)],
                        M3g[:, g:g + 1], None, ALU.is_ge))(g), r=[GG, M3g], w=[K1])
                on("vector", lambda E: E.tensor_mul(GS[:], GG[:], K1[:]),
                   r=[GG, K1], w=[GS])
                for g in range(NG):
                    on("vector", (lambda g: lambda E: E.reduce_sum(
                        S2[:, g:g + 1], GS[:, 8 * g:8 * (g + 1)], AX.X))(g),
                        r=[GS], w=[S2])
                on("vector", lambda E: E.tensor_scalar(
                    S2[:], S2[:], 1e-8, None, ALU.add), r=[S2], w=[S2])
                on("vector", lambda E: E.reciprocal(R2[:], S2[:]), r=[S2], w=[R2])
                for g in range(NG):
                    on("vector", (lambda g: lambda E: E.tensor_scalar(
                        GS[:, 8 * g:8 * (g + 1)], GS[:, 8 * g:8 * (g + 1)],
                        R2[:, g:g + 1], None, ALU.mult))(g), r=[GS, R2], w=[GS])
                for g in range(NG):
                    gb = PGB[g % 2]
                    on("tensor", (lambda g, gb: lambda E: E.transpose(
                        gb[:], GS[:, 8 * g:8 * (g + 1)], ID[:]))(g, gb),
                        r=[GS, ID], w=[gb])
                    on("vector", (lambda g, gb: lambda E: E.tensor_copy(
                        CWFMB[:, 128 * g:128 * (g + 1)], gb[:]))(g, gb),
                        r=[gb], w=[CWFMB])
                dma("sync", cw_hbm[:], CWFMB[:], r=[CWFMB], w=[cw_hbm])
                barrier()

            if upto < 4:
                raise _Stop()
            with contextlib.ExitStack() as PD_:
                RD = lambda *a: PD_.enter_context(nc.sbuf_tensor(*a, side="right"))
                W1S = RD("W1S", [128, DMC * DH], BF16)
                W2S = RD("W2S", [128, DIC * DM], BF16)
                H1 = RD("H1", [128, DIC * TT], BF16)
                XGb = [RD("XGb0", [128, DMC * TT], BF16),
                       RD("XGb1", [128, DMC * TT], BF16)]
                CWB = RD("CWB", [128, tok], BF16)
                CWFM = RD("CWFM", [8, tok], BF16)
                SEL8 = RD("SEL8", [8, 1024], BF16)
                B1STG = RD("B1STG", [16, 128], F32)
                B2STG = RD("B2STG", [8, 128], F32)
                B1T = RD("B1T", [128, 16], F32)
                B2T = RD("B2T", [128, 8], F32)
                TE = RD("TE", [128, TT], F32)
                TE2 = RD("TE2", [128, TT], F32)
                PH = [PD_.enter_context(nc.psum_tensor("PHa", [128, TT], F32)),
                      PD_.enter_context(nc.psum_tensor("PHb", [128, TT], F32))]
                PO = [PD_.enter_context(nc.psum_tensor("POa", [128, TT], F32)),
                      PD_.enter_context(nc.psum_tensor("POb", [128, TT], F32))]
                PCW = PD_.enter_context(nc.psum_tensor("PCW", [128, TT], F32))
                PBT = PD_.enter_context(nc.psum_tensor("PBT", [128, 16], F32))

                dma("gpsimd", SEL8[:], sel8[:], r=[sel8], w=[SEL8])
                dma("gpsimd", CWFM[:], cw_hbm[:], r=[cw_hbm], w=[CWFM])

                def load_w1(e):
                    wload("gpsimd", W1S, W1[e], DMC, 2, r=[W1], w=[W1S])

                def load_w2(e):
                    wload("gpsimd", W2S, W2[e], DIC, 4, r=[W2], w=[W2S])

                def load_xg(cti, buf):
                    dma("gpsimd", cview(buf, DMC),
                        xg_hbm[:, :, cti * TT:(cti + 1) * TT].rearrange(
                            "c p t -> p c t"),
                        r=[xg_hbm], w=[buf])

                load_w1(0)
                load_w2(0)
                for i in range(min(2, NT)):
                    load_xg(i, XGb[i % 2])
                for e in range(NE):
                    dma("gpsimd", B1STG[:], b1[e], r=[b1], w=[B1STG])
                    dma("gpsimd", B2STG[:], b2[e], r=[b2], w=[B2STG])
                    on("tensor", lambda E: E.transpose(
                        PBT[:, 0:16], B1STG[:], ID[0:16, 0:16]),
                        r=[B1STG, ID], w=[PBT])
                    on("vector", lambda E: E.tensor_copy(B1T[:], PBT[:, 0:16]),
                       r=[PBT], w=[B1T])
                    on("tensor", lambda E: E.transpose(
                        PBT[:, 0:8], B2STG[:], ID[0:8, 0:8]), r=[B2STG, ID], w=[PBT])
                    on("vector", lambda E: E.tensor_copy(B2T[:], PBT[:, 0:8]),
                       r=[PBT], w=[B2T])
                    for i in range(NT):
                        on("tensor", (lambda e, i: lambda E: E.matmul(
                            PCW[:], SEL8[:, 128 * e:128 * (e + 1)],
                            CWFM[:, i * TT:(i + 1) * TT], start=True, stop=True))(e, i),
                            r=[SEL8, CWFM], w=[PCW])
                        on("vector", (lambda i: lambda E: E.tensor_copy(
                            CWB[:, i * TT:(i + 1) * TT], PCW[:]))(i), r=[PCW], w=[CWB])
                    for ti in range(NT):
                        XGt = XGb[ti % 2]
                        for jh in range(DIC):
                            pb = jh % 2
                            for c in range(DMC):
                                on("tensor", (lambda jh, c, pb, XGt: lambda E: E.matmul(
                                    PH[pb][:],
                                    W1S[:, c * DH + jh * 128:c * DH + (jh + 1) * 128],
                                    XGt[:, c * TT:(c + 1) * TT],
                                    start=(c == 0), stop=(c == DMC - 1)))(jh, c, pb, XGt),
                                    r=[W1S, XGt], w=[PH[pb]], defer=(c < DMC - 1))
                            on("scalar", (lambda jh, pb: lambda E: E.activation(
                                H1[:, jh * TT:(jh + 1) * TT], PH[pb][:], AF.Relu,
                                bias=B1T[:, jh:jh + 1]))(jh, pb),
                                r=[PH[pb], B1T], w=[H1])
                        nxt = e * NT + ti + 2
                        if nxt < NE * NT:
                            load_xg((nxt % NT) if NT > 1 else 0, XGb[(ti + 2) % 2] if NT > 1 else XGb[0])
                        if ti == NT - 1 and e + 1 < NE:
                            load_w1(e + 1)
                        for jo in range(DMC):
                            pb = jo % 2
                            for k in range(DIC):
                                on("tensor", (lambda jo, k, pb: lambda E: E.matmul(
                                    PO[pb][:],
                                    W2S[:, k * DM + jo * 128:k * DM + (jo + 1) * 128],
                                    H1[:, k * TT:(k + 1) * TT],
                                    start=(k == 0), stop=(k == DIC - 1)))(jo, k, pb),
                                    r=[W2S, H1], w=[PO[pb]], defer=(k < DIC - 1))
                            on("vector", (lambda jo, pb: lambda E: E.tensor_scalar(
                                TE[:], PO[pb][:], B2T[:, jo:jo + 1], None, ALU.add))(jo, pb),
                                r=[PO[pb], B2T], w=[TE])
                            on("vector", (lambda ti: lambda E: E.tensor_mul(
                                TE2[:], TE[:], CWB[:, ti * TT:(ti + 1) * TT]))(ti),
                                r=[TE, CWB], w=[TE2])
                            on("vector", (lambda jo, ti: lambda E: E.tensor_add(
                                ACC[:, jo * tok + ti * TT:jo * tok + (ti + 1) * TT],
                                ACC[:, jo * tok + ti * TT:jo * tok + (ti + 1) * TT],
                                TE2[:]))(jo, ti), r=[ACC, TE2], w=[ACC])
                    if e + 1 < NE:
                        load_w2(e + 1)
                barrier()

            if upto < 5:
                raise _Stop()
            with contextlib.ExitStack() as PE_:
                RE = lambda *a: PE_.enter_context(nc.sbuf_tensor(*a, side="right"))
                HD = RE("HD", [128, DMC * TT], F32)
                ACB = RE("ACB", [128, DMC * TT], BF16)
                SQD = RE("SQD", [128, DMC * TT], BF16)
                MT4 = RE("MT4", [128, TT], F32)
                VT4 = RE("VT4", [128, TT], F32)
                T14 = RE("T14", [128, TT], F32)
                RT4 = RE("RT4", [128, TT], F32)
                OSB = RE("OSB", [128, DMC * TT], F32)
                PS4 = PE_.enter_context(nc.psum_tensor("PS4", [128, TT], F32))
                PQ4 = PE_.enter_context(nc.psum_tensor("PQ4", [128, TT], F32))

                for ti in range(NT):
                    for hf in range(2):
                        dma("sync",
                            HD[:].rearrange("p (c t) -> p c t", c=DMC)
                            [:, :, hf * 256:(hf + 1) * 256],
                            h_hbm[2 * ti + hf].rearrange("c p t -> p c t"),
                            r=[h_hbm], w=[HD])
                    for c in range(DMC):
                        on("vector", (lambda c, ti: lambda E: E.tensor_copy(
                            ACB[:, c * TT:(c + 1) * TT],
                            ACC[:, c * tok + ti * TT:c * tok + (ti + 1) * TT]))(c, ti),
                            r=[ACC], w=[ACB])
                    for c in range(DMC):
                        on("scalar", (lambda c: lambda E: E.activation(
                            SQD[:, c * TT:(c + 1) * TT], ACB[:, c * TT:(c + 1) * TT],
                            AF.Square))(c), r=[ACB], w=[SQD])
                    for c in range(DMC):
                        on("tensor", (lambda c: lambda E: E.matmul(
                            PS4[:], ONESB[:], ACB[:, c * TT:(c + 1) * TT],
                            start=(c == 0), stop=(c == DMC - 1)))(c),
                            r=[ONESB, ACB], w=[PS4], defer=(c < DMC - 1))
                    for c in range(DMC):
                        on("tensor", (lambda c: lambda E: E.matmul(
                            PQ4[:], ONESB[:], SQD[:, c * TT:(c + 1) * TT],
                            start=(c == 0), stop=(c == DMC - 1)))(c),
                            r=[ONESB, SQD], w=[PQ4], defer=(c < DMC - 1))
                    on("vector", lambda E: E.tensor_scalar(
                        MT4[:], PS4[:], 1.0 / DM, None, ALU.mult), r=[PS4], w=[MT4])
                    on("vector", lambda E: E.tensor_scalar(
                        VT4[:], PQ4[:], 1.0 / DM, None, ALU.mult), r=[PQ4], w=[VT4])
                    on("vector", lambda E: E.tensor_mul(T14[:], MT4[:], MT4[:]),
                       r=[MT4], w=[T14])
                    on("vector", lambda E: E.tensor_sub(VT4[:], VT4[:], T14[:]),
                       r=[VT4, T14], w=[VT4])
                    on("vector", lambda E: E.tensor_scalar(
                        VT4[:], VT4[:], LN_EPS, None, ALU.add), r=[VT4], w=[VT4])
                    on("vector", lambda E: E.reciprocal(T14[:], VT4[:]), r=[VT4], w=[T14])
                    on("scalar", lambda E: E.activation(RT4[:], T14[:], AF.Sqrt),
                       r=[T14], w=[RT4])
                    for c in range(DMC):
                        on("vector", (lambda c, ti: lambda E: E.tensor_sub(
                            VT4[:], ACC[:, c * tok + ti * TT:c * tok + (ti + 1) * TT],
                            MT4[:]))(c, ti), r=[ACC, MT4], w=[VT4])
                        on("vector", lambda E: E.tensor_mul(VT4[:], VT4[:], RT4[:]),
                           r=[VT4, RT4], w=[VT4])
                        on("vector", (lambda c: lambda E: E.tensor_add(
                            OSB[:, c * TT:(c + 1) * TT], VT4[:],
                            HD[:, c * TT:(c + 1) * TT]))(c),
                            r=[VT4, HD], w=[OSB])
                    dma("sync", outT[:, ti * TT:(ti + 1) * TT].rearrange(
                        "(c p) t -> p c t", p=128), cview(OSB, DMC),
                        r=[OSB], w=[outT])

        except _Stop:
            pass
        on("sync", lambda E: E.nop(),
           waits=[("sy", now("sy")), ("gq", now("gq"))], defer=True)
        on("gpsimd", lambda E: E.nop(),
           waits=[("gq", now("gq")), ("sy", now("sy"))], defer=True)

        with nc.Block() as block:
            block.tensor(lambda E: [f(E) for f in streams["tensor"]])
            block.vector(lambda E: [f(E) for f in streams["vector"]])
            block.scalar(lambda E: [f(E) for f in streams["scalar"]])
            block.sync(lambda E: [f(E) for f in streams["sync"]])
            block.gpsimd(lambda E: [f(E) for f in streams["gpsimd"]])

    return nc


def make_aux():
    masks = []
    for c in range(NCORES):
        m = np.zeros((64, 8), np.float32)
        for k in range(4 * (c // 4), c):
            m[:, k] = 1.0
        masks.append(m)
    sel = np.zeros((8, 1024), np.float32)
    for e in range(8):
        sel[e, 128 * e:128 * (e + 1)] = 1.0
    return masks, sel.astype(BFNP), np.eye(128, dtype=np.float32)


def make_in_maps(inputs, tok=2048):
    x = np.asarray(inputs["L0_attn_x"], np.float32)
    B, S, _ = x.shape
    xf = x.reshape(B * S, DM)
    masks, sel, idm = make_aux()
    shared = {
        "W_in": np.asarray(inputs["L0_attn_W_in"], np.float32),
        "b_in": np.asarray(inputs["L0_attn_b_in"], np.float32).reshape(32, 128),
        "W_xp": np.asarray(inputs["L0_attn_W_xp"], np.float32),
        "W_dt": np.asarray(inputs["L0_attn_W_dt"], np.float32),
        "b_dt": np.asarray(inputs["L0_attn_b_dt"], np.float32).reshape(8, 128),
        "W_out": np.asarray(inputs["L0_attn_W_out"], np.float32),
        "b_out": np.asarray(inputs["L0_attn_b_out"], np.float32).reshape(8, 128),
        "bxa": np.stack([np.asarray(inputs["L0_attn_b_xp"], np.float32),
                         np.asarray(inputs["L0_attn_A_log"], np.float32)]),
        "Dv": np.asarray(inputs["L0_attn_D"], np.float32).reshape(8, 128),
        "gate_W": np.asarray(inputs["L0_ffn_gate_W"], np.float32),
        "W1": np.asarray(inputs["L0_ffn_W1"], np.float32),
        "b1": np.asarray(inputs["L0_ffn_b1"], np.float32).reshape(NE, 16, 128),
        "W2": np.asarray(inputs["L0_ffn_W2"], np.float32),
        "b2": np.asarray(inputs["L0_ffn_b2"], np.float32).reshape(NE, 8, 128),
        "idm": idm, "sel8": sel,
    }
    in_maps = []
    for c in range(NCORES):
        sl = xf[c * tok:(c + 1) * tok]
        in_maps.append({**shared,
                        "xT": np.ascontiguousarray(sl.T),
                        "maskI": masks[c]})
    return in_maps


def kernel(**inputs):
    tok = 2048
    nc = build_nc(tok)
    in_maps = make_in_maps(inputs, tok)
    res = run_bass_kernel_spmd(nc, in_maps, core_ids=list(range(NCORES)))
    x = np.asarray(inputs["L0_attn_x"])
    B, S, _ = x.shape
    parts = [np.asarray(res.results[c]["outT"]).T for c in range(NCORES)]
    out = np.concatenate(parts, axis=0).reshape(B, S, DM).astype(np.float32)
    return out


# version 03
# speedup vs baseline: 1.1306x; 1.1306x over previous
import contextlib
import numpy as np
import ml_dtypes
import concourse.bass as bass
import concourse.mybir as mybir
from concourse.bass_utils import run_bass_kernel_spmd
from concourse.replica_groups import maybe_share_collective_output_space

F32 = mybir.dt.float32
BF16 = mybir.dt.bfloat16
AF = mybir.ActivationFunctionType
ALU = mybir.AluOpType
AX = mybir.AxisListType
BFNP = ml_dtypes.bfloat16

NCORES = 8
DM = 1024
DI = 2048
DS = 64
DH = 2048
NE = 8
DMC = DM // 128
DIC = DI // 128
LN_EPS = 1e-5


def _nm(t):
    n = getattr(t, "name", None)
    if isinstance(n, str):
        return n
    try:
        return t.name()
    except Exception:
        return str(t)


class _Stop(Exception):
    pass


def build_nc(tok=2048, upto=99):
    TT = 512
    NT = tok // TT
    NG = tok // 128

    nc = bass.Bass("TRN2", target_bir_lowering=False)

    xT = nc.dram_tensor("xT", [DM, tok], F32, kind="ExternalInput")
    W_in = nc.dram_tensor("W_in", [DM, 2 * DI], F32, kind="ExternalInput")
    b_in = nc.dram_tensor("b_in", [32, 128], F32, kind="ExternalInput")
    W_xp = nc.dram_tensor("W_xp", [DI, DS], F32, kind="ExternalInput")
    W_dt = nc.dram_tensor("W_dt", [DI, DM], F32, kind="ExternalInput")
    b_dt = nc.dram_tensor("b_dt", [8, 128], F32, kind="ExternalInput")
    W_out = nc.dram_tensor("W_out", [DI, DM], F32, kind="ExternalInput")
    b_out = nc.dram_tensor("b_out", [8, 128], F32, kind="ExternalInput")
    bxa = nc.dram_tensor("bxa", [2, 64], F32, kind="ExternalInput")
    Dv = nc.dram_tensor("Dv", [8, 128], F32, kind="ExternalInput")
    gate_W = nc.dram_tensor("gate_W", [DM, NE], F32, kind="ExternalInput")
    W1 = nc.dram_tensor("W1", [NE, DM, DH], F32, kind="ExternalInput")
    b1 = nc.dram_tensor("b1", [NE, 16, 128], F32, kind="ExternalInput")
    W2 = nc.dram_tensor("W2", [NE, DH, DM], F32, kind="ExternalInput")
    b2 = nc.dram_tensor("b2", [NE, 8, 128], F32, kind="ExternalInput")
    maskI = nc.dram_tensor("maskI", [64, 8], F32, kind="ExternalInput")
    idm = nc.dram_tensor("idm", [128, 128], F32, kind="ExternalInput")
    sel8 = nc.dram_tensor("sel8", [8, 1024], BF16, kind="ExternalInput")
    outT = nc.dram_tensor("outT", [DM, tok], F32, kind="ExternalOutput")

    xi_hbm = nc.dram_tensor("xi_hbm", [NT, 128, DIC * TT], BF16, kind="Internal")
    sr_hbm = nc.dram_tensor("sr_hbm", [NT, 128, DIC * TT], BF16, kind="Internal")
    xn_hbm = nc.dram_tensor("xn_hbm", [NT, 128, DMC * TT], BF16, kind="Internal")
    NTB = tok // 256
    h_hbm = nc.dram_tensor("h_hbm", [NTB, DMC, 128, 256], BF16, kind="Internal")
    ssum_hbm = nc.dram_tensor("ssum_hbm", [128, tok], F32, kind="Internal")
    cw_hbm = nc.dram_tensor("cw_hbm", [8, tok], BF16, kind="Internal")
    xg_hbm = nc.dram_tensor("xg_hbm", [DMC, 128, tok], BF16, kind="Internal")
    rg = [list(range(NCORES))]
    space = maybe_share_collective_output_space("AllGather", rg)
    ag_in = nc.dram_tensor("ag_in", [2, 64], F32, kind="Internal")
    ag_out = nc.dram_tensor("ag_out", [16, 64], F32, kind="Internal",
                            addr_space=space)

    ENGS = ("tensor", "vector", "scalar", "sync", "gpsimd")
    ESEM = {"tensor": "t", "vector": "v", "scalar": "s"}
    streams = {e: [] for e in ENGS}
    cnt = {}
    wm = {e: {} for e in ENGS}
    SEM = {}
    last_w = {}
    readers = {}
    pending = {e: [] for e in ENGS}

    def _reg_read(name, ev):
        d = readers.setdefault(name, {})
        d[ev[0]] = max(d.get(ev[0], 0), ev[1])

    def _reg_write(name, ev):
        last_w[name] = ev
        readers[name] = {}

    def on(eng, f, r=(), w=(), waits=(), sem=None, amt=1, defer=False):
        rn = [_nm(t) for t in r]
        wn = [_nm(t) for t in w]
        auto = {}
        for n in rn:
            if n in last_w:
                s_, v_ = last_w[n]
                auto[s_] = max(auto.get(s_, 0), v_)
        for n in wn:
            if n in last_w:
                s_, v_ = last_w[n]
                auto[s_] = max(auto.get(s_, 0), v_)
            for s_, v_ in readers.get(n, {}).items():
                auto[s_] = max(auto.get(s_, 0), v_)
        for s_, v_ in waits:
            if v_ > 0:
                auto[s_] = max(auto.get(s_, 0), v_)
        if sem is None and not defer:
            sem = ESEM.get(eng)
            if sem is None:
                raise RuntimeError(f"need explicit sem for {eng}")
        wl = sorted(auto.items())

        def op(E):
            for s_, v_ in wl:
                if wm[eng].get(s_, 0) < v_:
                    E.wait_ge(SEM[s_], v_)
                    wm[eng][s_] = v_
            ins = f(E)
            if not defer:
                ins.then_inc(SEM[sem], amt)
        streams[eng].append(op)
        if defer:
            pending[eng].extend([("r", n) for n in rn] + [("w", n) for n in wn])
            return None
        cnt[sem] = cnt.get(sem, 0) + amt
        ev = (sem, cnt[sem])
        for kind, n in pending[eng]:
            (_reg_read if kind == "r" else _reg_write)(n, ev)
        pending[eng] = []
        for n in rn:
            _reg_read(n, ev)
        for n in wn:
            _reg_write(n, ev)
        return ev

    def now(s):
        return cnt.get(s, 0)

    def dma(eng, out_ap, in_ap, r, w, waits=()):
        q = "sy" if eng == "sync" else "gq"
        chain = [(q, cnt.get(q, 0))]
        return on(eng, lambda E: E.dma_start(out=out_ap, in_=in_ap),
                  r=r, w=w, waits=list(waits) + chain, sem=q, amt=16)

    def cview(sb, nch):
        return sb[:].rearrange("p (c j) -> p c j", c=nch)

    def wload(eng, dst_sb, src2d, nch, parts, r, w):
        s3 = src2d.rearrange("(c p) j -> p c j", p=128)
        d3 = cview(dst_sb, nch)
        step = nch // parts
        for i in range(parts):
            dma(eng, d3[:, i * step:(i + 1) * step, :],
                s3[:, i * step:(i + 1) * step, :], r=r, w=w)

    def barrier():
        on("vector", lambda E: E.memzero(FLUSH[0:1, 0:4]), w=[FLUSH])
        on("scalar", lambda E: E.memzero(FLUSH[0:1, 4:8]), w=[FLUSH])
        snap = {s: now(s) for s in ("t", "v", "s", "sy", "gq", "cc")}
        own = {"tensor": "t", "vector": "v", "scalar": "s"}
        for eng in ENGS:
            wl = [(s, v) for s, v in snap.items() if v > 0 and own.get(eng) != s]
            on(eng, lambda E: E.nop(), waits=wl, defer=True)

    with contextlib.ExitStack() as STK:
        for s in ("t", "v", "s", "sy", "gq", "cc"):
            SEM[s] = STK.enter_context(nc.semaphore("sem_" + s))
            cnt[s] = 0

        L = lambda *a: STK.enter_context(nc.sbuf_tensor(*a, side="left"))
        FLUSH = L("FLUSH", [1, 8], F32)
        ID = L("ID", [128, 128], F32)
        ONESB = L("ONESB", [128, 128], BF16)
        MASK = L("MASK", [64, 8], F32)
        BINT = L("BINT", [128, 32], F32)
        BDTT = L("BDTT", [128, 8], F32)
        BOUT = L("BOUT", [128, 8], F32)
        DT = L("DT", [128, 8], F32)
        BXAT = L("BXAT", [64, 2], F32)
        ASC = L("ASC", [64, 1], F32)
        GWF = L("GWF", [128, 8 * 8], F32)
        BSTG1 = L("BSTG1", [32, 128], F32)
        BSTG2 = L("BSTG2", [8, 128], F32)
        BSTG3 = L("BSTG3", [8, 128], F32)
        BSTG4 = L("BSTG4", [8, 128], F32)
        BSTG5 = L("BSTG5", [2, 64], F32)

        dma("gpsimd", ID[:], idm[:], r=[idm], w=[ID])
        dma("gpsimd", MASK[:], maskI[:], r=[maskI], w=[MASK])
        dma("gpsimd", BSTG1[:], b_in[:], r=[b_in], w=[BSTG1])
        dma("gpsimd", BSTG2[:], b_dt[:], r=[b_dt], w=[BSTG2])
        dma("gpsimd", BSTG3[:], b_out[:], r=[b_out], w=[BSTG3])
        dma("gpsimd", BSTG4[:], Dv[:], r=[Dv], w=[BSTG4])
        dma("gpsimd", BSTG5[:], bxa[:], r=[bxa], w=[BSTG5])
        dma("sync", cview(GWF, DMC),
            gate_W.ap().rearrange("(c p) e -> p c e", p=128), r=[gate_W], w=[GWF])
        on("vector", lambda E: E.memset(ONESB[:], 1.0), w=[ONESB])

        with contextlib.ExitStack() as P0:
            PT = P0.enter_context(nc.psum_tensor("PT0", [128, 32], F32))
            for (stg, dst) in ((BSTG1, BINT), (BSTG2, BDTT), (BSTG3, BOUT),
                               (BSTG4, DT), (BSTG5, BXAT)):
                p, q = stg.shape[0], stg.shape[1]
                on("tensor",
                   (lambda stg, p, q: lambda E: E.transpose(
                       PT[0:q, 0:p], stg[:], ID[0:p, 0:p]))(stg, p, q),
                   r=[stg, ID], w=[PT])
                on("vector",
                   (lambda dst, p, q: lambda E: E.tensor_copy(
                       dst[0:q, 0:p], PT[0:q, 0:p]))(dst, p, q),
                   r=[PT], w=[dst])
            on("scalar", lambda E: E.activation(ASC[:], BXAT[:, 1:2], AF.Exp),
               r=[BXAT], w=[ASC])
            on("vector", lambda E: E.tensor_scalar(
                ASC[:], ASC[:], -1.0 / DM, None, ALU.mult), r=[ASC], w=[ASC])
            barrier()

        try:
            if upto < 1:
                raise _Stop()
            with contextlib.ExitStack() as PA:
                RA = lambda *a: PA.enter_context(nc.sbuf_tensor(*a, side="right"))
                WIN = RA("WIN", [128, DMC * 4096], BF16)
                XTf = [RA("XTf0", [128, DMC * TT], F32), RA("XTf1", [128, DMC * TT], F32)]
                XB = RA("XB", [128, DMC * TT], BF16)
                SQ = RA("SQ", [128, DMC * TT], BF16)
                XN = RA("XN", [128, DMC * TT], BF16)
                XIT = RA("XIT", [128, DIC * TT], BF16)
                SRT = RA("SRT", [128, DIC * TT], BF16)
                MT = RA("MT", [128, TT], F32)
                VT = RA("VT", [128, TT], F32)
                T1 = RA("T1", [128, TT], F32)
                RT = RA("RT", [128, TT], F32)
                PS_S = PA.enter_context(nc.psum_tensor("PS_S", [128, TT], F32))
                PS_Q = PA.enter_context(nc.psum_tensor("PS_Q", [128, TT], F32))
                PX = [PA.enter_context(nc.psum_tensor("PXa", [128, TT], F32)),
                      PA.enter_context(nc.psum_tensor("PXb", [128, TT], F32))]

                wload("gpsimd", WIN, W_in.ap(), DMC, 2, r=[W_in], w=[WIN])

                for ti in range(NT):
                    XTt = XTf[ti % 2]
                    dma("sync", cview(XTt, DMC),
                        xT[:, ti * TT:(ti + 1) * TT].rearrange("(c p) t -> p c t", p=128),
                        r=[xT], w=[XTt])
                    for c in range(DMC):
                        on("vector", (lambda c, XTt: lambda E: E.tensor_copy(
                            XB[:, c * TT:(c + 1) * TT], XTt[:, c * TT:(c + 1) * TT]))(c, XTt),
                            r=[XTt], w=[XB])
                    for c in range(DMC):
                        on("scalar", (lambda c: lambda E: E.activation(
                            SQ[:, c * TT:(c + 1) * TT], XB[:, c * TT:(c + 1) * TT],
                            AF.Square))(c), r=[XB], w=[SQ])
                    for c in range(DMC):
                        on("tensor", (lambda c: lambda E: E.matmul(
                            PS_S[:], ONESB[:], XB[:, c * TT:(c + 1) * TT],
                            start=(c == 0), stop=(c == DMC - 1)))(c),
                            r=[ONESB, XB], w=[PS_S], defer=(c < DMC - 1))
                    for c in range(DMC):
                        on("tensor", (lambda c: lambda E: E.matmul(
                            PS_Q[:], ONESB[:], SQ[:, c * TT:(c + 1) * TT],
                            start=(c == 0), stop=(c == DMC - 1)))(c),
                            r=[ONESB, SQ], w=[PS_Q], defer=(c < DMC - 1))
                    on("vector", lambda E: E.tensor_scalar(
                        MT[:], PS_S[:], 1.0 / DM, None, ALU.mult), r=[PS_S], w=[MT])
                    on("vector", lambda E: E.tensor_scalar(
                        VT[:], PS_Q[:], 1.0 / DM, None, ALU.mult), r=[PS_Q], w=[VT])
                    on("vector", lambda E: E.tensor_mul(T1[:], MT[:], MT[:]),
                       r=[MT], w=[T1])
                    on("vector", lambda E: E.tensor_sub(VT[:], VT[:], T1[:]),
                       r=[VT, T1], w=[VT])
                    on("vector", lambda E: E.tensor_scalar(
                        VT[:], VT[:], LN_EPS, None, ALU.add), r=[VT], w=[VT])
                    on("vector", lambda E: E.reciprocal(T1[:], VT[:]), r=[VT], w=[T1])
                    on("scalar", lambda E: E.activation(RT[:], T1[:], AF.Sqrt),
                       r=[T1], w=[RT])
                    for c in range(DMC):
                        on("vector", (lambda c, XTt: lambda E: E.tensor_sub(
                            VT[:], XTt[:, c * TT:(c + 1) * TT], MT[:]))(c, XTt),
                            r=[XTt, MT], w=[VT])
                        on("vector", (lambda c: lambda E: E.tensor_mul(
                            XN[:, c * TT:(c + 1) * TT], VT[:], RT[:]))(c),
                            r=[VT, RT], w=[XN])
                    for j in range(32):
                        pb = j % 2
                        for c in range(DMC):
                            on("tensor", (lambda j, c, pb: lambda E: E.matmul(
                                PX[pb][:],
                                WIN[:, c * 4096 + j * 128:c * 4096 + (j + 1) * 128],
                                XN[:, c * TT:(c + 1) * TT],
                                start=(c == 0), stop=(c == DMC - 1)))(j, c, pb),
                                r=[WIN, XN], w=[PX[pb]], defer=(c < DMC - 1))
                        dst = XIT if j < 16 else SRT
                        jj = j if j < 16 else j - 16
                        on("scalar", (lambda j, jj, dst, pb: lambda E: E.activation(
                            dst[:, jj * TT:(jj + 1) * TT], PX[pb][:], AF.Silu,
                            bias=BINT[:, j:j + 1]))(j, jj, dst, pb),
                            r=[PX[pb], BINT], w=[dst])
                        if j >= 16:
                            on("vector", (lambda jj: lambda E: E.tensor_mul(
                                SRT[:, jj * TT:(jj + 1) * TT],
                                SRT[:, jj * TT:(jj + 1) * TT],
                                XIT[:, jj * TT:(jj + 1) * TT]))(jj),
                                r=[SRT, XIT], w=[SRT])
                    dma("sync", xi_hbm[ti], XIT[:], r=[XIT], w=[xi_hbm])
                    dma("sync", sr_hbm[ti], SRT[:], r=[SRT], w=[sr_hbm])
                    dma("sync", xn_hbm[ti], XN[:], r=[XN], w=[xn_hbm])
                barrier()

            if upto < 2:
                raise _Stop()
            with contextlib.ExitStack() as PB:
                RB = lambda *a: PB.enter_context(nc.sbuf_tensor(*a, side="right"))
                WDT = RB("WDT", [128, DIC * 1024], BF16)
                WXP = RB("WXP", [128, DIC * 64], BF16)
                BX = RB("BX", [64, tok], F32)
                AFULL = RB("AFULL", [64, tok], F32)
                XIJ = [RB("XIJ0", [128, DIC * TT], BF16), RB("XIJ1", [128, DIC * TT], BF16)]
                SPB = RB("SPB", [128, DMC * TT], BF16)
                TA = RB("TA", [128, TT], F32)
                TB = RB("TB", [128, TT], F32)
                TC = RB("TC", [128, TT], F32)
                TD = RB("TD", [128, TT], F32)
                PD = [PB.enter_context(nc.psum_tensor("PDa", [128, TT], F32)),
                      PB.enter_context(nc.psum_tensor("PDb", [128, TT], F32))]
                PMD = PB.enter_context(nc.psum_tensor("PMD", [128, TT], F32))
                PBX = PB.enter_context(nc.psum_tensor("PBX", [64, TT], F32))
                PSS = PB.enter_context(nc.psum_tensor("PSS", [128, TT], F32))
                PSC = PB.enter_context(nc.psum_tensor("PSC", [128, 128], F32))

                A2L = (upto - 210) if 210 <= upto < 220 else 9
                wload("gpsimd", WDT, W_dt.ap(), DIC, 4, r=[W_dt], w=[WDT])
                wload("gpsimd", WXP, W_xp.ap(), DIC, 1, r=[W_xp], w=[WXP])

                for ti in range(NT):
                    XJ = XIJ[ti % 2]
                    dma("sync", XJ[:], xi_hbm[ti], r=[xi_hbm], w=[XJ])
                    for j8 in range(DMC if A2L >= 2 else 0):
                        pb = j8 % 2
                        for k in range(DIC):
                            on("tensor", (lambda j8, k, pb, XJ: lambda E: E.matmul(
                                PD[pb][:],
                                WDT[:, k * 1024 + j8 * 128:k * 1024 + (j8 + 1) * 128],
                                XJ[:, k * TT:(k + 1) * TT],
                                start=(k == 0), stop=(k == DIC - 1)))(j8, k, pb, XJ),
                                r=[WDT, XJ], w=[PD[pb]], defer=(k < DIC - 1))
                        on("scalar", (lambda j8, pb: lambda E: E.activation(
                            TA[:], PD[pb][:], AF.Abs, bias=BDTT[:, j8:j8 + 1]))(j8, pb),
                            r=[PD[pb], BDTT], w=[TA])
                        on("scalar", lambda E: E.activation(
                            TB[:], TA[:], AF.Exp, scale=-1.0), r=[TA], w=[TB])
                        on("scalar", lambda E: E.activation(
                            TC[:], TB[:], AF.Ln, bias=1.0), r=[TB], w=[TC])
                        on("scalar", (lambda j8, pb: lambda E: E.activation(
                            TD[:], PD[pb][:], AF.Relu,
                            bias=BDTT[:, j8:j8 + 1]))(j8, pb),
                            r=[PD[pb], BDTT], w=[TD])
                        on("vector", (lambda j8: lambda E: E.tensor_add(
                            SPB[:, j8 * TT:(j8 + 1) * TT], TD[:], TC[:]))(j8),
                            r=[TD, TC], w=[SPB])
                        on("tensor", (lambda j8: lambda E: E.matmul(
                            PMD[:], ONESB[:], SPB[:, j8 * TT:(j8 + 1) * TT],
                            start=(j8 == 0), stop=(j8 == DMC - 1)))(j8),
                            r=[ONESB, SPB], w=[PMD], defer=(j8 < DMC - 1))
                    if A2L >= 3:
                        on("scalar", (lambda ti: lambda E: E.activation(
                            AFULL[:, ti * TT:(ti + 1) * TT], PMD[0:64, :], AF.Exp,
                            scale=ASC[:, 0:1]))(ti),
                            r=[PMD, ASC], w=[AFULL])
                    for k in range(DIC if A2L >= 4 else 0):
                        on("tensor", (lambda k, XJ: lambda E: E.matmul(
                            PBX[:], WXP[:, 64 * k:64 * (k + 1)],
                            XJ[:, k * TT:(k + 1) * TT],
                            start=(k == 0), stop=(k == DIC - 1)))(k, XJ),
                            r=[WXP, XJ], w=[PBX], defer=(k < DIC - 1))
                    if A2L >= 4:
                        on("vector", (lambda ti: lambda E: E.tensor_scalar(
                            BX[:, ti * TT:(ti + 1) * TT], PBX[:], BXAT[:, 0:1], None,
                            ALU.add))(ti),
                            r=[PBX, BXAT], w=[BX])

                if upto < 22:
                    raise _Stop()
                ZER = RB("ZER", [64, tok], F32)
                STATE = RB("STATE", [64, tok], F32)
                CP = RB("CP", [64, tok], F32)
                STCB = RB("STCB", [64, tok], BF16)
                SUM2 = RB("SUM2", [64, 2], F32)
                SUM2T = RB("SUM2T", [2, 64], F32)
                AGSB = RB("AGSB", [16, 64], F32)
                AGT = RB("AGT", [64, 16], F32)
                SIN = RB("SIN", [64, 1], F32)
                TF = RB("TF", [64, 1], F32)
                SSB = RB("SSB", [128, tok], F32)

                on("vector", lambda E: E.memset(ZER[:], 0.0), w=[ZER])
                on("vector", lambda E: E.tensor_tensor_scan(
                    STATE[:], AFULL[:], BX[:], 0.0, ALU.mult, ALU.add),
                    r=[AFULL, BX], w=[STATE])
                on("vector", lambda E: E.tensor_tensor_scan(
                    CP[:], AFULL[:], ZER[:], 1.0, ALU.mult, ALU.add),
                    r=[AFULL, ZER], w=[CP])
                on("vector", lambda E: E.tensor_copy(SUM2[:, 0:1], CP[:, tok - 1:tok]),
                   r=[CP], w=[SUM2])
                on("vector", lambda E: E.tensor_copy(SUM2[:, 1:2], STATE[:, tok - 1:tok]),
                   r=[STATE], w=[SUM2])
                on("tensor", lambda E: E.transpose(
                    PSC[0:2, 0:64], SUM2[:], ID[0:64, 0:64]), r=[SUM2, ID], w=[PSC])
                on("vector", lambda E: E.tensor_copy(SUM2T[:], PSC[0:2, 0:64]),
                   r=[PSC], w=[SUM2T])
                if upto < 23:
                    raise _Stop()
                dma("gpsimd", ag_in[:], SUM2T[:], r=[SUM2T], w=[ag_in])
                on("gpsimd", lambda E: E.collective_compute(
                    "AllGather", ALU.bypass, replica_groups=rg,
                    ins=[ag_in[:]], outs=[ag_out[:]]),
                    r=[ag_in], w=[ag_out], sem="cc", amt=1)
                dma("gpsimd", AGSB[:], ag_out[:], r=[ag_out], w=[AGSB])
                on("tensor", lambda E: E.transpose(
                    PSC[0:64, 0:16], AGSB[:], ID[0:16, 0:16]), r=[AGSB, ID], w=[PSC])
                on("vector", lambda E: E.tensor_copy(AGT[:], PSC[0:64, 0:16]),
                   r=[PSC], w=[AGT])
                if upto < 24:
                    raise _Stop()
                on("vector", lambda E: E.memset(SIN[:], 0.0), w=[SIN])
                for k in range(NCORES):
                    on("vector", (lambda k: lambda E: E.scalar_tensor_tensor(
                        TF[:], SIN[:], AGT[:, 2 * k:2 * k + 1],
                        AGT[:, 2 * k + 1:2 * k + 2], ALU.mult, ALU.add))(k),
                        r=[SIN, AGT], w=[TF])
                    on("vector", lambda E: E.tensor_sub(TF[:], TF[:], SIN[:]),
                       r=[TF, SIN], w=[TF])
                    on("vector", (lambda k: lambda E: E.tensor_mul(
                        TF[:], TF[:], MASK[:, k:k + 1]))(k), r=[TF, MASK], w=[TF])
                    on("vector", lambda E: E.tensor_add(SIN[:], SIN[:], TF[:]),
                       r=[SIN, TF], w=[SIN])
                on("vector", lambda E: E.scalar_tensor_tensor(
                    STATE[:], CP[:], SIN[:, 0:1], STATE[:], ALU.mult, ALU.add),
                    r=[CP, SIN, STATE], w=[STATE])
                on("vector", lambda E: E.tensor_copy(STCB[:], STATE[:]),
                   r=[STATE], w=[STCB])
                if upto < 25:
                    raise _Stop()
                for i in range(NT):
                    on("tensor", (lambda i: lambda E: E.matmul(
                        PSS[:], ONESB[0:64, :], STCB[:, i * TT:(i + 1) * TT],
                        start=True, stop=True))(i),
                        r=[ONESB, STCB], w=[PSS])
                    on("vector", (lambda i: lambda E: E.tensor_copy(
                        SSB[:, i * TT:(i + 1) * TT], PSS[:]))(i), r=[PSS], w=[SSB])
                dma("sync", ssum_hbm[:], SSB[:], r=[SSB], w=[ssum_hbm])
                barrier()

            if upto < 3:
                raise _Stop()
            ACC = L("ACC", [128, DMC * tok], F32)
            TTB = 256
            NTB2 = tok // TTB

            with contextlib.ExitStack() as PC:
                RC = lambda *a: PC.enter_context(nc.sbuf_tensor(*a, side="right"))
                WOUT = RC("WOUT", [128, DIC * 1024], BF16)
                SRJ = [RC("SRJ0", [128, DIC * TTB], BF16),
                       RC("SRJ1", [128, DIC * TTB], BF16)]
                XTb = [RC("XTb0", [128, DMC * TTB], BF16),
                       RC("XTb1", [128, DMC * TTB], BF16)]
                XNb = [RC("XNb0", [128, DMC * TTB], BF16),
                       RC("XNb1", [128, DMC * TTB], BF16)]
                MIX = [RC("MIX0", [128, DMC * TTB], BF16),
                       RC("MIX1", [128, DMC * TTB], BF16)]
                HTb = [RC("HTb0", [128, DMC * TTB], BF16),
                       RC("HTb1", [128, DMC * TTB], BF16)]
                SQB = [RC("SQB0", [128, DMC * TTB], BF16),
                       RC("SQB1", [128, DMC * TTB], BF16)]
                SSBb = [RC("SSBb0", [128, TTB], F32), RC("SSBb1", [128, TTB], F32)]
                LGF = RC("LGF", [8, tok], F32)
                MT2 = [RC("MT2a", [128, TTB], F32), RC("MT2b", [128, TTB], F32)]
                RT2 = [RC("RT2a", [128, TTB], F32), RC("RT2b", [128, TTB], F32)]
                M3T = [RC("M3Ta", [128, TTB], F32), RC("M3Tb", [128, TTB], F32)]
                R3T = [RC("R3Ta", [128, TTB], F32), RC("R3Tb", [128, TTB], F32)]
                VT2 = RC("VT2", [128, TTB], F32)
                T12 = RC("T12", [128, TTB], F32)
                PY = [PC.enter_context(nc.psum_tensor("PYa", [128, TTB], F32)),
                      PC.enter_context(nc.psum_tensor("PYb", [128, TTB], F32))]
                PS2 = PC.enter_context(nc.psum_tensor("PS2", [128, TTB], F32))
                PQ2 = PC.enter_context(nc.psum_tensor("PQ2", [128, TTB], F32))
                PLG = PC.enter_context(nc.psum_tensor("PLG", [8, TTB], F32))

                wload("gpsimd", WOUT, W_out.ap(), DIC, 4, r=[W_out], w=[WOUT])

                def lnstats(src_bf, sq, mdst, rdst):
                    for c in range(DMC):
                        on("scalar", (lambda c: lambda E: E.activation(
                            sq[:, c * TTB:(c + 1) * TTB],
                            src_bf[:, c * TTB:(c + 1) * TTB],
                            AF.Square))(c), r=[src_bf], w=[sq])
                    for c in range(DMC):
                        on("tensor", (lambda c: lambda E: E.matmul(
                            PS2[:], ONESB[:], src_bf[:, c * TTB:(c + 1) * TTB],
                            start=(c == 0), stop=(c == DMC - 1)))(c),
                            r=[ONESB, src_bf], w=[PS2], defer=(c < DMC - 1))
                    for c in range(DMC):
                        on("tensor", (lambda c: lambda E: E.matmul(
                            PQ2[:], ONESB[:], sq[:, c * TTB:(c + 1) * TTB],
                            start=(c == 0), stop=(c == DMC - 1)))(c),
                            r=[ONESB, sq], w=[PQ2], defer=(c < DMC - 1))
                    on("vector", (lambda mdst: lambda E: E.tensor_scalar(
                        mdst[:], PS2[:], 1.0 / DM, None, ALU.mult))(mdst),
                        r=[PS2], w=[mdst])
                    on("vector", lambda E: E.tensor_scalar(
                        VT2[:], PQ2[:], 1.0 / DM, None, ALU.mult), r=[PQ2], w=[VT2])
                    on("vector", (lambda mdst: lambda E: E.tensor_mul(
                        T12[:], mdst[:], mdst[:]))(mdst), r=[mdst], w=[T12])
                    on("vector", lambda E: E.tensor_sub(VT2[:], VT2[:], T12[:]),
                       r=[VT2, T12], w=[VT2])
                    on("vector", lambda E: E.tensor_scalar(
                        VT2[:], VT2[:], LN_EPS, None, ALU.add), r=[VT2], w=[VT2])
                    on("vector", lambda E: E.reciprocal(T12[:], VT2[:]),
                       r=[VT2], w=[T12])
                    on("scalar", (lambda rdst: lambda E: E.activation(
                        rdst[:], T12[:], AF.Sqrt))(rdst), r=[T12], w=[rdst])

                for tb in range(NTB2):
                    p = tb % 2
                    ti, hf = tb // 2, tb % 2
                    sr, xb_, xn_, mx, ht, sq = (SRJ[p], XTb[p], XNb[p], MIX[p],
                                                HTb[p], SQB[p])
                    m2, r2, m3, r3 = MT2[p], RT2[p], M3T[p], R3T[p]
                    ss = SSBb[p]
                    dma("sync", sr[:].rearrange("p (k t) -> p k t", k=DIC),
                        sr_hbm[ti].rearrange("p (k t) -> p k t", k=DIC)
                        [:, :, hf * TTB:(hf + 1) * TTB],
                        r=[sr_hbm], w=[sr])
                    dma("sync", xn_[:].rearrange("p (c t) -> p c t", c=DMC),
                        xn_hbm[ti].rearrange("p (c t) -> p c t", c=DMC)
                        [:, :, hf * TTB:(hf + 1) * TTB],
                        r=[xn_hbm], w=[xn_])
                    dma("gpsimd", cview(xb_, DMC),
                        xT[:, tb * TTB:(tb + 1) * TTB].rearrange(
                            "(c p) t -> p c t", p=128),
                        r=[xT], w=[xb_])
                    dma("gpsimd", ss[:], ssum_hbm[:, tb * TTB:(tb + 1) * TTB],
                        r=[ssum_hbm], w=[ss])
                    for k in range(DIC):
                        on("vector", (lambda k, sr, ss: lambda E: E.tensor_mul(
                            sr[:, k * TTB:(k + 1) * TTB], ss[:],
                            sr[:, k * TTB:(k + 1) * TTB]))(k, sr, ss),
                            r=[ss, sr], w=[sr])
                    for j8 in range(DMC):
                        pb = j8 % 2
                        for k in range(DIC):
                            on("tensor", (lambda j8, k, pb, sr: lambda E: E.matmul(
                                PY[pb][:],
                                WOUT[:, k * 1024 + j8 * 128:k * 1024 + (j8 + 1) * 128],
                                sr[:, k * TTB:(k + 1) * TTB],
                                start=(k == 0), stop=(k == DIC - 1)))(j8, k, pb, sr),
                                r=[WOUT, sr], w=[PY[pb]], defer=(k < DIC - 1))
                        on("vector", (lambda j8, pb: lambda E: E.tensor_scalar(
                            T12[:], PY[pb][:], BOUT[:, j8:j8 + 1], None,
                            ALU.add))(j8, pb),
                            r=[PY[pb], BOUT], w=[T12])
                        on("vector", (lambda j8, mx, xn_: lambda E: E.scalar_tensor_tensor(
                            mx[:, j8 * TTB:(j8 + 1) * TTB],
                            xn_[:, j8 * TTB:(j8 + 1) * TTB],
                            DT[:, j8:j8 + 1], T12[:], ALU.mult, ALU.add))(j8, mx, xn_),
                            r=[xn_, DT, T12], w=[mx])
                    lnstats(mx, sq, m2, r2)
                    for c in range(DMC):
                        on("vector", (lambda c, mx, m2: lambda E: E.tensor_sub(
                            VT2[:], mx[:, c * TTB:(c + 1) * TTB], m2[:]))(c, mx, m2),
                            r=[mx, m2], w=[VT2])
                        on("vector", (lambda r2: lambda E: E.tensor_mul(
                            VT2[:], VT2[:], r2[:]))(r2), r=[VT2, r2], w=[VT2])
                        on("vector", (lambda c, ht, xb_: lambda E: E.tensor_add(
                            ht[:, c * TTB:(c + 1) * TTB], VT2[:],
                            xb_[:, c * TTB:(c + 1) * TTB]))(c, ht, xb_),
                            r=[VT2, xb_], w=[ht])
                    dma("sync", h_hbm[tb].rearrange("c p t -> p c t"), cview(ht, DMC),
                        r=[ht], w=[h_hbm])
                    lnstats(ht, sq, m3, r3)
                    for c in range(DMC):
                        on("vector", (lambda c, ht, m3: lambda E: E.tensor_sub(
                            VT2[:], ht[:, c * TTB:(c + 1) * TTB], m3[:]))(c, ht, m3),
                            r=[ht, m3], w=[VT2])
                        on("vector", (lambda c, tb, r3: lambda E: E.tensor_mul(
                            ACC[:, c * tok + tb * TTB:c * tok + (tb + 1) * TTB],
                            VT2[:], r3[:]))(c, tb, r3), r=[VT2, r3], w=[ACC])
                        on("scalar", (lambda c, tb, sq: lambda E: E.activation(
                            sq[:, c * TTB:(c + 1) * TTB],
                            ACC[:, c * tok + tb * TTB:c * tok + (tb + 1) * TTB],
                            AF.Copy))(c, tb, sq),
                            r=[ACC], w=[sq])
                    dma("sync",
                        xg_hbm[:, :, tb * TTB:(tb + 1) * TTB].rearrange(
                            "c p t -> p c t"),
                        cview(sq, DMC), r=[sq], w=[xg_hbm])
                    for c in range(DMC):
                        on("tensor", (lambda c, tb: lambda E: E.matmul(
                            PLG[:], GWF[:, 8 * c:8 * (c + 1)],
                            ACC[:, c * tok + tb * TTB:c * tok + (tb + 1) * TTB],
                            start=(c == 0), stop=(c == DMC - 1)))(c, tb),
                            r=[GWF, ACC], w=[PLG], defer=(c < DMC - 1))
                    on("vector", (lambda tb: lambda E: E.tensor_copy(
                        LGF[:, tb * TTB:(tb + 1) * TTB], PLG[:]))(tb),
                        r=[PLG], w=[LGF])

                CWFMB = RC("CWFMB", [8, tok], BF16)
                CWG = RC("CWG", [128, NG * 8], F32)
                EG = RC("EG", [128, NG * 8], F32)
                GG = RC("GG", [128, NG * 8], F32)
                K1 = RC("K1", [128, NG * 8], F32)
                GS = RC("GS", [128, NG * 8], F32)
                M1 = RC("M1", [128, NG], F32)
                M1N = RC("M1N", [128, NG], F32)
                S1 = RC("S1", [128, NG], F32)
                R1 = RC("R1", [128, NG], F32)
                M2 = RC("M2", [128, NG], F32)
                M3g = RC("M3g", [128, NG], F32)
                S2 = RC("S2", [128, NG], F32)
                R2 = RC("R2", [128, NG], F32)
                PGT = PC.enter_context(nc.psum_tensor("PGT", [128, NG * 8], F32))
                PGB = [PC.enter_context(nc.psum_tensor("PGB0", [8, 128], F32)),
                       PC.enter_context(nc.psum_tensor("PGB1", [8, 128], F32))]

                for g in range(NG):
                    on("tensor", (lambda g: lambda E: E.transpose(
                        PGT[:, 8 * g:8 * (g + 1)], LGF[:, 128 * g:128 * (g + 1)],
                        ID[0:8, 0:8]))(g),
                        r=[LGF, ID], w=[PGT], defer=(g < NG - 1))
                on("vector", lambda E: E.tensor_copy(CWG[:], PGT[:]), r=[PGT], w=[CWG])
                for g in range(NG):
                    on("vector", (lambda g: lambda E: E.reduce_max(
                        M1[:, g:g + 1], CWG[:, 8 * g:8 * (g + 1)], AX.X))(g),
                        r=[CWG], w=[M1])
                on("vector", lambda E: E.tensor_scalar(
                    M1N[:], M1[:], -1.0, None, ALU.mult), r=[M1], w=[M1N])
                for g in range(NG):
                    on("scalar", (lambda g: lambda E: E.activation(
                        EG[:, 8 * g:8 * (g + 1)], CWG[:, 8 * g:8 * (g + 1)], AF.Exp,
                        bias=M1N[:, g:g + 1]))(g), r=[CWG, M1N], w=[EG])
                for g in range(NG):
                    on("vector", (lambda g: lambda E: E.reduce_sum(
                        S1[:, g:g + 1], EG[:, 8 * g:8 * (g + 1)], AX.X))(g),
                        r=[EG], w=[S1])
                on("vector", lambda E: E.reciprocal(R1[:], S1[:]), r=[S1], w=[R1])
                for g in range(NG):
                    on("vector", (lambda g: lambda E: E.tensor_scalar(
                        GG[:, 8 * g:8 * (g + 1)], EG[:, 8 * g:8 * (g + 1)],
                        R1[:, g:g + 1], None, ALU.mult))(g), r=[EG, R1], w=[GG])
                for g in range(NG):
                    on("vector", (lambda g: lambda E: E.reduce_max(
                        M2[:, g:g + 1], GG[:, 8 * g:8 * (g + 1)], AX.X))(g),
                        r=[GG], w=[M2])
                for g in range(NG):
                    on("vector", (lambda g: lambda E: E.tensor_scalar(
                        K1[:, 8 * g:8 * (g + 1)], GG[:, 8 * g:8 * (g + 1)],
                        M2[:, g:g + 1], None, ALU.is_ge))(g), r=[GG, M2], w=[K1])
                on("vector", lambda E: E.scalar_tensor_tensor(
                    K1[:], K1[:], -1e9, GG[:], ALU.mult, ALU.add), r=[K1, GG], w=[K1])
                for g in range(NG):
                    on("vector", (lambda g: lambda E: E.reduce_max(
                        M3g[:, g:g + 1], K1[:, 8 * g:8 * (g + 1)], AX.X))(g),
                        r=[K1], w=[M3g])
                for g in range(NG):
                    on("vector", (lambda g: lambda E: E.tensor_scalar(
                        K1[:, 8 * g:8 * (g + 1)], GG[:, 8 * g:8 * (g + 1)],
                        M3g[:, g:g + 1], None, ALU.is_ge))(g), r=[GG, M3g], w=[K1])
                on("vector", lambda E: E.tensor_mul(GS[:], GG[:], K1[:]),
                   r=[GG, K1], w=[GS])
                for g in range(NG):
                    on("vector", (lambda g: lambda E: E.reduce_sum(
                        S2[:, g:g + 1], GS[:, 8 * g:8 * (g + 1)], AX.X))(g),
                        r=[GS], w=[S2])
                on("vector", lambda E: E.tensor_scalar(
                    S2[:], S2[:], 1e-8, None, ALU.add), r=[S2], w=[S2])
                on("vector", lambda E: E.reciprocal(R2[:], S2[:]), r=[S2], w=[R2])
                for g in range(NG):
                    on("vector", (lambda g: lambda E: E.tensor_scalar(
                        GS[:, 8 * g:8 * (g + 1)], GS[:, 8 * g:8 * (g + 1)],
                        R2[:, g:g + 1], None, ALU.mult))(g), r=[GS, R2], w=[GS])
                for g in range(NG):
                    gb = PGB[g % 2]
                    on("tensor", (lambda g, gb: lambda E: E.transpose(
                        gb[:], GS[:, 8 * g:8 * (g + 1)], ID[:]))(g, gb),
                        r=[GS, ID], w=[gb])
                    on("vector", (lambda g, gb: lambda E: E.tensor_copy(
                        CWFMB[:, 128 * g:128 * (g + 1)], gb[:]))(g, gb),
                        r=[gb], w=[CWFMB])
                dma("sync", cw_hbm[:], CWFMB[:], r=[CWFMB], w=[cw_hbm])
                barrier()

            if upto < 4:
                raise _Stop()
            with contextlib.ExitStack() as PD_:
                RD = lambda *a: PD_.enter_context(nc.sbuf_tensor(*a, side="right"))
                W1S = RD("W1S", [128, DMC * DH], BF16)
                W2S = RD("W2S", [128, DIC * DM], BF16)
                H1 = RD("H1", [128, DIC * TT], BF16)
                XGb = [RD("XGb0", [128, DMC * TT], BF16),
                       RD("XGb1", [128, DMC * TT], BF16)]
                CWB = RD("CWB", [128, tok], BF16)
                CWFM = RD("CWFM", [8, tok], BF16)
                SEL8 = RD("SEL8", [8, 1024], BF16)
                B1STG = RD("B1STG", [16, 128], F32)
                B2STG = RD("B2STG", [8, 128], F32)
                B1TA = RD("B1TA", [128, NE * 16], F32)
                B2TA = RD("B2TA", [128, NE * 8], F32)
                TE = RD("TE", [128, TT], F32)
                PH = [PD_.enter_context(nc.psum_tensor("PHa", [128, TT], F32)),
                      PD_.enter_context(nc.psum_tensor("PHb", [128, TT], F32))]
                PO = [PD_.enter_context(nc.psum_tensor("POa", [128, TT], F32)),
                      PD_.enter_context(nc.psum_tensor("POb", [128, TT], F32))]
                PCW = PD_.enter_context(nc.psum_tensor("PCW", [128, TT], F32))
                PBT = PD_.enter_context(nc.psum_tensor("PBT", [128, 16], F32))

                dma("gpsimd", SEL8[:], sel8[:], r=[sel8], w=[SEL8])
                dma("gpsimd", CWFM[:], cw_hbm[:], r=[cw_hbm], w=[CWFM])
                for e in range(NE):
                    dma("sync", B1STG[:], b1[e], r=[b1], w=[B1STG])
                    on("tensor", (lambda e: lambda E: E.transpose(
                        PBT[:, 0:16], B1STG[:], ID[0:16, 0:16]))(e),
                        r=[B1STG, ID], w=[PBT])
                    on("vector", (lambda e: lambda E: E.tensor_copy(
                        B1TA[:, 16 * e:16 * (e + 1)], PBT[:, 0:16]))(e),
                        r=[PBT], w=[B1TA])
                    dma("sync", B2STG[:], b2[e], r=[b2], w=[B2STG])
                    on("tensor", (lambda e: lambda E: E.transpose(
                        PBT[:, 0:8], B2STG[:], ID[0:8, 0:8]))(e),
                        r=[B2STG, ID], w=[PBT])
                    on("vector", (lambda e: lambda E: E.tensor_copy(
                        B2TA[:, 8 * e:8 * (e + 1)], PBT[:, 0:8]))(e),
                        r=[PBT], w=[B2TA])

                def load_w1(e):
                    wload("gpsimd", W1S, W1[e], DMC, 2, r=[W1], w=[W1S])

                def load_w2(e):
                    wload("gpsimd", W2S, W2[e], DIC, 4, r=[W2], w=[W2S])

                def load_xg(cti, buf):
                    dma("sync", cview(buf, DMC),
                        xg_hbm[:, :, cti * TT:(cti + 1) * TT].rearrange(
                            "c p t -> p c t"),
                        r=[xg_hbm], w=[buf])

                load_w1(0)
                load_w2(0)
                for i in range(min(2, NT)):
                    load_xg(i, XGb[i % 2])
                for e in range(NE):
                    for i in range(NT):
                        on("tensor", (lambda e, i: lambda E: E.matmul(
                            PCW[:], SEL8[:, 128 * e:128 * (e + 1)],
                            CWFM[:, i * TT:(i + 1) * TT], start=True, stop=True))(e, i),
                            r=[SEL8, CWFM], w=[PCW])
                        on("vector", (lambda i: lambda E: E.tensor_copy(
                            CWB[:, i * TT:(i + 1) * TT], PCW[:]))(i), r=[PCW], w=[CWB])
                    for ti in range(NT):
                        XGt = XGb[ti % 2]
                        for jh in range(DIC):
                            pb = jh % 2
                            for c in range(DMC):
                                on("tensor", (lambda jh, c, pb, XGt: lambda E: E.matmul(
                                    PH[pb][:],
                                    W1S[:, c * DH + jh * 128:c * DH + (jh + 1) * 128],
                                    XGt[:, c * TT:(c + 1) * TT],
                                    start=(c == 0), stop=(c == DMC - 1)))(jh, c, pb, XGt),
                                    r=[W1S, XGt], w=[PH[pb]], defer=(c < DMC - 1))
                            on("scalar", (lambda jh, pb, e: lambda E: E.activation(
                                H1[:, jh * TT:(jh + 1) * TT], PH[pb][:], AF.Relu,
                                bias=B1TA[:, 16 * e + jh:16 * e + jh + 1]))(jh, pb, e),
                                r=[PH[pb], B1TA], w=[H1])
                        nxt = e * NT + ti + 2
                        if nxt < NE * NT:
                            load_xg((nxt % NT) if NT > 1 else 0, XGb[(ti + 2) % 2] if NT > 1 else XGb[0])
                        if ti == NT - 1 and e + 1 < NE:
                            load_w1(e + 1)
                        for jo in range(DMC):
                            pb = jo % 2
                            for k in range(DIC):
                                on("tensor", (lambda jo, k, pb: lambda E: E.matmul(
                                    PO[pb][:],
                                    W2S[:, k * DM + jo * 128:k * DM + (jo + 1) * 128],
                                    H1[:, k * TT:(k + 1) * TT],
                                    start=(k == 0), stop=(k == DIC - 1)))(jo, k, pb),
                                    r=[W2S, H1], w=[PO[pb]], defer=(k < DIC - 1))
                            on("vector", (lambda jo, pb, e: lambda E: E.tensor_scalar(
                                TE[:], PO[pb][:], B2TA[:, 8 * e + jo:8 * e + jo + 1],
                                None, ALU.add))(jo, pb, e),
                                r=[PO[pb], B2TA], w=[TE])
                            on("vector", (lambda ti: lambda E: E.tensor_mul(
                                TE[:], TE[:], CWB[:, ti * TT:(ti + 1) * TT]))(ti),
                                r=[TE, CWB], w=[TE])
                            on("vector", (lambda jo, ti: lambda E: E.tensor_add(
                                ACC[:, jo * tok + ti * TT:jo * tok + (ti + 1) * TT],
                                ACC[:, jo * tok + ti * TT:jo * tok + (ti + 1) * TT],
                                TE[:]))(jo, ti), r=[ACC, TE], w=[ACC])
                    if e + 1 < NE:
                        load_w2(e + 1)
                barrier()

            if upto < 5:
                raise _Stop()
            with contextlib.ExitStack() as PE_:
                RE = lambda *a: PE_.enter_context(nc.sbuf_tensor(*a, side="right"))
                HD = RE("HD", [128, DMC * TT], BF16)
                ACB = RE("ACB", [128, DMC * TT], BF16)
                SQD = RE("SQD", [128, DMC * TT], BF16)
                MT4 = RE("MT4", [128, TT], F32)
                VT4 = RE("VT4", [128, TT], F32)
                T14 = RE("T14", [128, TT], F32)
                RT4 = RE("RT4", [128, TT], F32)
                OSB = RE("OSB", [128, DMC * TT], F32)
                PS4 = PE_.enter_context(nc.psum_tensor("PS4", [128, TT], F32))
                PQ4 = PE_.enter_context(nc.psum_tensor("PQ4", [128, TT], F32))

                for ti in range(NT):
                    for hf in range(2):
                        dma("sync",
                            HD[:].rearrange("p (c t) -> p c t", c=DMC)
                            [:, :, hf * 256:(hf + 1) * 256],
                            h_hbm[2 * ti + hf].rearrange("c p t -> p c t"),
                            r=[h_hbm], w=[HD])
                    for c in range(DMC):
                        on("scalar", (lambda c, ti: lambda E: E.activation(
                            ACB[:, c * TT:(c + 1) * TT],
                            ACC[:, c * tok + ti * TT:c * tok + (ti + 1) * TT],
                            AF.Copy))(c, ti),
                            r=[ACC], w=[ACB])
                    for c in range(DMC):
                        on("scalar", (lambda c: lambda E: E.activation(
                            SQD[:, c * TT:(c + 1) * TT], ACB[:, c * TT:(c + 1) * TT],
                            AF.Square))(c), r=[ACB], w=[SQD])
                    for c in range(DMC):
                        on("tensor", (lambda c: lambda E: E.matmul(
                            PS4[:], ONESB[:], ACB[:, c * TT:(c + 1) * TT],
                            start=(c == 0), stop=(c == DMC - 1)))(c),
                            r=[ONESB, ACB], w=[PS4], defer=(c < DMC - 1))
                    for c in range(DMC):
                        on("tensor", (lambda c: lambda E: E.matmul(
                            PQ4[:], ONESB[:], SQD[:, c * TT:(c + 1) * TT],
                            start=(c == 0), stop=(c == DMC - 1)))(c),
                            r=[ONESB, SQD], w=[PQ4], defer=(c < DMC - 1))
                    on("vector", lambda E: E.tensor_scalar(
                        MT4[:], PS4[:], 1.0 / DM, None, ALU.mult), r=[PS4], w=[MT4])
                    on("vector", lambda E: E.tensor_scalar(
                        VT4[:], PQ4[:], 1.0 / DM, None, ALU.mult), r=[PQ4], w=[VT4])
                    on("vector", lambda E: E.tensor_mul(T14[:], MT4[:], MT4[:]),
                       r=[MT4], w=[T14])
                    on("vector", lambda E: E.tensor_sub(VT4[:], VT4[:], T14[:]),
                       r=[VT4, T14], w=[VT4])
                    on("vector", lambda E: E.tensor_scalar(
                        VT4[:], VT4[:], LN_EPS, None, ALU.add), r=[VT4], w=[VT4])
                    on("vector", lambda E: E.reciprocal(T14[:], VT4[:]), r=[VT4], w=[T14])
                    on("scalar", lambda E: E.activation(RT4[:], T14[:], AF.Sqrt),
                       r=[T14], w=[RT4])
                    for c in range(DMC):
                        on("vector", (lambda c, ti: lambda E: E.tensor_sub(
                            VT4[:], ACC[:, c * tok + ti * TT:c * tok + (ti + 1) * TT],
                            MT4[:]))(c, ti), r=[ACC, MT4], w=[VT4])
                        on("vector", lambda E: E.tensor_mul(VT4[:], VT4[:], RT4[:]),
                           r=[VT4, RT4], w=[VT4])
                        on("vector", (lambda c: lambda E: E.tensor_add(
                            OSB[:, c * TT:(c + 1) * TT], VT4[:],
                            HD[:, c * TT:(c + 1) * TT]))(c),
                            r=[VT4, HD], w=[OSB])
                    dma("sync", outT[:, ti * TT:(ti + 1) * TT].rearrange(
                        "(c p) t -> p c t", p=128), cview(OSB, DMC),
                        r=[OSB], w=[outT])

        except _Stop:
            pass
        on("sync", lambda E: E.nop(),
           waits=[("sy", now("sy")), ("gq", now("gq"))], defer=True)
        on("gpsimd", lambda E: E.nop(),
           waits=[("gq", now("gq")), ("sy", now("sy"))], defer=True)

        with nc.Block() as block:
            block.tensor(lambda E: [f(E) for f in streams["tensor"]])
            block.vector(lambda E: [f(E) for f in streams["vector"]])
            block.scalar(lambda E: [f(E) for f in streams["scalar"]])
            block.sync(lambda E: [f(E) for f in streams["sync"]])
            block.gpsimd(lambda E: [f(E) for f in streams["gpsimd"]])

    return nc


def make_aux():
    masks = []
    for c in range(NCORES):
        m = np.zeros((64, 8), np.float32)
        for k in range(4 * (c // 4), c):
            m[:, k] = 1.0
        masks.append(m)
    sel = np.zeros((8, 1024), np.float32)
    for e in range(8):
        sel[e, 128 * e:128 * (e + 1)] = 1.0
    return masks, sel.astype(BFNP), np.eye(128, dtype=np.float32)


def make_in_maps(inputs, tok=2048):
    x = np.asarray(inputs["L0_attn_x"], np.float32)
    B, S, _ = x.shape
    xf = x.reshape(B * S, DM)
    masks, sel, idm = make_aux()
    shared = {
        "W_in": np.asarray(inputs["L0_attn_W_in"], np.float32),
        "b_in": np.asarray(inputs["L0_attn_b_in"], np.float32).reshape(32, 128),
        "W_xp": np.asarray(inputs["L0_attn_W_xp"], np.float32),
        "W_dt": np.asarray(inputs["L0_attn_W_dt"], np.float32),
        "b_dt": np.asarray(inputs["L0_attn_b_dt"], np.float32).reshape(8, 128),
        "W_out": np.asarray(inputs["L0_attn_W_out"], np.float32),
        "b_out": np.asarray(inputs["L0_attn_b_out"], np.float32).reshape(8, 128),
        "bxa": np.stack([np.asarray(inputs["L0_attn_b_xp"], np.float32),
                         np.asarray(inputs["L0_attn_A_log"], np.float32)]),
        "Dv": np.asarray(inputs["L0_attn_D"], np.float32).reshape(8, 128),
        "gate_W": np.asarray(inputs["L0_ffn_gate_W"], np.float32),
        "W1": np.asarray(inputs["L0_ffn_W1"], np.float32),
        "b1": np.asarray(inputs["L0_ffn_b1"], np.float32).reshape(NE, 16, 128),
        "W2": np.asarray(inputs["L0_ffn_W2"], np.float32),
        "b2": np.asarray(inputs["L0_ffn_b2"], np.float32).reshape(NE, 8, 128),
        "idm": idm, "sel8": sel,
    }
    in_maps = []
    for c in range(NCORES):
        sl = xf[c * tok:(c + 1) * tok]
        in_maps.append({**shared,
                        "xT": np.ascontiguousarray(sl.T),
                        "maskI": masks[c]})
    return in_maps


def kernel(**inputs):
    tok = 2048
    nc = build_nc(tok)
    in_maps = make_in_maps(inputs, tok)
    res = run_bass_kernel_spmd(nc, in_maps, core_ids=list(range(NCORES)))
    x = np.asarray(inputs["L0_attn_x"])
    B, S, _ = x.shape
    parts = [np.asarray(res.results[c]["outT"]).T for c in range(NCORES)]
    out = np.concatenate(parts, axis=0).reshape(B, S, DM).astype(np.float32)
    return out


# version 04
# speedup vs baseline: 1.1902x; 1.0527x over previous
import contextlib
import numpy as np
import ml_dtypes
import concourse.bass as bass
import concourse.mybir as mybir
from concourse.bass_utils import run_bass_kernel_spmd
from concourse.replica_groups import maybe_share_collective_output_space

F32 = mybir.dt.float32
BF16 = mybir.dt.bfloat16
AF = mybir.ActivationFunctionType
ALU = mybir.AluOpType
AX = mybir.AxisListType
BFNP = ml_dtypes.bfloat16

NCORES = 8
DM = 1024
DI = 2048
DS = 64
DH = 2048
NE = 8
DMC = DM // 128
DIC = DI // 128
LN_EPS = 1e-5


def _nm(t):
    n = getattr(t, "name", None)
    if isinstance(n, str):
        return n
    try:
        return t.name()
    except Exception:
        return str(t)


class _Stop(Exception):
    pass


def build_nc(tok=2048, upto=99):
    TT = 512
    NT = tok // TT
    NG = tok // 128

    nc = bass.Bass("TRN2", target_bir_lowering=False)

    xT = nc.dram_tensor("xT", [DM, tok], F32, kind="ExternalInput")
    W_in = nc.dram_tensor("W_in", [DM, 2 * DI], F32, kind="ExternalInput")
    b_in = nc.dram_tensor("b_in", [32, 128], F32, kind="ExternalInput")
    W_xp = nc.dram_tensor("W_xp", [DI, DS], F32, kind="ExternalInput")
    W_dt = nc.dram_tensor("W_dt", [DI, DM], F32, kind="ExternalInput")
    b_dt = nc.dram_tensor("b_dt", [8, 128], F32, kind="ExternalInput")
    W_out = nc.dram_tensor("W_out", [DI, DM], F32, kind="ExternalInput")
    b_out = nc.dram_tensor("b_out", [8, 128], F32, kind="ExternalInput")
    bxa = nc.dram_tensor("bxa", [2, 64], F32, kind="ExternalInput")
    Dv = nc.dram_tensor("Dv", [8, 128], F32, kind="ExternalInput")
    gate_W = nc.dram_tensor("gate_W", [DM, NE], F32, kind="ExternalInput")
    W1 = nc.dram_tensor("W1", [NE, DM, DH], F32, kind="ExternalInput")
    b1 = nc.dram_tensor("b1", [NE, 16, 128], F32, kind="ExternalInput")
    W2 = nc.dram_tensor("W2", [NE, DH, DM], F32, kind="ExternalInput")
    b2 = nc.dram_tensor("b2", [NE, 8, 128], F32, kind="ExternalInput")
    maskI = nc.dram_tensor("maskI", [64, 8], F32, kind="ExternalInput")
    idm = nc.dram_tensor("idm", [128, 128], F32, kind="ExternalInput")
    sel8 = nc.dram_tensor("sel8", [8, 1024], BF16, kind="ExternalInput")
    outT = nc.dram_tensor("outT", [DM, tok], F32, kind="ExternalOutput")

    xi_hbm = nc.dram_tensor("xi_hbm", [NT, 128, DIC * TT], BF16, kind="Internal")
    sr_hbm = nc.dram_tensor("sr_hbm", [NT, 128, DIC * TT], BF16, kind="Internal")
    xn_hbm = nc.dram_tensor("xn_hbm", [NT, 128, DMC * TT], BF16, kind="Internal")
    NTB = tok // 256
    h_hbm = nc.dram_tensor("h_hbm", [NTB, DMC, 128, 256], BF16, kind="Internal")
    ssum_hbm = nc.dram_tensor("ssum_hbm", [128, tok], F32, kind="Internal")
    cw_hbm = nc.dram_tensor("cw_hbm", [8, tok], BF16, kind="Internal")
    xg_hbm = nc.dram_tensor("xg_hbm", [DMC, 128, tok], BF16, kind="Internal")
    rg = [list(range(NCORES))]
    space = maybe_share_collective_output_space("AllGather", rg)
    ag_in = nc.dram_tensor("ag_in", [2, 64], F32, kind="Internal")
    ag_out = nc.dram_tensor("ag_out", [16, 64], F32, kind="Internal",
                            addr_space=space)

    ENGS = ("tensor", "vector", "scalar", "sync", "gpsimd")
    ESEM = {"tensor": "t", "vector": "v", "scalar": "s"}
    streams = {e: [] for e in ENGS}
    cnt = {}
    wm = {e: {} for e in ENGS}
    SEM = {}
    last_w = {}
    readers = {}
    pending = {e: [] for e in ENGS}
    regions = []
    init_waits = {}

    def _alloc(stack, guard):
        h = stack.enter_context(guard)
        try:
            ml = nc.lookup_mloc(h)
            sp = str(h.space)
            sz = ml.size() if callable(ml.size) else ml.size
            lo, hi = ml.addr, ml.addr + sz
        except Exception:
            return h
        req = {}
        for sp2, lo2, hi2, nm2 in regions:
            if sp2 == sp and lo < hi2 and lo2 < hi:
                for ev in ([last_w[nm2]] if nm2 in last_w else []):
                    req[ev[0]] = max(req.get(ev[0], 0), ev[1])
                for s2, v2 in readers.get(nm2, {}).items():
                    req[s2] = max(req.get(s2, 0), v2)
        regions.append((sp, lo, hi, h.name))
        if req:
            init_waits[h.name] = req
        return h

    def _reg_read(name, ev):
        d = readers.setdefault(name, {})
        d[ev[0]] = max(d.get(ev[0], 0), ev[1])

    def _reg_write(name, ev):
        last_w[name] = ev
        readers[name] = {}

    def on(eng, f, r=(), w=(), waits=(), sem=None, amt=1, defer=False):
        rn = [_nm(t) for t in r]
        wn = [_nm(t) for t in w]
        auto = {}
        for n in rn:
            if n in last_w:
                s_, v_ = last_w[n]
                auto[s_] = max(auto.get(s_, 0), v_)
        for n in wn:
            if n in last_w:
                s_, v_ = last_w[n]
                auto[s_] = max(auto.get(s_, 0), v_)
            for s_, v_ in readers.get(n, {}).items():
                auto[s_] = max(auto.get(s_, 0), v_)
        for s_, v_ in waits:
            if v_ > 0:
                auto[s_] = max(auto.get(s_, 0), v_)
        for n in rn + wn:
            iw = init_waits.pop(n, None)
            if iw:
                for s_, v_ in iw.items():
                    auto[s_] = max(auto.get(s_, 0), v_)
        if sem is None and not defer:
            sem = ESEM.get(eng)
            if sem is None:
                raise RuntimeError(f"need explicit sem for {eng}")
        wl = sorted(auto.items())

        def op(E):
            for s_, v_ in wl:
                if wm[eng].get(s_, 0) < v_:
                    E.wait_ge(SEM[s_], v_)
                    wm[eng][s_] = v_
            ins = f(E)
            if not defer:
                ins.then_inc(SEM[sem], amt)
        streams[eng].append(op)
        if defer:
            pending[eng].extend([("r", n) for n in rn] + [("w", n) for n in wn])
            return None
        cnt[sem] = cnt.get(sem, 0) + amt
        ev = (sem, cnt[sem])
        for kind, n in pending[eng]:
            (_reg_read if kind == "r" else _reg_write)(n, ev)
        pending[eng] = []
        for n in rn:
            _reg_read(n, ev)
        for n in wn:
            _reg_write(n, ev)
        return ev

    def now(s):
        return cnt.get(s, 0)

    def dma(eng, out_ap, in_ap, r, w, waits=()):
        q = "sy" if eng == "sync" else "gq"
        chain = [(q, cnt.get(q, 0))]
        return on(eng, lambda E: E.dma_start(out=out_ap, in_=in_ap),
                  r=r, w=w, waits=list(waits) + chain, sem=q, amt=16)

    def cview(sb, nch):
        return sb[:].rearrange("p (c j) -> p c j", c=nch)

    def wload(eng, dst_sb, src2d, nch, parts, r, w, alt=False):
        s3 = src2d.rearrange("(c p) j -> p c j", p=128)
        d3 = cview(dst_sb, nch)
        step = nch // parts
        for i in range(parts):
            e = ("sync" if i % 2 else "gpsimd") if alt else eng
            if alt and e == "sync":
                e = "gpsimd"
            dma(e, d3[:, i * step:(i + 1) * step, :],
                s3[:, i * step:(i + 1) * step, :], r=r, w=w)

    def barrier():
        on("vector", lambda E: E.memzero(FLUSH[0:1, 0:4]), w=[FLUSH])
        on("scalar", lambda E: E.memzero(FLUSH[0:1, 4:8]), w=[FLUSH])
        snap = {s: now(s) for s in ("t", "v", "s", "sy", "gq", "cc")}
        own = {"tensor": "t", "vector": "v", "scalar": "s"}
        for eng in ENGS:
            wl = [(s, v) for s, v in snap.items() if v > 0 and own.get(eng) != s]
            on(eng, lambda E: E.nop(), waits=wl, defer=True)

    with contextlib.ExitStack() as STK:
        for s in ("t", "v", "s", "sy", "gq", "cc"):
            SEM[s] = STK.enter_context(nc.semaphore("sem_" + s))
            cnt[s] = 0

        L = lambda *a: _alloc(STK, nc.sbuf_tensor(*a, side="left"))
        FLUSH = L("FLUSH", [1, 8], F32)
        ID = L("ID", [128, 128], F32)
        ONESB = L("ONESB", [128, 128], BF16)
        MASK = L("MASK", [64, 8], F32)
        BINT = L("BINT", [128, 32], F32)
        BDTT = L("BDTT", [128, 8], F32)
        BOUT = L("BOUT", [128, 8], F32)
        DT = L("DT", [128, 8], F32)
        BXAT = L("BXAT", [64, 2], F32)
        ASC = L("ASC", [64, 1], F32)
        GWF = L("GWF", [128, 8 * 8], F32)
        BSTG1 = L("BSTG1", [32, 128], F32)
        BSTG2 = L("BSTG2", [8, 128], F32)
        BSTG3 = L("BSTG3", [8, 128], F32)
        BSTG4 = L("BSTG4", [8, 128], F32)
        BSTG5 = L("BSTG5", [2, 64], F32)

        dma("gpsimd", ID[:], idm[:], r=[idm], w=[ID])
        dma("gpsimd", MASK[:], maskI[:], r=[maskI], w=[MASK])
        dma("gpsimd", BSTG1[:], b_in[:], r=[b_in], w=[BSTG1])
        dma("gpsimd", BSTG2[:], b_dt[:], r=[b_dt], w=[BSTG2])
        dma("gpsimd", BSTG3[:], b_out[:], r=[b_out], w=[BSTG3])
        dma("gpsimd", BSTG4[:], Dv[:], r=[Dv], w=[BSTG4])
        dma("gpsimd", BSTG5[:], bxa[:], r=[bxa], w=[BSTG5])
        dma("sync", cview(GWF, DMC),
            gate_W.ap().rearrange("(c p) e -> p c e", p=128), r=[gate_W], w=[GWF])
        on("vector", lambda E: E.memset(ONESB[:], 1.0), w=[ONESB])

        with contextlib.ExitStack() as P0:
            PT = _alloc(P0, nc.psum_tensor("PT0", [128, 32], F32))
            for (stg, dst) in ((BSTG1, BINT), (BSTG2, BDTT), (BSTG3, BOUT),
                               (BSTG4, DT), (BSTG5, BXAT)):
                p, q = stg.shape[0], stg.shape[1]
                on("tensor",
                   (lambda stg, p, q: lambda E: E.transpose(
                       PT[0:q, 0:p], stg[:], ID[0:p, 0:p]))(stg, p, q),
                   r=[stg, ID], w=[PT])
                on("vector",
                   (lambda dst, p, q: lambda E: E.tensor_copy(
                       dst[0:q, 0:p], PT[0:q, 0:p]))(dst, p, q),
                   r=[PT], w=[dst])
            on("scalar", lambda E: E.activation(ASC[:], BXAT[:, 1:2], AF.Exp),
               r=[BXAT], w=[ASC])
            on("vector", lambda E: E.tensor_scalar(
                ASC[:], ASC[:], -1.0 / DM, None, ALU.mult), r=[ASC], w=[ASC])

        try:
            if upto < 1:
                raise _Stop()
            with contextlib.ExitStack() as PA:
                RA = lambda *a: _alloc(PA, nc.sbuf_tensor(*a, side="right"))
                WIN = RA("WIN", [128, DMC * 4096], BF16)
                XTf = [RA("XTf0", [128, DMC * TT], F32), RA("XTf1", [128, DMC * TT], F32)]
                XB = RA("XB", [128, DMC * TT], BF16)
                SQ = RA("SQ", [128, DMC * TT], BF16)
                XN = RA("XN", [128, DMC * TT], BF16)
                XIT = RA("XIT", [128, DIC * TT], BF16)
                SRT = RA("SRT", [128, DIC * TT], BF16)
                MT = RA("MT", [128, TT], F32)
                VT = RA("VT", [128, TT], F32)
                T1 = RA("T1", [128, TT], F32)
                RT = RA("RT", [128, TT], F32)
                PS_S = _alloc(PA, nc.psum_tensor("PS_S", [128, TT], F32))
                PS_Q = _alloc(PA, nc.psum_tensor("PS_Q", [128, TT], F32))
                PX = [_alloc(PA, nc.psum_tensor("PXa", [128, TT], F32)),
                      _alloc(PA, nc.psum_tensor("PXb", [128, TT], F32))]

                wload("gpsimd", WIN, W_in.ap(), DMC, 2, r=[W_in], w=[WIN])

                XN2 = [XN, RA("XNq", [128, DMC * TT], BF16)]

                def a1_stage1(ti):
                    XTt = XTf[ti % 2]
                    XNt = XN2[ti % 2]
                    dma("sync", cview(XTt, DMC),
                        xT[:, ti * TT:(ti + 1) * TT].rearrange("(c p) t -> p c t", p=128),
                        r=[xT], w=[XTt])
                    for c in range(DMC):
                        on("vector", (lambda c, XTt: lambda E: E.tensor_copy(
                            XB[:, c * TT:(c + 1) * TT], XTt[:, c * TT:(c + 1) * TT]))(c, XTt),
                            r=[XTt], w=[XB])
                    for c in range(DMC):
                        on("scalar", (lambda c: lambda E: E.activation(
                            SQ[:, c * TT:(c + 1) * TT], XB[:, c * TT:(c + 1) * TT],
                            AF.Square))(c), r=[XB], w=[SQ])
                    for c in range(DMC):
                        on("tensor", (lambda c: lambda E: E.matmul(
                            PS_S[:], ONESB[:], XB[:, c * TT:(c + 1) * TT],
                            start=(c == 0), stop=(c == DMC - 1)))(c),
                            r=[ONESB, XB], w=[PS_S], defer=(c < DMC - 1))
                    for c in range(DMC):
                        on("tensor", (lambda c: lambda E: E.matmul(
                            PS_Q[:], ONESB[:], SQ[:, c * TT:(c + 1) * TT],
                            start=(c == 0), stop=(c == DMC - 1)))(c),
                            r=[ONESB, SQ], w=[PS_Q], defer=(c < DMC - 1))
                    on("vector", lambda E: E.tensor_scalar(
                        MT[:], PS_S[:], 1.0 / DM, None, ALU.mult), r=[PS_S], w=[MT])
                    on("vector", lambda E: E.tensor_scalar(
                        VT[:], PS_Q[:], 1.0 / DM, None, ALU.mult), r=[PS_Q], w=[VT])
                    on("vector", lambda E: E.tensor_mul(T1[:], MT[:], MT[:]),
                       r=[MT], w=[T1])
                    on("vector", lambda E: E.tensor_sub(VT[:], VT[:], T1[:]),
                       r=[VT, T1], w=[VT])
                    on("vector", lambda E: E.tensor_scalar(
                        VT[:], VT[:], LN_EPS, None, ALU.add), r=[VT], w=[VT])
                    on("vector", lambda E: E.reciprocal(T1[:], VT[:]), r=[VT], w=[T1])
                    on("scalar", lambda E: E.activation(RT[:], T1[:], AF.Sqrt),
                       r=[T1], w=[RT])
                    for c in range(DMC):
                        on("vector", (lambda c, XTt: lambda E: E.tensor_sub(
                            VT[:], XTt[:, c * TT:(c + 1) * TT], MT[:]))(c, XTt),
                            r=[XTt, MT], w=[VT])
                        on("vector", (lambda c, XNt: lambda E: E.tensor_mul(
                            XNt[:, c * TT:(c + 1) * TT], VT[:], RT[:]))(c, XNt),
                            r=[VT, RT], w=[XNt])

                def a1_stage2(ti):
                    XNt = XN2[ti % 2]
                    for j in range(32):
                        pb = j % 2
                        for c in range(DMC):
                            on("tensor", (lambda j, c, pb, XNt: lambda E: E.matmul(
                                PX[pb][:],
                                WIN[:, c * 4096 + j * 128:c * 4096 + (j + 1) * 128],
                                XNt[:, c * TT:(c + 1) * TT],
                                start=(c == 0), stop=(c == DMC - 1)))(j, c, pb, XNt),
                                r=[WIN, XNt], w=[PX[pb]], defer=(c < DMC - 1))
                        dst = XIT if j < 16 else SRT
                        jj = j if j < 16 else j - 16
                        on("scalar", (lambda j, jj, dst, pb: lambda E: E.activation(
                            dst[:, jj * TT:(jj + 1) * TT], PX[pb][:], AF.Silu,
                            bias=BINT[:, j:j + 1]))(j, jj, dst, pb),
                            r=[PX[pb], BINT], w=[dst])
                        if j >= 16:
                            on("vector", (lambda jj: lambda E: E.tensor_mul(
                                SRT[:, jj * TT:(jj + 1) * TT],
                                SRT[:, jj * TT:(jj + 1) * TT],
                                XIT[:, jj * TT:(jj + 1) * TT]))(jj),
                                r=[SRT, XIT], w=[SRT])
                    dma("sync", xi_hbm[ti], XIT[:], r=[XIT], w=[xi_hbm])
                    dma("gpsimd", sr_hbm[ti], SRT[:], r=[SRT], w=[sr_hbm])
                    dma("gpsimd", xn_hbm[ti], XNt[:], r=[XNt], w=[xn_hbm])

                a1_stage1(0)
                for ti in range(NT):
                    if ti + 1 < NT:
                        a1_stage1(ti + 1)
                    a1_stage2(ti)

            if upto < 2:
                raise _Stop()
            with contextlib.ExitStack() as PB:
                RB = lambda *a: _alloc(PB, nc.sbuf_tensor(*a, side="right"))
                WDT = RB("WDT", [128, DIC * 1024], BF16)
                WXP = RB("WXP", [128, DIC * 64], BF16)
                BX = RB("BX", [64, tok], F32)
                AFULL = RB("AFULL", [64, tok], F32)
                XIJ = [RB("XIJ0", [128, DIC * TT], BF16), RB("XIJ1", [128, DIC * TT], BF16)]
                SPB = RB("SPB", [128, DMC * TT], BF16)
                TA = RB("TA", [128, TT], F32)
                TB = RB("TB", [128, TT], F32)
                TC = RB("TC", [128, TT], F32)
                TD = RB("TD", [128, TT], F32)
                PD = [_alloc(PB, nc.psum_tensor("PDa", [128, TT], F32)),
                      _alloc(PB, nc.psum_tensor("PDb", [128, TT], F32))]
                PMD = _alloc(PB, nc.psum_tensor("PMD", [128, TT], F32))
                PBX = _alloc(PB, nc.psum_tensor("PBX", [64, TT], F32))
                PSS = _alloc(PB, nc.psum_tensor("PSS", [128, TT], F32))
                PSC = _alloc(PB, nc.psum_tensor("PSC", [128, 128], F32))

                A2L = (upto - 210) if 210 <= upto < 220 else 9
                wload("gpsimd", WDT, W_dt.ap(), DIC, 4, r=[W_dt], w=[WDT])
                wload("gpsimd", WXP, W_xp.ap(), DIC, 1, r=[W_xp], w=[WXP])

                for ti in range(NT):
                    XJ = XIJ[ti % 2]
                    dma("sync", XJ[:], xi_hbm[ti], r=[xi_hbm], w=[XJ])
                    for j8 in range(DMC if A2L >= 2 else 0):
                        pb = j8 % 2
                        for k in range(DIC):
                            on("tensor", (lambda j8, k, pb, XJ: lambda E: E.matmul(
                                PD[pb][:],
                                WDT[:, k * 1024 + j8 * 128:k * 1024 + (j8 + 1) * 128],
                                XJ[:, k * TT:(k + 1) * TT],
                                start=(k == 0), stop=(k == DIC - 1)))(j8, k, pb, XJ),
                                r=[WDT, XJ], w=[PD[pb]], defer=(k < DIC - 1))
                        on("scalar", (lambda j8, pb: lambda E: E.activation(
                            TA[:], PD[pb][:], AF.Abs, bias=BDTT[:, j8:j8 + 1]))(j8, pb),
                            r=[PD[pb], BDTT], w=[TA])
                        on("scalar", lambda E: E.activation(
                            TB[:], TA[:], AF.Exp, scale=-1.0), r=[TA], w=[TB])
                        on("scalar", lambda E: E.activation(
                            TC[:], TB[:], AF.Ln, bias=1.0), r=[TB], w=[TC])
                        on("scalar", (lambda j8, pb: lambda E: E.activation(
                            TD[:], PD[pb][:], AF.Relu,
                            bias=BDTT[:, j8:j8 + 1]))(j8, pb),
                            r=[PD[pb], BDTT], w=[TD])
                        on("vector", (lambda j8: lambda E: E.tensor_add(
                            SPB[:, j8 * TT:(j8 + 1) * TT], TD[:], TC[:]))(j8),
                            r=[TD, TC], w=[SPB])
                        on("tensor", (lambda j8: lambda E: E.matmul(
                            PMD[:], ONESB[:], SPB[:, j8 * TT:(j8 + 1) * TT],
                            start=(j8 == 0), stop=(j8 == DMC - 1)))(j8),
                            r=[ONESB, SPB], w=[PMD], defer=(j8 < DMC - 1))
                    if A2L >= 3:
                        on("scalar", (lambda ti: lambda E: E.activation(
                            AFULL[:, ti * TT:(ti + 1) * TT], PMD[0:64, :], AF.Exp,
                            scale=ASC[:, 0:1]))(ti),
                            r=[PMD, ASC], w=[AFULL])
                    for k in range(DIC if A2L >= 4 else 0):
                        on("tensor", (lambda k, XJ: lambda E: E.matmul(
                            PBX[:], WXP[:, 64 * k:64 * (k + 1)],
                            XJ[:, k * TT:(k + 1) * TT],
                            start=(k == 0), stop=(k == DIC - 1)))(k, XJ),
                            r=[WXP, XJ], w=[PBX], defer=(k < DIC - 1))
                    if A2L >= 4:
                        on("vector", (lambda ti: lambda E: E.tensor_scalar(
                            BX[:, ti * TT:(ti + 1) * TT], PBX[:], BXAT[:, 0:1], None,
                            ALU.add))(ti),
                            r=[PBX, BXAT], w=[BX])

                if upto < 22:
                    raise _Stop()
                ZER = RB("ZER", [64, tok], F32)
                STATE = RB("STATE", [64, tok], F32)
                CP = RB("CP", [64, tok], F32)
                STCB = RB("STCB", [64, tok], BF16)
                SUM2 = RB("SUM2", [64, 2], F32)
                SUM2T = RB("SUM2T", [2, 64], F32)
                AGSB = RB("AGSB", [16, 64], F32)
                AGT = RB("AGT", [64, 16], F32)
                SIN = RB("SIN", [64, 1], F32)
                TF = RB("TF", [64, 1], F32)
                SSB = RB("SSB", [128, tok], F32)

                on("vector", lambda E: E.memset(ZER[:], 0.0), w=[ZER])
                on("vector", lambda E: E.tensor_tensor_scan(
                    STATE[:], AFULL[:], BX[:], 0.0, ALU.mult, ALU.add),
                    r=[AFULL, BX], w=[STATE])
                on("vector", lambda E: E.tensor_tensor_scan(
                    CP[:], AFULL[:], ZER[:], 1.0, ALU.mult, ALU.add),
                    r=[AFULL, ZER], w=[CP])
                on("vector", lambda E: E.tensor_copy(SUM2[:, 0:1], CP[:, tok - 1:tok]),
                   r=[CP], w=[SUM2])
                on("vector", lambda E: E.tensor_copy(SUM2[:, 1:2], STATE[:, tok - 1:tok]),
                   r=[STATE], w=[SUM2])
                on("tensor", lambda E: E.transpose(
                    PSC[0:2, 0:64], SUM2[:], ID[0:64, 0:64]), r=[SUM2, ID], w=[PSC])
                on("vector", lambda E: E.tensor_copy(SUM2T[:], PSC[0:2, 0:64]),
                   r=[PSC], w=[SUM2T])
                if upto < 23:
                    raise _Stop()
                dma("gpsimd", ag_in[:], SUM2T[:], r=[SUM2T], w=[ag_in])
                on("gpsimd", lambda E: E.collective_compute(
                    "AllGather", ALU.bypass, replica_groups=rg,
                    ins=[ag_in[:]], outs=[ag_out[:]]),
                    r=[ag_in], w=[ag_out], sem="cc", amt=1)
                dma("gpsimd", AGSB[:], ag_out[:], r=[ag_out], w=[AGSB])
                on("tensor", lambda E: E.transpose(
                    PSC[0:64, 0:16], AGSB[:], ID[0:16, 0:16]), r=[AGSB, ID], w=[PSC])
                on("vector", lambda E: E.tensor_copy(AGT[:], PSC[0:64, 0:16]),
                   r=[PSC], w=[AGT])
                if upto < 24:
                    raise _Stop()
                on("vector", lambda E: E.memset(SIN[:], 0.0), w=[SIN])
                for k in range(NCORES):
                    on("vector", (lambda k: lambda E: E.scalar_tensor_tensor(
                        TF[:], SIN[:], AGT[:, 2 * k:2 * k + 1],
                        AGT[:, 2 * k + 1:2 * k + 2], ALU.mult, ALU.add))(k),
                        r=[SIN, AGT], w=[TF])
                    on("vector", lambda E: E.tensor_sub(TF[:], TF[:], SIN[:]),
                       r=[TF, SIN], w=[TF])
                    on("vector", (lambda k: lambda E: E.tensor_mul(
                        TF[:], TF[:], MASK[:, k:k + 1]))(k), r=[TF, MASK], w=[TF])
                    on("vector", lambda E: E.tensor_add(SIN[:], SIN[:], TF[:]),
                       r=[SIN, TF], w=[SIN])
                on("vector", lambda E: E.scalar_tensor_tensor(
                    STATE[:], CP[:], SIN[:, 0:1], STATE[:], ALU.mult, ALU.add),
                    r=[CP, SIN, STATE], w=[STATE])
                on("vector", lambda E: E.tensor_copy(STCB[:], STATE[:]),
                   r=[STATE], w=[STCB])
                if upto < 25:
                    raise _Stop()
                for i in range(NT):
                    on("tensor", (lambda i: lambda E: E.matmul(
                        PSS[:], ONESB[0:64, :], STCB[:, i * TT:(i + 1) * TT],
                        start=True, stop=True))(i),
                        r=[ONESB, STCB], w=[PSS])
                    on("vector", (lambda i: lambda E: E.tensor_copy(
                        SSB[:, i * TT:(i + 1) * TT], PSS[:]))(i), r=[PSS], w=[SSB])
                dma("sync", ssum_hbm[:], SSB[:], r=[SSB], w=[ssum_hbm])
    
            if upto < 3:
                raise _Stop()
            ACC = L("ACC", [128, DMC * tok], F32)
            TTB = 256
            NTB2 = tok // TTB

            with contextlib.ExitStack() as PC:
                RC = lambda *a: _alloc(PC, nc.sbuf_tensor(*a, side="right"))
                WOUT = RC("WOUT", [128, DIC * 1024], BF16)
                SRJ = [RC("SRJ0", [128, DIC * TTB], BF16),
                       RC("SRJ1", [128, DIC * TTB], BF16)]
                XTb = [RC("XTb0", [128, DMC * TTB], BF16),
                       RC("XTb1", [128, DMC * TTB], BF16)]
                XNb = [RC("XNb0", [128, DMC * TTB], BF16),
                       RC("XNb1", [128, DMC * TTB], BF16)]
                MIX = [RC("MIX0", [128, DMC * TTB], BF16),
                       RC("MIX1", [128, DMC * TTB], BF16)]
                HTb = [RC("HTb0", [128, DMC * TTB], BF16),
                       RC("HTb1", [128, DMC * TTB], BF16)]
                SQB = [RC("SQB0", [128, DMC * TTB], BF16),
                       RC("SQB1", [128, DMC * TTB], BF16)]
                SSBb = [RC("SSBb0", [128, TTB], F32), RC("SSBb1", [128, TTB], F32)]
                LGF = RC("LGF", [8, tok], F32)
                MT2 = [RC("MT2a", [128, TTB], F32), RC("MT2b", [128, TTB], F32)]
                RT2 = [RC("RT2a", [128, TTB], F32), RC("RT2b", [128, TTB], F32)]
                M3T = [RC("M3Ta", [128, TTB], F32), RC("M3Tb", [128, TTB], F32)]
                R3T = [RC("R3Ta", [128, TTB], F32), RC("R3Tb", [128, TTB], F32)]
                VT2 = RC("VT2", [128, TTB], F32)
                T12 = RC("T12", [128, TTB], F32)
                PY = [_alloc(PC, nc.psum_tensor("PYa", [128, TTB], F32)),
                      _alloc(PC, nc.psum_tensor("PYb", [128, TTB], F32))]
                PS2 = _alloc(PC, nc.psum_tensor("PS2", [128, TTB], F32))
                PQ2 = _alloc(PC, nc.psum_tensor("PQ2", [128, TTB], F32))
                PLG = _alloc(PC, nc.psum_tensor("PLG", [8, TTB], F32))

                wload("gpsimd", WOUT, W_out.ap(), DIC, 4, r=[W_out], w=[WOUT])

                def lnstats(src_bf, sq, mdst, rdst):
                    for c in range(DMC):
                        on("scalar", (lambda c: lambda E: E.activation(
                            sq[:, c * TTB:(c + 1) * TTB],
                            src_bf[:, c * TTB:(c + 1) * TTB],
                            AF.Square))(c), r=[src_bf], w=[sq])
                    for c in range(DMC):
                        on("tensor", (lambda c: lambda E: E.matmul(
                            PS2[:], ONESB[:], src_bf[:, c * TTB:(c + 1) * TTB],
                            start=(c == 0), stop=(c == DMC - 1)))(c),
                            r=[ONESB, src_bf], w=[PS2], defer=(c < DMC - 1))
                    for c in range(DMC):
                        on("tensor", (lambda c: lambda E: E.matmul(
                            PQ2[:], ONESB[:], sq[:, c * TTB:(c + 1) * TTB],
                            start=(c == 0), stop=(c == DMC - 1)))(c),
                            r=[ONESB, sq], w=[PQ2], defer=(c < DMC - 1))
                    on("vector", (lambda mdst: lambda E: E.tensor_scalar(
                        mdst[:], PS2[:], 1.0 / DM, None, ALU.mult))(mdst),
                        r=[PS2], w=[mdst])
                    on("vector", lambda E: E.tensor_scalar(
                        VT2[:], PQ2[:], 1.0 / DM, None, ALU.mult), r=[PQ2], w=[VT2])
                    on("vector", (lambda mdst: lambda E: E.tensor_mul(
                        T12[:], mdst[:], mdst[:]))(mdst), r=[mdst], w=[T12])
                    on("vector", lambda E: E.tensor_sub(VT2[:], VT2[:], T12[:]),
                       r=[VT2, T12], w=[VT2])
                    on("vector", lambda E: E.tensor_scalar(
                        VT2[:], VT2[:], LN_EPS, None, ALU.add), r=[VT2], w=[VT2])
                    on("vector", lambda E: E.reciprocal(T12[:], VT2[:]),
                       r=[VT2], w=[T12])
                    on("scalar", (lambda rdst: lambda E: E.activation(
                        rdst[:], T12[:], AF.Sqrt))(rdst), r=[T12], w=[rdst])

                def b_stage1(tb):
                    p = tb % 2
                    ti, hf = tb // 2, tb % 2
                    sr, xb_, xn_, mx, ss = SRJ[p], XTb[p], XNb[p], MIX[p], SSBb[p]
                    dma("sync", sr[:].rearrange("p (k t) -> p k t", k=DIC),
                        sr_hbm[ti].rearrange("p (k t) -> p k t", k=DIC)
                        [:, :, hf * TTB:(hf + 1) * TTB],
                        r=[sr_hbm], w=[sr])
                    dma("sync", xn_[:].rearrange("p (c t) -> p c t", c=DMC),
                        xn_hbm[ti].rearrange("p (c t) -> p c t", c=DMC)
                        [:, :, hf * TTB:(hf + 1) * TTB],
                        r=[xn_hbm], w=[xn_])
                    dma("gpsimd", cview(xb_, DMC),
                        xT[:, tb * TTB:(tb + 1) * TTB].rearrange(
                            "(c p) t -> p c t", p=128),
                        r=[xT], w=[xb_])
                    dma("gpsimd", ss[:], ssum_hbm[:, tb * TTB:(tb + 1) * TTB],
                        r=[ssum_hbm], w=[ss])
                    for k in range(DIC):
                        on("vector", (lambda k, sr, ss: lambda E: E.tensor_mul(
                            sr[:, k * TTB:(k + 1) * TTB], ss[:],
                            sr[:, k * TTB:(k + 1) * TTB]))(k, sr, ss),
                            r=[ss, sr], w=[sr])
                    for j8 in range(DMC):
                        pb = j8 % 2
                        for k in range(DIC):
                            on("tensor", (lambda j8, k, pb, sr: lambda E: E.matmul(
                                PY[pb][:],
                                WOUT[:, k * 1024 + j8 * 128:k * 1024 + (j8 + 1) * 128],
                                sr[:, k * TTB:(k + 1) * TTB],
                                start=(k == 0), stop=(k == DIC - 1)))(j8, k, pb, sr),
                                r=[WOUT, sr], w=[PY[pb]], defer=(k < DIC - 1))
                        on("vector", (lambda j8, pb: lambda E: E.tensor_scalar(
                            T12[:], PY[pb][:], BOUT[:, j8:j8 + 1], None,
                            ALU.add))(j8, pb),
                            r=[PY[pb], BOUT], w=[T12])
                        on("vector", (lambda j8, mx, xn_: lambda E: E.scalar_tensor_tensor(
                            mx[:, j8 * TTB:(j8 + 1) * TTB],
                            xn_[:, j8 * TTB:(j8 + 1) * TTB],
                            DT[:, j8:j8 + 1], T12[:], ALU.mult, ALU.add))(j8, mx, xn_),
                            r=[xn_, DT, T12], w=[mx])

                def b_stage2(tb):
                    p = tb % 2
                    mx, ht, sq, xb_ = MIX[p], HTb[p], SQB[p], XTb[p]
                    m2, r2, m3, r3 = MT2[p], RT2[p], M3T[p], R3T[p]
                    lnstats(mx, sq, m2, r2)
                    for c in range(DMC):
                        on("vector", (lambda c, mx, m2: lambda E: E.tensor_sub(
                            VT2[:], mx[:, c * TTB:(c + 1) * TTB], m2[:]))(c, mx, m2),
                            r=[mx, m2], w=[VT2])
                        on("vector", (lambda r2: lambda E: E.tensor_mul(
                            VT2[:], VT2[:], r2[:]))(r2), r=[VT2, r2], w=[VT2])
                        on("vector", (lambda c, ht, xb_: lambda E: E.tensor_add(
                            ht[:, c * TTB:(c + 1) * TTB], VT2[:],
                            xb_[:, c * TTB:(c + 1) * TTB]))(c, ht, xb_),
                            r=[VT2, xb_], w=[ht])
                    dma("sync", h_hbm[tb].rearrange("c p t -> p c t"), cview(ht, DMC),
                        r=[ht], w=[h_hbm])
                    lnstats(ht, sq, m3, r3)
                    for c in range(DMC):
                        on("vector", (lambda c, ht, m3: lambda E: E.tensor_sub(
                            VT2[:], ht[:, c * TTB:(c + 1) * TTB], m3[:]))(c, ht, m3),
                            r=[ht, m3], w=[VT2])
                        on("vector", (lambda c, tb, r3: lambda E: E.tensor_mul(
                            ACC[:, c * tok + tb * TTB:c * tok + (tb + 1) * TTB],
                            VT2[:], r3[:]))(c, tb, r3), r=[VT2, r3], w=[ACC])
                        on("scalar", (lambda c, tb, sq: lambda E: E.activation(
                            sq[:, c * TTB:(c + 1) * TTB],
                            ACC[:, c * tok + tb * TTB:c * tok + (tb + 1) * TTB],
                            AF.Copy))(c, tb, sq),
                            r=[ACC], w=[sq])
                    dma("sync",
                        xg_hbm[:, :, tb * TTB:(tb + 1) * TTB].rearrange(
                            "c p t -> p c t"),
                        cview(sq, DMC), r=[sq], w=[xg_hbm])
                    for c in range(DMC):
                        on("tensor", (lambda c, tb: lambda E: E.matmul(
                            PLG[:], GWF[:, 8 * c:8 * (c + 1)],
                            ACC[:, c * tok + tb * TTB:c * tok + (tb + 1) * TTB],
                            start=(c == 0), stop=(c == DMC - 1)))(c, tb),
                            r=[GWF, ACC], w=[PLG], defer=(c < DMC - 1))
                    on("vector", (lambda tb: lambda E: E.tensor_copy(
                        LGF[:, tb * TTB:(tb + 1) * TTB], PLG[:]))(tb),
                        r=[PLG], w=[LGF])

                b_stage1(0)
                for tb in range(NTB2):
                    if tb + 1 < NTB2:
                        b_stage1(tb + 1)
                    b_stage2(tb)

                CWFMB = RC("CWFMB", [8, tok], BF16)
                CWG = RC("CWG", [128, NG * 8], F32)
                EG = RC("EG", [128, NG * 8], F32)
                GG = RC("GG", [128, NG * 8], F32)
                K1 = RC("K1", [128, NG * 8], F32)
                GS = RC("GS", [128, NG * 8], F32)
                M1 = RC("M1", [128, NG], F32)
                M1N = RC("M1N", [128, NG], F32)
                S1 = RC("S1", [128, NG], F32)
                R1 = RC("R1", [128, NG], F32)
                M2 = RC("M2", [128, NG], F32)
                M3g = RC("M3g", [128, NG], F32)
                S2 = RC("S2", [128, NG], F32)
                R2 = RC("R2", [128, NG], F32)
                PGT = _alloc(PC, nc.psum_tensor("PGT", [128, NG * 8], F32))
                PGB = [_alloc(PC, nc.psum_tensor("PGB0", [8, 128], F32)),
                       _alloc(PC, nc.psum_tensor("PGB1", [8, 128], F32))]

                for g in range(NG):
                    on("tensor", (lambda g: lambda E: E.transpose(
                        PGT[:, 8 * g:8 * (g + 1)], LGF[:, 128 * g:128 * (g + 1)],
                        ID[0:8, 0:8]))(g),
                        r=[LGF, ID], w=[PGT], defer=(g < NG - 1))
                on("vector", lambda E: E.tensor_copy(CWG[:], PGT[:]), r=[PGT], w=[CWG])
                for g in range(NG):
                    on("vector", (lambda g: lambda E: E.reduce_max(
                        M1[:, g:g + 1], CWG[:, 8 * g:8 * (g + 1)], AX.X))(g),
                        r=[CWG], w=[M1])
                on("vector", lambda E: E.tensor_scalar(
                    M1N[:], M1[:], -1.0, None, ALU.mult), r=[M1], w=[M1N])
                for g in range(NG):
                    on("scalar", (lambda g: lambda E: E.activation(
                        EG[:, 8 * g:8 * (g + 1)], CWG[:, 8 * g:8 * (g + 1)], AF.Exp,
                        bias=M1N[:, g:g + 1]))(g), r=[CWG, M1N], w=[EG])
                for g in range(NG):
                    on("vector", (lambda g: lambda E: E.reduce_sum(
                        S1[:, g:g + 1], EG[:, 8 * g:8 * (g + 1)], AX.X))(g),
                        r=[EG], w=[S1])
                on("vector", lambda E: E.reciprocal(R1[:], S1[:]), r=[S1], w=[R1])
                for g in range(NG):
                    on("vector", (lambda g: lambda E: E.tensor_scalar(
                        GG[:, 8 * g:8 * (g + 1)], EG[:, 8 * g:8 * (g + 1)],
                        R1[:, g:g + 1], None, ALU.mult))(g), r=[EG, R1], w=[GG])
                for g in range(NG):
                    on("vector", (lambda g: lambda E: E.reduce_max(
                        M2[:, g:g + 1], GG[:, 8 * g:8 * (g + 1)], AX.X))(g),
                        r=[GG], w=[M2])
                for g in range(NG):
                    on("vector", (lambda g: lambda E: E.tensor_scalar(
                        K1[:, 8 * g:8 * (g + 1)], GG[:, 8 * g:8 * (g + 1)],
                        M2[:, g:g + 1], None, ALU.is_ge))(g), r=[GG, M2], w=[K1])
                on("vector", lambda E: E.scalar_tensor_tensor(
                    K1[:], K1[:], -1e9, GG[:], ALU.mult, ALU.add), r=[K1, GG], w=[K1])
                for g in range(NG):
                    on("vector", (lambda g: lambda E: E.reduce_max(
                        M3g[:, g:g + 1], K1[:, 8 * g:8 * (g + 1)], AX.X))(g),
                        r=[K1], w=[M3g])
                for g in range(NG):
                    on("vector", (lambda g: lambda E: E.tensor_scalar(
                        K1[:, 8 * g:8 * (g + 1)], GG[:, 8 * g:8 * (g + 1)],
                        M3g[:, g:g + 1], None, ALU.is_ge))(g), r=[GG, M3g], w=[K1])
                on("vector", lambda E: E.tensor_mul(GS[:], GG[:], K1[:]),
                   r=[GG, K1], w=[GS])
                for g in range(NG):
                    on("vector", (lambda g: lambda E: E.reduce_sum(
                        S2[:, g:g + 1], GS[:, 8 * g:8 * (g + 1)], AX.X))(g),
                        r=[GS], w=[S2])
                on("vector", lambda E: E.tensor_scalar(
                    S2[:], S2[:], 1e-8, None, ALU.add), r=[S2], w=[S2])
                on("vector", lambda E: E.reciprocal(R2[:], S2[:]), r=[S2], w=[R2])
                for g in range(NG):
                    on("vector", (lambda g: lambda E: E.tensor_scalar(
                        GS[:, 8 * g:8 * (g + 1)], GS[:, 8 * g:8 * (g + 1)],
                        R2[:, g:g + 1], None, ALU.mult))(g), r=[GS, R2], w=[GS])
                for g in range(NG):
                    gb = PGB[g % 2]
                    on("tensor", (lambda g, gb: lambda E: E.transpose(
                        gb[:], GS[:, 8 * g:8 * (g + 1)], ID[:]))(g, gb),
                        r=[GS, ID], w=[gb])
                    on("vector", (lambda g, gb: lambda E: E.tensor_copy(
                        CWFMB[:, 128 * g:128 * (g + 1)], gb[:]))(g, gb),
                        r=[gb], w=[CWFMB])
                dma("sync", cw_hbm[:], CWFMB[:], r=[CWFMB], w=[cw_hbm])
    
            if upto < 4:
                raise _Stop()
            with contextlib.ExitStack() as PD_:
                RD = lambda *a: _alloc(PD_, nc.sbuf_tensor(*a, side="right"))
                W1S = RD("W1S", [128, DMC * DH], BF16)
                W2S = RD("W2S", [128, DIC * DM], BF16)
                H1 = RD("H1", [128, DIC * TT], BF16)
                XGb = [RD("XGb0", [128, DMC * TT], BF16),
                       RD("XGb1", [128, DMC * TT], BF16)]
                CWB = RD("CWB", [128, tok], BF16)
                CWFM = RD("CWFM", [8, tok], BF16)
                SEL8 = RD("SEL8", [8, 1024], BF16)
                B1STG = RD("B1STG", [16, 128], F32)
                B2STG = RD("B2STG", [8, 128], F32)
                B1TA = RD("B1TA", [128, NE * 16], F32)
                B2TA = RD("B2TA", [128, NE * 8], F32)
                TE = RD("TE", [128, TT], F32)
                PH = [_alloc(PD_, nc.psum_tensor("PHa", [128, TT], F32)),
                      _alloc(PD_, nc.psum_tensor("PHb", [128, TT], F32))]
                PO = [_alloc(PD_, nc.psum_tensor("POa", [128, TT], F32)),
                      _alloc(PD_, nc.psum_tensor("POb", [128, TT], F32))]
                PCW = _alloc(PD_, nc.psum_tensor("PCW", [128, TT], F32))
                PBT = _alloc(PD_, nc.psum_tensor("PBT", [128, 16], F32))

                dma("gpsimd", SEL8[:], sel8[:], r=[sel8], w=[SEL8])
                dma("gpsimd", CWFM[:], cw_hbm[:], r=[cw_hbm], w=[CWFM])
                for e in range(NE):
                    dma("sync", B1STG[:], b1[e], r=[b1], w=[B1STG])
                    on("tensor", (lambda e: lambda E: E.transpose(
                        PBT[:, 0:16], B1STG[:], ID[0:16, 0:16]))(e),
                        r=[B1STG, ID], w=[PBT])
                    on("vector", (lambda e: lambda E: E.tensor_copy(
                        B1TA[:, 16 * e:16 * (e + 1)], PBT[:, 0:16]))(e),
                        r=[PBT], w=[B1TA])
                    dma("sync", B2STG[:], b2[e], r=[b2], w=[B2STG])
                    on("tensor", (lambda e: lambda E: E.transpose(
                        PBT[:, 0:8], B2STG[:], ID[0:8, 0:8]))(e),
                        r=[B2STG, ID], w=[PBT])
                    on("vector", (lambda e: lambda E: E.tensor_copy(
                        B2TA[:, 8 * e:8 * (e + 1)], PBT[:, 0:8]))(e),
                        r=[PBT], w=[B2TA])

                def load_w1(e):
                    wload("gpsimd", W1S, W1[e], DMC, 2, r=[W1], w=[W1S])

                def load_w2(e):
                    wload("gpsimd", W2S, W2[e], DIC, 4, r=[W2], w=[W2S])

                def load_xg(cti, buf):
                    dma("sync", cview(buf, DMC),
                        xg_hbm[:, :, cti * TT:(cti + 1) * TT].rearrange(
                            "c p t -> p c t"),
                        r=[xg_hbm], w=[buf])

                load_w1(0)
                load_w2(0)
                for i in range(min(2, NT)):
                    load_xg(i, XGb[i % 2])
                for e in range(NE):
                    for i in range(NT):
                        on("tensor", (lambda e, i: lambda E: E.matmul(
                            PCW[:], SEL8[:, 128 * e:128 * (e + 1)],
                            CWFM[:, i * TT:(i + 1) * TT], start=True, stop=True))(e, i),
                            r=[SEL8, CWFM], w=[PCW])
                        on("vector", (lambda i: lambda E: E.tensor_copy(
                            CWB[:, i * TT:(i + 1) * TT], PCW[:]))(i), r=[PCW], w=[CWB])
                    for ti in range(NT):
                        XGt = XGb[ti % 2]
                        for jh in range(DIC):
                            pb = jh % 2
                            for c in range(DMC):
                                on("tensor", (lambda jh, c, pb, XGt: lambda E: E.matmul(
                                    PH[pb][:],
                                    W1S[:, c * DH + jh * 128:c * DH + (jh + 1) * 128],
                                    XGt[:, c * TT:(c + 1) * TT],
                                    start=(c == 0), stop=(c == DMC - 1)))(jh, c, pb, XGt),
                                    r=[W1S, XGt], w=[PH[pb]], defer=(c < DMC - 1))
                            on("scalar", (lambda jh, pb, e: lambda E: E.activation(
                                H1[:, jh * TT:(jh + 1) * TT], PH[pb][:], AF.Relu,
                                bias=B1TA[:, 16 * e + jh:16 * e + jh + 1]))(jh, pb, e),
                                r=[PH[pb], B1TA], w=[H1])
                        nxt = e * NT + ti + 2
                        if nxt < NE * NT:
                            load_xg((nxt % NT) if NT > 1 else 0, XGb[(ti + 2) % 2] if NT > 1 else XGb[0])
                        if ti == NT - 1 and e + 1 < NE:
                            load_w1(e + 1)
                        for jo in range(DMC):
                            pb = jo % 2
                            for k in range(DIC):
                                on("tensor", (lambda jo, k, pb: lambda E: E.matmul(
                                    PO[pb][:],
                                    W2S[:, k * DM + jo * 128:k * DM + (jo + 1) * 128],
                                    H1[:, k * TT:(k + 1) * TT],
                                    start=(k == 0), stop=(k == DIC - 1)))(jo, k, pb),
                                    r=[W2S, H1], w=[PO[pb]], defer=(k < DIC - 1))
                            on("vector", (lambda jo, pb, e: lambda E: E.tensor_scalar(
                                TE[:], PO[pb][:], B2TA[:, 8 * e + jo:8 * e + jo + 1],
                                None, ALU.add))(jo, pb, e),
                                r=[PO[pb], B2TA], w=[TE])
                            on("vector", (lambda ti: lambda E: E.tensor_mul(
                                TE[:], TE[:], CWB[:, ti * TT:(ti + 1) * TT]))(ti),
                                r=[TE, CWB], w=[TE])
                            on("vector", (lambda jo, ti: lambda E: E.tensor_add(
                                ACC[:, jo * tok + ti * TT:jo * tok + (ti + 1) * TT],
                                ACC[:, jo * tok + ti * TT:jo * tok + (ti + 1) * TT],
                                TE[:]))(jo, ti), r=[ACC, TE], w=[ACC])
                    if e + 1 < NE:
                        load_w2(e + 1)
    
            if upto < 5:
                raise _Stop()
            with contextlib.ExitStack() as PE_:
                RE = lambda *a: _alloc(PE_, nc.sbuf_tensor(*a, side="right"))
                HD = RE("HD", [128, DMC * TT], BF16)
                ACB = RE("ACB", [128, DMC * TT], BF16)
                SQD = RE("SQD", [128, DMC * TT], BF16)
                MT4 = RE("MT4", [128, TT], F32)
                VT4 = RE("VT4", [128, TT], F32)
                T14 = RE("T14", [128, TT], F32)
                RT4 = RE("RT4", [128, TT], F32)
                OSB = RE("OSB", [128, DMC * TT], F32)
                PS4 = _alloc(PE_, nc.psum_tensor("PS4", [128, TT], F32))
                PQ4 = _alloc(PE_, nc.psum_tensor("PQ4", [128, TT], F32))

                for ti in range(NT):
                    for hf in range(2):
                        dma("sync",
                            HD[:].rearrange("p (c t) -> p c t", c=DMC)
                            [:, :, hf * 256:(hf + 1) * 256],
                            h_hbm[2 * ti + hf].rearrange("c p t -> p c t"),
                            r=[h_hbm], w=[HD])
                    for c in range(DMC):
                        on("scalar", (lambda c, ti: lambda E: E.activation(
                            ACB[:, c * TT:(c + 1) * TT],
                            ACC[:, c * tok + ti * TT:c * tok + (ti + 1) * TT],
                            AF.Copy))(c, ti),
                            r=[ACC], w=[ACB])
                    for c in range(DMC):
                        on("scalar", (lambda c: lambda E: E.activation(
                            SQD[:, c * TT:(c + 1) * TT], ACB[:, c * TT:(c + 1) * TT],
                            AF.Square))(c), r=[ACB], w=[SQD])
                    for c in range(DMC):
                        on("tensor", (lambda c: lambda E: E.matmul(
                            PS4[:], ONESB[:], ACB[:, c * TT:(c + 1) * TT],
                            start=(c == 0), stop=(c == DMC - 1)))(c),
                            r=[ONESB, ACB], w=[PS4], defer=(c < DMC - 1))
                    for c in range(DMC):
                        on("tensor", (lambda c: lambda E: E.matmul(
                            PQ4[:], ONESB[:], SQD[:, c * TT:(c + 1) * TT],
                            start=(c == 0), stop=(c == DMC - 1)))(c),
                            r=[ONESB, SQD], w=[PQ4], defer=(c < DMC - 1))
                    on("vector", lambda E: E.tensor_scalar(
                        MT4[:], PS4[:], 1.0 / DM, None, ALU.mult), r=[PS4], w=[MT4])
                    on("vector", lambda E: E.tensor_scalar(
                        VT4[:], PQ4[:], 1.0 / DM, None, ALU.mult), r=[PQ4], w=[VT4])
                    on("vector", lambda E: E.tensor_mul(T14[:], MT4[:], MT4[:]),
                       r=[MT4], w=[T14])
                    on("vector", lambda E: E.tensor_sub(VT4[:], VT4[:], T14[:]),
                       r=[VT4, T14], w=[VT4])
                    on("vector", lambda E: E.tensor_scalar(
                        VT4[:], VT4[:], LN_EPS, None, ALU.add), r=[VT4], w=[VT4])
                    on("vector", lambda E: E.reciprocal(T14[:], VT4[:]), r=[VT4], w=[T14])
                    on("scalar", lambda E: E.activation(RT4[:], T14[:], AF.Sqrt),
                       r=[T14], w=[RT4])
                    for c in range(DMC):
                        on("vector", (lambda c, ti: lambda E: E.tensor_sub(
                            VT4[:], ACC[:, c * tok + ti * TT:c * tok + (ti + 1) * TT],
                            MT4[:]))(c, ti), r=[ACC, MT4], w=[VT4])
                        on("vector", lambda E: E.tensor_mul(VT4[:], VT4[:], RT4[:]),
                           r=[VT4, RT4], w=[VT4])
                        on("vector", (lambda c: lambda E: E.tensor_add(
                            OSB[:, c * TT:(c + 1) * TT], VT4[:],
                            HD[:, c * TT:(c + 1) * TT]))(c),
                            r=[VT4, HD], w=[OSB])
                    dma("sync", outT[:, ti * TT:(ti + 1) * TT].rearrange(
                        "(c p) t -> p c t", p=128), cview(OSB, DMC),
                        r=[OSB], w=[outT])

        except _Stop:
            pass
        on("sync", lambda E: E.nop(),
           waits=[("sy", now("sy")), ("gq", now("gq"))], defer=True)
        on("gpsimd", lambda E: E.nop(),
           waits=[("gq", now("gq")), ("sy", now("sy"))], defer=True)

        with nc.Block() as block:
            block.tensor(lambda E: [f(E) for f in streams["tensor"]])
            block.vector(lambda E: [f(E) for f in streams["vector"]])
            block.scalar(lambda E: [f(E) for f in streams["scalar"]])
            block.sync(lambda E: [f(E) for f in streams["sync"]])
            block.gpsimd(lambda E: [f(E) for f in streams["gpsimd"]])

    return nc


def make_aux():
    masks = []
    for c in range(NCORES):
        m = np.zeros((64, 8), np.float32)
        for k in range(4 * (c // 4), c):
            m[:, k] = 1.0
        masks.append(m)
    sel = np.zeros((8, 1024), np.float32)
    for e in range(8):
        sel[e, 128 * e:128 * (e + 1)] = 1.0
    return masks, sel.astype(BFNP), np.eye(128, dtype=np.float32)


def make_in_maps(inputs, tok=2048):
    x = np.asarray(inputs["L0_attn_x"], np.float32)
    B, S, _ = x.shape
    xf = x.reshape(B * S, DM)
    masks, sel, idm = make_aux()
    shared = {
        "W_in": np.asarray(inputs["L0_attn_W_in"], np.float32),
        "b_in": np.asarray(inputs["L0_attn_b_in"], np.float32).reshape(32, 128),
        "W_xp": np.asarray(inputs["L0_attn_W_xp"], np.float32),
        "W_dt": np.asarray(inputs["L0_attn_W_dt"], np.float32),
        "b_dt": np.asarray(inputs["L0_attn_b_dt"], np.float32).reshape(8, 128),
        "W_out": np.asarray(inputs["L0_attn_W_out"], np.float32),
        "b_out": np.asarray(inputs["L0_attn_b_out"], np.float32).reshape(8, 128),
        "bxa": np.stack([np.asarray(inputs["L0_attn_b_xp"], np.float32),
                         np.asarray(inputs["L0_attn_A_log"], np.float32)]),
        "Dv": np.asarray(inputs["L0_attn_D"], np.float32).reshape(8, 128),
        "gate_W": np.asarray(inputs["L0_ffn_gate_W"], np.float32),
        "W1": np.asarray(inputs["L0_ffn_W1"], np.float32),
        "b1": np.asarray(inputs["L0_ffn_b1"], np.float32).reshape(NE, 16, 128),
        "W2": np.asarray(inputs["L0_ffn_W2"], np.float32),
        "b2": np.asarray(inputs["L0_ffn_b2"], np.float32).reshape(NE, 8, 128),
        "idm": idm, "sel8": sel,
    }
    in_maps = []
    for c in range(NCORES):
        sl = xf[c * tok:(c + 1) * tok]
        in_maps.append({**shared,
                        "xT": np.ascontiguousarray(sl.T),
                        "maskI": masks[c]})
    return in_maps


def kernel(**inputs):
    tok = 2048
    nc = build_nc(tok)
    in_maps = make_in_maps(inputs, tok)
    res = run_bass_kernel_spmd(nc, in_maps, core_ids=list(range(NCORES)))
    x = np.asarray(inputs["L0_attn_x"])
    B, S, _ = x.shape
    parts = [np.asarray(res.results[c]["outT"]).T for c in range(NCORES)]
    out = np.concatenate(parts, axis=0).reshape(B, S, DM).astype(np.float32)
    return out


# version 05
# speedup vs baseline: 1.2044x; 1.0119x over previous
import contextlib
import numpy as np
import ml_dtypes
import concourse.bass as bass
import concourse.mybir as mybir
from concourse.bass_utils import run_bass_kernel_spmd
from concourse.replica_groups import maybe_share_collective_output_space

F32 = mybir.dt.float32
BF16 = mybir.dt.bfloat16
AF = mybir.ActivationFunctionType
ALU = mybir.AluOpType
AX = mybir.AxisListType
BFNP = ml_dtypes.bfloat16

NCORES = 8
DM = 1024
DI = 2048
DS = 64
DH = 2048
NE = 8
DMC = DM // 128
DIC = DI // 128
LN_EPS = 1e-5


def _nm(t):
    n = getattr(t, "name", None)
    if isinstance(n, str):
        return n
    try:
        return t.name()
    except Exception:
        return str(t)


class _Stop(Exception):
    pass


def build_nc(tok=2048, upto=99):
    TT = 512
    NT = tok // TT
    NG = tok // 128

    nc = bass.Bass("TRN2", target_bir_lowering=False)

    xT = nc.dram_tensor("xT", [DM, tok], F32, kind="ExternalInput")
    W_in = nc.dram_tensor("W_in", [DM, 2 * DI], F32, kind="ExternalInput")
    b_in = nc.dram_tensor("b_in", [32, 128], F32, kind="ExternalInput")
    W_xp = nc.dram_tensor("W_xp", [DI, DS], F32, kind="ExternalInput")
    W_dt = nc.dram_tensor("W_dt", [DI, DM], F32, kind="ExternalInput")
    b_dt = nc.dram_tensor("b_dt", [8, 128], F32, kind="ExternalInput")
    W_out = nc.dram_tensor("W_out", [DI, DM], F32, kind="ExternalInput")
    b_out = nc.dram_tensor("b_out", [8, 128], F32, kind="ExternalInput")
    bxa = nc.dram_tensor("bxa", [2, 64], F32, kind="ExternalInput")
    Dv = nc.dram_tensor("Dv", [8, 128], F32, kind="ExternalInput")
    gate_W = nc.dram_tensor("gate_W", [DM, NE], F32, kind="ExternalInput")
    W1 = nc.dram_tensor("W1", [NE, DM, DH], F32, kind="ExternalInput")
    b1 = nc.dram_tensor("b1", [NE, 16, 128], F32, kind="ExternalInput")
    W2 = nc.dram_tensor("W2", [NE, DH, DM], F32, kind="ExternalInput")
    b2 = nc.dram_tensor("b2", [NE, 8, 128], F32, kind="ExternalInput")
    maskI = nc.dram_tensor("maskI", [64, 8], F32, kind="ExternalInput")
    idm = nc.dram_tensor("idm", [128, 128], F32, kind="ExternalInput")
    sel8 = nc.dram_tensor("sel8", [8, 1024], BF16, kind="ExternalInput")
    outT = nc.dram_tensor("outT", [DM, tok], F32, kind="ExternalOutput")

    xi_hbm = nc.dram_tensor("xi_hbm", [NT, 128, DIC * TT], BF16, kind="Internal")
    sr_hbm = nc.dram_tensor("sr_hbm", [NT, 128, DIC * TT], BF16, kind="Internal")
    xn_hbm = nc.dram_tensor("xn_hbm", [NT, 128, DMC * TT], BF16, kind="Internal")
    NTB = tok // 256
    h_hbm = nc.dram_tensor("h_hbm", [NTB, DMC, 128, 256], BF16, kind="Internal")
    ssum_hbm = nc.dram_tensor("ssum_hbm", [128, tok], F32, kind="Internal")
    cw_hbm = nc.dram_tensor("cw_hbm", [8, tok], BF16, kind="Internal")
    xg_hbm = nc.dram_tensor("xg_hbm", [DMC, 128, tok], BF16, kind="Internal")
    rg = [list(range(NCORES))]
    space = maybe_share_collective_output_space("AllGather", rg)
    ag_in = nc.dram_tensor("ag_in", [2, 64], F32, kind="Internal")
    ag_out = nc.dram_tensor("ag_out", [16, 64], F32, kind="Internal",
                            addr_space=space)

    ENGS = ("tensor", "vector", "scalar", "sync", "gpsimd")
    ESEM = {"tensor": "t", "vector": "v", "scalar": "s"}
    streams = {e: [] for e in ENGS}
    cnt = {}
    wm = {e: {} for e in ENGS}
    SEM = {}
    last_w = {}
    readers = {}
    pending = {e: [] for e in ENGS}
    regions = []
    init_waits = {}

    def _alloc(stack, guard):
        h = stack.enter_context(guard)
        try:
            ml = nc.lookup_mloc(h)
            sp = str(h.space)
            sz = ml.size() if callable(ml.size) else ml.size
            lo, hi = ml.addr, ml.addr + sz
        except Exception:
            return h
        req = {}
        for sp2, lo2, hi2, nm2 in regions:
            if sp2 == sp and lo < hi2 and lo2 < hi:
                for ev in ([last_w[nm2]] if nm2 in last_w else []):
                    req[ev[0]] = max(req.get(ev[0], 0), ev[1])
                for s2, v2 in readers.get(nm2, {}).items():
                    req[s2] = max(req.get(s2, 0), v2)
        regions.append((sp, lo, hi, h.name))
        if req:
            init_waits[h.name] = req
        return h

    def _reg_read(name, ev):
        d = readers.setdefault(name, {})
        d[ev[0]] = max(d.get(ev[0], 0), ev[1])

    def _reg_write(name, ev):
        last_w[name] = ev
        readers[name] = {}

    def on(eng, f, r=(), w=(), waits=(), sem=None, amt=1, defer=False):
        rn = [_nm(t) for t in r]
        wn = [_nm(t) for t in w]
        auto = {}
        for n in rn:
            if n in last_w:
                s_, v_ = last_w[n]
                auto[s_] = max(auto.get(s_, 0), v_)
        for n in wn:
            if n in last_w:
                s_, v_ = last_w[n]
                auto[s_] = max(auto.get(s_, 0), v_)
            for s_, v_ in readers.get(n, {}).items():
                auto[s_] = max(auto.get(s_, 0), v_)
        for s_, v_ in waits:
            if v_ > 0:
                auto[s_] = max(auto.get(s_, 0), v_)
        for n in rn + wn:
            iw = init_waits.pop(n, None)
            if iw:
                for s_, v_ in iw.items():
                    auto[s_] = max(auto.get(s_, 0), v_)
        if sem is None and not defer:
            sem = ESEM.get(eng)
            if sem is None:
                raise RuntimeError(f"need explicit sem for {eng}")
        wl = sorted(auto.items())

        def op(E):
            for s_, v_ in wl:
                if wm[eng].get(s_, 0) < v_:
                    E.wait_ge(SEM[s_], v_)
                    wm[eng][s_] = v_
            ins = f(E)
            if not defer:
                ins.then_inc(SEM[sem], amt)
        streams[eng].append(op)
        if defer:
            pending[eng].extend([("r", n) for n in rn] + [("w", n) for n in wn])
            return None
        cnt[sem] = cnt.get(sem, 0) + amt
        ev = (sem, cnt[sem])
        for kind, n in pending[eng]:
            (_reg_read if kind == "r" else _reg_write)(n, ev)
        pending[eng] = []
        for n in rn:
            _reg_read(n, ev)
        for n in wn:
            _reg_write(n, ev)
        return ev

    def now(s):
        return cnt.get(s, 0)

    qidx = {"sy": 0, "gq": 0}

    def dma(eng, out_ap, in_ap, r, w, waits=()):
        q = "sy" if eng == "sync" else "gq"
        s = f"{q}{qidx[q] % 4}"
        qidx[q] += 1
        chain = [(s, cnt.get(s, 0))]
        return on(eng, lambda E: E.dma_start(out=out_ap, in_=in_ap),
                  r=r, w=w, waits=list(waits) + chain, sem=s, amt=16)

    def cview(sb, nch):
        return sb[:].rearrange("p (c j) -> p c j", c=nch)

    def wload(eng, dst_sb, src2d, nch, parts, r, w, alt=False):
        s3 = src2d.rearrange("(c p) j -> p c j", p=128)
        d3 = cview(dst_sb, nch)
        step = nch // parts
        for i in range(parts):
            e = ("sync" if i % 2 else "gpsimd") if alt else eng
            if alt and e == "sync":
                e = "gpsimd"
            dma(e, d3[:, i * step:(i + 1) * step, :],
                s3[:, i * step:(i + 1) * step, :], r=r, w=w)

    def barrier():
        on("vector", lambda E: E.memzero(FLUSH[0:1, 0:4]), w=[FLUSH])
        on("scalar", lambda E: E.memzero(FLUSH[0:1, 4:8]), w=[FLUSH])
        snap = {s: now(s) for s in SEM}
        own = {"tensor": "t", "vector": "v", "scalar": "s"}
        for eng in ENGS:
            wl = [(s, v) for s, v in snap.items() if v > 0 and own.get(eng) != s]
            on(eng, lambda E: E.nop(), waits=wl, defer=True)

    with contextlib.ExitStack() as STK:
        for s in ("t", "v", "s", "cc", "sy0", "sy1", "sy2", "sy3",
                  "gq0", "gq1", "gq2", "gq3"):
            SEM[s] = STK.enter_context(nc.semaphore("sem_" + s))
            cnt[s] = 0

        L = lambda *a: _alloc(STK, nc.sbuf_tensor(*a, side="left"))
        FLUSH = L("FLUSH", [1, 8], F32)
        ID = L("ID", [128, 128], F32)
        ONESB = L("ONESB", [128, 128], BF16)
        MASK = L("MASK", [64, 8], F32)
        BINT = L("BINT", [128, 32], F32)
        BDTT = L("BDTT", [128, 8], F32)
        BOUT = L("BOUT", [128, 8], F32)
        DT = L("DT", [128, 8], F32)
        BXAT = L("BXAT", [64, 2], F32)
        ASC = L("ASC", [64, 1], F32)
        GWF = L("GWF", [128, 8 * 8], F32)
        BSTG1 = L("BSTG1", [32, 128], F32)
        BSTG2 = L("BSTG2", [8, 128], F32)
        BSTG3 = L("BSTG3", [8, 128], F32)
        BSTG4 = L("BSTG4", [8, 128], F32)
        BSTG5 = L("BSTG5", [2, 64], F32)

        dma("gpsimd", ID[:], idm[:], r=[idm], w=[ID])
        dma("gpsimd", MASK[:], maskI[:], r=[maskI], w=[MASK])
        dma("gpsimd", BSTG1[:], b_in[:], r=[b_in], w=[BSTG1])
        dma("gpsimd", BSTG2[:], b_dt[:], r=[b_dt], w=[BSTG2])
        dma("gpsimd", BSTG3[:], b_out[:], r=[b_out], w=[BSTG3])
        dma("gpsimd", BSTG4[:], Dv[:], r=[Dv], w=[BSTG4])
        dma("gpsimd", BSTG5[:], bxa[:], r=[bxa], w=[BSTG5])
        dma("sync", cview(GWF, DMC),
            gate_W.ap().rearrange("(c p) e -> p c e", p=128), r=[gate_W], w=[GWF])
        on("vector", lambda E: E.memset(ONESB[:], 1.0), w=[ONESB])

        with contextlib.ExitStack() as P0:
            PT = _alloc(P0, nc.psum_tensor("PT0", [128, 32], F32))
            for (stg, dst) in ((BSTG1, BINT), (BSTG2, BDTT), (BSTG3, BOUT),
                               (BSTG4, DT), (BSTG5, BXAT)):
                p, q = stg.shape[0], stg.shape[1]
                on("tensor",
                   (lambda stg, p, q: lambda E: E.transpose(
                       PT[0:q, 0:p], stg[:], ID[0:p, 0:p]))(stg, p, q),
                   r=[stg, ID], w=[PT])
                on("vector",
                   (lambda dst, p, q: lambda E: E.tensor_copy(
                       dst[0:q, 0:p], PT[0:q, 0:p]))(dst, p, q),
                   r=[PT], w=[dst])
            on("scalar", lambda E: E.activation(ASC[:], BXAT[:, 1:2], AF.Exp),
               r=[BXAT], w=[ASC])
            on("vector", lambda E: E.tensor_scalar(
                ASC[:], ASC[:], -1.0 / DM, None, ALU.mult), r=[ASC], w=[ASC])

        try:
            if upto < 1:
                raise _Stop()
            with contextlib.ExitStack() as PA:
                RA = lambda *a: _alloc(PA, nc.sbuf_tensor(*a, side="right"))
                WIN = RA("WIN", [128, DMC * 4096], BF16)
                XTf = [RA("XTf0", [128, DMC * TT], F32), RA("XTf1", [128, DMC * TT], F32)]
                XB = RA("XB", [128, DMC * TT], BF16)
                SQ = RA("SQ", [128, DMC * TT], BF16)
                XN = RA("XN", [128, DMC * TT], BF16)
                XIT = RA("XIT", [128, DIC * TT], BF16)
                SRT = RA("SRT", [128, DIC * TT], BF16)
                MT = RA("MT", [128, TT], F32)
                VT = RA("VT", [128, TT], F32)
                T1 = RA("T1", [128, TT], F32)
                RT = RA("RT", [128, TT], F32)
                PS_S = _alloc(PA, nc.psum_tensor("PS_S", [128, TT], F32))
                PS_Q = _alloc(PA, nc.psum_tensor("PS_Q", [128, TT], F32))
                PX = [_alloc(PA, nc.psum_tensor("PXa", [128, TT], F32)),
                      _alloc(PA, nc.psum_tensor("PXb", [128, TT], F32))]

                wload("gpsimd", WIN, W_in.ap(), DMC, 2, r=[W_in], w=[WIN])

                XN2 = [XN, RA("XNq", [128, DMC * TT], BF16)]

                def a1_stage1(ti):
                    XTt = XTf[ti % 2]
                    XNt = XN2[ti % 2]
                    dma("sync", cview(XTt, DMC),
                        xT[:, ti * TT:(ti + 1) * TT].rearrange("(c p) t -> p c t", p=128),
                        r=[xT], w=[XTt])
                    for c in range(DMC):
                        on("vector", (lambda c, XTt: lambda E: E.tensor_copy(
                            XB[:, c * TT:(c + 1) * TT], XTt[:, c * TT:(c + 1) * TT]))(c, XTt),
                            r=[XTt], w=[XB])
                    for c in range(DMC):
                        on("scalar", (lambda c: lambda E: E.activation(
                            SQ[:, c * TT:(c + 1) * TT], XB[:, c * TT:(c + 1) * TT],
                            AF.Square))(c), r=[XB], w=[SQ])
                    for c in range(DMC):
                        on("tensor", (lambda c: lambda E: E.matmul(
                            PS_S[:], ONESB[:], XB[:, c * TT:(c + 1) * TT],
                            start=(c == 0), stop=(c == DMC - 1)))(c),
                            r=[ONESB, XB], w=[PS_S], defer=(c < DMC - 1))
                    for c in range(DMC):
                        on("tensor", (lambda c: lambda E: E.matmul(
                            PS_Q[:], ONESB[:], SQ[:, c * TT:(c + 1) * TT],
                            start=(c == 0), stop=(c == DMC - 1)))(c),
                            r=[ONESB, SQ], w=[PS_Q], defer=(c < DMC - 1))
                    on("vector", lambda E: E.tensor_scalar(
                        MT[:], PS_S[:], 1.0 / DM, None, ALU.mult), r=[PS_S], w=[MT])
                    on("vector", lambda E: E.tensor_scalar(
                        VT[:], PS_Q[:], 1.0 / DM, None, ALU.mult), r=[PS_Q], w=[VT])
                    on("vector", lambda E: E.tensor_mul(T1[:], MT[:], MT[:]),
                       r=[MT], w=[T1])
                    on("vector", lambda E: E.tensor_sub(VT[:], VT[:], T1[:]),
                       r=[VT, T1], w=[VT])
                    on("vector", lambda E: E.tensor_scalar(
                        VT[:], VT[:], LN_EPS, None, ALU.add), r=[VT], w=[VT])
                    on("vector", lambda E: E.reciprocal(T1[:], VT[:]), r=[VT], w=[T1])
                    on("scalar", lambda E: E.activation(RT[:], T1[:], AF.Sqrt),
                       r=[T1], w=[RT])
                    for c in range(DMC):
                        on("vector", (lambda c, XTt: lambda E: E.tensor_sub(
                            VT[:], XTt[:, c * TT:(c + 1) * TT], MT[:]))(c, XTt),
                            r=[XTt, MT], w=[VT])
                        on("vector", (lambda c, XNt: lambda E: E.tensor_mul(
                            XNt[:, c * TT:(c + 1) * TT], VT[:], RT[:]))(c, XNt),
                            r=[VT, RT], w=[XNt])

                def a1_stage2(ti):
                    XNt = XN2[ti % 2]
                    for j in range(32):
                        pb = j % 2
                        for c in range(DMC):
                            on("tensor", (lambda j, c, pb, XNt: lambda E: E.matmul(
                                PX[pb][:],
                                WIN[:, c * 4096 + j * 128:c * 4096 + (j + 1) * 128],
                                XNt[:, c * TT:(c + 1) * TT],
                                start=(c == 0), stop=(c == DMC - 1)))(j, c, pb, XNt),
                                r=[WIN, XNt], w=[PX[pb]], defer=(c < DMC - 1))
                        dst = XIT if j < 16 else SRT
                        jj = j if j < 16 else j - 16
                        on("scalar", (lambda j, jj, dst, pb: lambda E: E.activation(
                            dst[:, jj * TT:(jj + 1) * TT], PX[pb][:], AF.Silu,
                            bias=BINT[:, j:j + 1]))(j, jj, dst, pb),
                            r=[PX[pb], BINT], w=[dst])
                        if j >= 16:
                            on("vector", (lambda jj: lambda E: E.tensor_mul(
                                SRT[:, jj * TT:(jj + 1) * TT],
                                SRT[:, jj * TT:(jj + 1) * TT],
                                XIT[:, jj * TT:(jj + 1) * TT]))(jj),
                                r=[SRT, XIT], w=[SRT])
                    dma("sync", xi_hbm[ti], XIT[:], r=[XIT], w=[xi_hbm])
                    dma("gpsimd", sr_hbm[ti], SRT[:], r=[SRT], w=[sr_hbm])
                    dma("gpsimd", xn_hbm[ti], XNt[:], r=[XNt], w=[xn_hbm])

                a1_stage1(0)
                for ti in range(NT):
                    if ti + 1 < NT:
                        a1_stage1(ti + 1)
                    a1_stage2(ti)

            if upto < 2:
                raise _Stop()
            with contextlib.ExitStack() as PB:
                RB = lambda *a: _alloc(PB, nc.sbuf_tensor(*a, side="right"))
                WDT = RB("WDT", [128, DIC * 1024], BF16)
                WXP = RB("WXP", [128, DIC * 64], BF16)
                BX = RB("BX", [64, tok], F32)
                AFULL = RB("AFULL", [64, tok], F32)
                XIJ = [RB("XIJ0", [128, DIC * TT], BF16), RB("XIJ1", [128, DIC * TT], BF16)]
                SPB = RB("SPB", [128, DMC * TT], BF16)
                TA = RB("TA", [128, TT], F32)
                TB = RB("TB", [128, TT], F32)
                TC = RB("TC", [128, TT], F32)
                TD = RB("TD", [128, TT], F32)
                PD = [_alloc(PB, nc.psum_tensor("PDa", [128, TT], F32)),
                      _alloc(PB, nc.psum_tensor("PDb", [128, TT], F32))]
                PMD = _alloc(PB, nc.psum_tensor("PMD", [128, TT], F32))
                PBX = _alloc(PB, nc.psum_tensor("PBX", [64, TT], F32))
                PSS = _alloc(PB, nc.psum_tensor("PSS", [128, TT], F32))
                PSC = _alloc(PB, nc.psum_tensor("PSC", [128, 128], F32))

                A2L = (upto - 210) if 210 <= upto < 220 else 9
                wload("gpsimd", WDT, W_dt.ap(), DIC, 4, r=[W_dt], w=[WDT])
                wload("gpsimd", WXP, W_xp.ap(), DIC, 1, r=[W_xp], w=[WXP])

                for ti in range(NT):
                    XJ = XIJ[ti % 2]
                    dma("sync", XJ[:], xi_hbm[ti], r=[xi_hbm], w=[XJ])
                    for j8 in range(DMC if A2L >= 2 else 0):
                        pb = j8 % 2
                        for k in range(DIC):
                            on("tensor", (lambda j8, k, pb, XJ: lambda E: E.matmul(
                                PD[pb][:],
                                WDT[:, k * 1024 + j8 * 128:k * 1024 + (j8 + 1) * 128],
                                XJ[:, k * TT:(k + 1) * TT],
                                start=(k == 0), stop=(k == DIC - 1)))(j8, k, pb, XJ),
                                r=[WDT, XJ], w=[PD[pb]], defer=(k < DIC - 1))
                        on("scalar", (lambda j8, pb: lambda E: E.activation(
                            TA[:], PD[pb][:], AF.Abs, bias=BDTT[:, j8:j8 + 1]))(j8, pb),
                            r=[PD[pb], BDTT], w=[TA])
                        on("scalar", lambda E: E.activation(
                            TB[:], TA[:], AF.Exp, scale=-1.0), r=[TA], w=[TB])
                        on("scalar", lambda E: E.activation(
                            TC[:], TB[:], AF.Ln, bias=1.0), r=[TB], w=[TC])
                        on("scalar", (lambda j8, pb: lambda E: E.activation(
                            TD[:], PD[pb][:], AF.Relu,
                            bias=BDTT[:, j8:j8 + 1]))(j8, pb),
                            r=[PD[pb], BDTT], w=[TD])
                        on("vector", (lambda j8: lambda E: E.tensor_add(
                            SPB[:, j8 * TT:(j8 + 1) * TT], TD[:], TC[:]))(j8),
                            r=[TD, TC], w=[SPB])
                        on("tensor", (lambda j8: lambda E: E.matmul(
                            PMD[:], ONESB[:], SPB[:, j8 * TT:(j8 + 1) * TT],
                            start=(j8 == 0), stop=(j8 == DMC - 1)))(j8),
                            r=[ONESB, SPB], w=[PMD], defer=(j8 < DMC - 1))
                    if A2L >= 3:
                        on("scalar", (lambda ti: lambda E: E.activation(
                            AFULL[:, ti * TT:(ti + 1) * TT], PMD[0:64, :], AF.Exp,
                            scale=ASC[:, 0:1]))(ti),
                            r=[PMD, ASC], w=[AFULL])
                    for k in range(DIC if A2L >= 4 else 0):
                        on("tensor", (lambda k, XJ: lambda E: E.matmul(
                            PBX[:], WXP[:, 64 * k:64 * (k + 1)],
                            XJ[:, k * TT:(k + 1) * TT],
                            start=(k == 0), stop=(k == DIC - 1)))(k, XJ),
                            r=[WXP, XJ], w=[PBX], defer=(k < DIC - 1))
                    if A2L >= 4:
                        on("vector", (lambda ti: lambda E: E.tensor_scalar(
                            BX[:, ti * TT:(ti + 1) * TT], PBX[:], BXAT[:, 0:1], None,
                            ALU.add))(ti),
                            r=[PBX, BXAT], w=[BX])

                if upto < 22:
                    raise _Stop()
                ZER = RB("ZER", [64, tok], F32)
                STATE = RB("STATE", [64, tok], F32)
                CP = RB("CP", [64, tok], F32)
                STCB = RB("STCB", [64, tok], BF16)
                SUM2 = RB("SUM2", [64, 2], F32)
                SUM2T = RB("SUM2T", [2, 64], F32)
                AGSB = RB("AGSB", [16, 64], F32)
                AGT = RB("AGT", [64, 16], F32)
                SIN = RB("SIN", [64, 1], F32)
                TF = RB("TF", [64, 1], F32)
                SSB = RB("SSB", [128, tok], F32)

                on("vector", lambda E: E.memset(ZER[:], 0.0), w=[ZER])
                on("vector", lambda E: E.tensor_tensor_scan(
                    STATE[:], AFULL[:], BX[:], 0.0, ALU.mult, ALU.add),
                    r=[AFULL, BX], w=[STATE])
                on("vector", lambda E: E.tensor_tensor_scan(
                    CP[:], AFULL[:], ZER[:], 1.0, ALU.mult, ALU.add),
                    r=[AFULL, ZER], w=[CP])
                on("vector", lambda E: E.tensor_copy(SUM2[:, 0:1], CP[:, tok - 1:tok]),
                   r=[CP], w=[SUM2])
                on("vector", lambda E: E.tensor_copy(SUM2[:, 1:2], STATE[:, tok - 1:tok]),
                   r=[STATE], w=[SUM2])
                on("tensor", lambda E: E.transpose(
                    PSC[0:2, 0:64], SUM2[:], ID[0:64, 0:64]), r=[SUM2, ID], w=[PSC])
                on("vector", lambda E: E.tensor_copy(SUM2T[:], PSC[0:2, 0:64]),
                   r=[PSC], w=[SUM2T])
                if upto < 23:
                    raise _Stop()
                dma("gpsimd", ag_in[:], SUM2T[:], r=[SUM2T], w=[ag_in])
                on("gpsimd", lambda E: E.collective_compute(
                    "AllGather", ALU.bypass, replica_groups=rg,
                    ins=[ag_in[:]], outs=[ag_out[:]]),
                    r=[ag_in], w=[ag_out], sem="cc", amt=1)
                dma("gpsimd", AGSB[:], ag_out[:], r=[ag_out], w=[AGSB])
                on("tensor", lambda E: E.transpose(
                    PSC[0:64, 0:16], AGSB[:], ID[0:16, 0:16]), r=[AGSB, ID], w=[PSC])
                on("vector", lambda E: E.tensor_copy(AGT[:], PSC[0:64, 0:16]),
                   r=[PSC], w=[AGT])
                if upto < 24:
                    raise _Stop()
                on("vector", lambda E: E.memset(SIN[:], 0.0), w=[SIN])
                for k in range(NCORES):
                    on("vector", (lambda k: lambda E: E.scalar_tensor_tensor(
                        TF[:], SIN[:], AGT[:, 2 * k:2 * k + 1],
                        AGT[:, 2 * k + 1:2 * k + 2], ALU.mult, ALU.add))(k),
                        r=[SIN, AGT], w=[TF])
                    on("vector", lambda E: E.tensor_sub(TF[:], TF[:], SIN[:]),
                       r=[TF, SIN], w=[TF])
                    on("vector", (lambda k: lambda E: E.tensor_mul(
                        TF[:], TF[:], MASK[:, k:k + 1]))(k), r=[TF, MASK], w=[TF])
                    on("vector", lambda E: E.tensor_add(SIN[:], SIN[:], TF[:]),
                       r=[SIN, TF], w=[SIN])
                on("vector", lambda E: E.scalar_tensor_tensor(
                    STATE[:], CP[:], SIN[:, 0:1], STATE[:], ALU.mult, ALU.add),
                    r=[CP, SIN, STATE], w=[STATE])
                on("vector", lambda E: E.tensor_copy(STCB[:], STATE[:]),
                   r=[STATE], w=[STCB])
                if upto < 25:
                    raise _Stop()
                for i in range(NT):
                    on("tensor", (lambda i: lambda E: E.matmul(
                        PSS[:], ONESB[0:64, :], STCB[:, i * TT:(i + 1) * TT],
                        start=True, stop=True))(i),
                        r=[ONESB, STCB], w=[PSS])
                    on("vector", (lambda i: lambda E: E.tensor_copy(
                        SSB[:, i * TT:(i + 1) * TT], PSS[:]))(i), r=[PSS], w=[SSB])
                dma("sync", ssum_hbm[:], SSB[:], r=[SSB], w=[ssum_hbm])
    
            if upto < 3:
                raise _Stop()
            ACC = L("ACC", [128, DMC * tok], F32)
            TTB = 256
            NTB2 = tok // TTB

            with contextlib.ExitStack() as PC:
                RC = lambda *a: _alloc(PC, nc.sbuf_tensor(*a, side="right"))
                WOUT = RC("WOUT", [128, DIC * 1024], BF16)
                SRJ = [RC("SRJ0", [128, DIC * TTB], BF16),
                       RC("SRJ1", [128, DIC * TTB], BF16)]
                XTb = [RC("XTb0", [128, DMC * TTB], BF16),
                       RC("XTb1", [128, DMC * TTB], BF16)]
                XNb = [RC("XNb0", [128, DMC * TTB], BF16),
                       RC("XNb1", [128, DMC * TTB], BF16)]
                MIX = [RC("MIX0", [128, DMC * TTB], BF16),
                       RC("MIX1", [128, DMC * TTB], BF16)]
                HTb = [RC("HTb0", [128, DMC * TTB], BF16),
                       RC("HTb1", [128, DMC * TTB], BF16)]
                SQB = [RC("SQB0", [128, DMC * TTB], BF16),
                       RC("SQB1", [128, DMC * TTB], BF16)]
                SSBb = [RC("SSBb0", [128, TTB], F32), RC("SSBb1", [128, TTB], F32)]
                LGF = RC("LGF", [8, tok], F32)
                MT2 = [RC("MT2a", [128, TTB], F32), RC("MT2b", [128, TTB], F32)]
                RT2 = [RC("RT2a", [128, TTB], F32), RC("RT2b", [128, TTB], F32)]
                M3T = [RC("M3Ta", [128, TTB], F32), RC("M3Tb", [128, TTB], F32)]
                R3T = [RC("R3Ta", [128, TTB], F32), RC("R3Tb", [128, TTB], F32)]
                VT2 = RC("VT2", [128, TTB], F32)
                T12 = RC("T12", [128, TTB], F32)
                PY = [_alloc(PC, nc.psum_tensor("PYa", [128, TTB], F32)),
                      _alloc(PC, nc.psum_tensor("PYb", [128, TTB], F32))]
                PS2 = _alloc(PC, nc.psum_tensor("PS2", [128, TTB], F32))
                PQ2 = _alloc(PC, nc.psum_tensor("PQ2", [128, TTB], F32))
                PLG = _alloc(PC, nc.psum_tensor("PLG", [8, TTB], F32))

                wload("gpsimd", WOUT, W_out.ap(), DIC, 4, r=[W_out], w=[WOUT])

                def lnstats(src_bf, sq, mdst, rdst):
                    for c in range(DMC):
                        on("scalar", (lambda c: lambda E: E.activation(
                            sq[:, c * TTB:(c + 1) * TTB],
                            src_bf[:, c * TTB:(c + 1) * TTB],
                            AF.Square))(c), r=[src_bf], w=[sq])
                    for c in range(DMC):
                        on("tensor", (lambda c: lambda E: E.matmul(
                            PS2[:], ONESB[:], src_bf[:, c * TTB:(c + 1) * TTB],
                            start=(c == 0), stop=(c == DMC - 1)))(c),
                            r=[ONESB, src_bf], w=[PS2], defer=(c < DMC - 1))
                    for c in range(DMC):
                        on("tensor", (lambda c: lambda E: E.matmul(
                            PQ2[:], ONESB[:], sq[:, c * TTB:(c + 1) * TTB],
                            start=(c == 0), stop=(c == DMC - 1)))(c),
                            r=[ONESB, sq], w=[PQ2], defer=(c < DMC - 1))
                    on("vector", (lambda mdst: lambda E: E.tensor_scalar(
                        mdst[:], PS2[:], 1.0 / DM, None, ALU.mult))(mdst),
                        r=[PS2], w=[mdst])
                    on("vector", lambda E: E.tensor_scalar(
                        VT2[:], PQ2[:], 1.0 / DM, None, ALU.mult), r=[PQ2], w=[VT2])
                    on("vector", (lambda mdst: lambda E: E.tensor_mul(
                        T12[:], mdst[:], mdst[:]))(mdst), r=[mdst], w=[T12])
                    on("vector", lambda E: E.tensor_sub(VT2[:], VT2[:], T12[:]),
                       r=[VT2, T12], w=[VT2])
                    on("vector", lambda E: E.tensor_scalar(
                        VT2[:], VT2[:], LN_EPS, None, ALU.add), r=[VT2], w=[VT2])
                    on("vector", lambda E: E.reciprocal(T12[:], VT2[:]),
                       r=[VT2], w=[T12])
                    on("scalar", (lambda rdst: lambda E: E.activation(
                        rdst[:], T12[:], AF.Sqrt))(rdst), r=[T12], w=[rdst])

                def b_stage1(tb):
                    p = tb % 2
                    ti, hf = tb // 2, tb % 2
                    sr, xb_, xn_, mx, ss = SRJ[p], XTb[p], XNb[p], MIX[p], SSBb[p]
                    dma("sync", sr[:].rearrange("p (k t) -> p k t", k=DIC),
                        sr_hbm[ti].rearrange("p (k t) -> p k t", k=DIC)
                        [:, :, hf * TTB:(hf + 1) * TTB],
                        r=[sr_hbm], w=[sr])
                    dma("sync", xn_[:].rearrange("p (c t) -> p c t", c=DMC),
                        xn_hbm[ti].rearrange("p (c t) -> p c t", c=DMC)
                        [:, :, hf * TTB:(hf + 1) * TTB],
                        r=[xn_hbm], w=[xn_])
                    dma("gpsimd", cview(xb_, DMC),
                        xT[:, tb * TTB:(tb + 1) * TTB].rearrange(
                            "(c p) t -> p c t", p=128),
                        r=[xT], w=[xb_])
                    dma("gpsimd", ss[:], ssum_hbm[:, tb * TTB:(tb + 1) * TTB],
                        r=[ssum_hbm], w=[ss])
                    for k in range(DIC):
                        on("vector", (lambda k, sr, ss: lambda E: E.tensor_mul(
                            sr[:, k * TTB:(k + 1) * TTB], ss[:],
                            sr[:, k * TTB:(k + 1) * TTB]))(k, sr, ss),
                            r=[ss, sr], w=[sr])
                    for j8 in range(DMC):
                        pb = j8 % 2
                        for k in range(DIC):
                            on("tensor", (lambda j8, k, pb, sr: lambda E: E.matmul(
                                PY[pb][:],
                                WOUT[:, k * 1024 + j8 * 128:k * 1024 + (j8 + 1) * 128],
                                sr[:, k * TTB:(k + 1) * TTB],
                                start=(k == 0), stop=(k == DIC - 1)))(j8, k, pb, sr),
                                r=[WOUT, sr], w=[PY[pb]], defer=(k < DIC - 1))
                        on("vector", (lambda j8, pb: lambda E: E.tensor_scalar(
                            T12[:], PY[pb][:], BOUT[:, j8:j8 + 1], None,
                            ALU.add))(j8, pb),
                            r=[PY[pb], BOUT], w=[T12])
                        on("vector", (lambda j8, mx, xn_: lambda E: E.scalar_tensor_tensor(
                            mx[:, j8 * TTB:(j8 + 1) * TTB],
                            xn_[:, j8 * TTB:(j8 + 1) * TTB],
                            DT[:, j8:j8 + 1], T12[:], ALU.mult, ALU.add))(j8, mx, xn_),
                            r=[xn_, DT, T12], w=[mx])

                def b_stage2(tb):
                    p = tb % 2
                    mx, ht, sq, xb_ = MIX[p], HTb[p], SQB[p], XTb[p]
                    m2, r2, m3, r3 = MT2[p], RT2[p], M3T[p], R3T[p]
                    lnstats(mx, sq, m2, r2)
                    for c in range(DMC):
                        on("vector", (lambda c, mx, m2: lambda E: E.tensor_sub(
                            VT2[:], mx[:, c * TTB:(c + 1) * TTB], m2[:]))(c, mx, m2),
                            r=[mx, m2], w=[VT2])
                        on("vector", (lambda r2: lambda E: E.tensor_mul(
                            VT2[:], VT2[:], r2[:]))(r2), r=[VT2, r2], w=[VT2])
                        on("vector", (lambda c, ht, xb_: lambda E: E.tensor_add(
                            ht[:, c * TTB:(c + 1) * TTB], VT2[:],
                            xb_[:, c * TTB:(c + 1) * TTB]))(c, ht, xb_),
                            r=[VT2, xb_], w=[ht])
                    dma("sync", h_hbm[tb].rearrange("c p t -> p c t"), cview(ht, DMC),
                        r=[ht], w=[h_hbm])
                    lnstats(ht, sq, m3, r3)
                    for c in range(DMC):
                        on("vector", (lambda c, ht, m3: lambda E: E.tensor_sub(
                            VT2[:], ht[:, c * TTB:(c + 1) * TTB], m3[:]))(c, ht, m3),
                            r=[ht, m3], w=[VT2])
                        on("vector", (lambda c, tb, r3: lambda E: E.tensor_mul(
                            ACC[:, c * tok + tb * TTB:c * tok + (tb + 1) * TTB],
                            VT2[:], r3[:]))(c, tb, r3),
                            r=[VT2, r3], w=[f"ACC.{c}.{tb}"])
                        on("scalar", (lambda c, tb, sq: lambda E: E.activation(
                            sq[:, c * TTB:(c + 1) * TTB],
                            ACC[:, c * tok + tb * TTB:c * tok + (tb + 1) * TTB],
                            AF.Copy))(c, tb, sq),
                            r=[f"ACC.{c}.{tb}"], w=[sq])
                    dma("sync",
                        xg_hbm[:, :, tb * TTB:(tb + 1) * TTB].rearrange(
                            "c p t -> p c t"),
                        cview(sq, DMC), r=[sq], w=[xg_hbm])
                    for c in range(DMC):
                        on("tensor", (lambda c, tb: lambda E: E.matmul(
                            PLG[:], GWF[:, 8 * c:8 * (c + 1)],
                            ACC[:, c * tok + tb * TTB:c * tok + (tb + 1) * TTB],
                            start=(c == 0), stop=(c == DMC - 1)))(c, tb),
                            r=[GWF, f"ACC.{c}.{tb}"], w=[PLG], defer=(c < DMC - 1))
                    on("vector", (lambda tb: lambda E: E.tensor_copy(
                        LGF[:, tb * TTB:(tb + 1) * TTB], PLG[:]))(tb),
                        r=[PLG], w=[LGF])

                b_stage1(0)
                for tb in range(NTB2):
                    if tb + 1 < NTB2:
                        b_stage1(tb + 1)
                    b_stage2(tb)

                CWFMB = RC("CWFMB", [8, tok], BF16)
                CWG = RC("CWG", [128, NG * 8], F32)
                EG = RC("EG", [128, NG * 8], F32)
                GG = RC("GG", [128, NG * 8], F32)
                K1 = RC("K1", [128, NG * 8], F32)
                GS = RC("GS", [128, NG * 8], F32)
                M1 = RC("M1", [128, NG], F32)
                M1N = RC("M1N", [128, NG], F32)
                S1 = RC("S1", [128, NG], F32)
                R1 = RC("R1", [128, NG], F32)
                M2 = RC("M2", [128, NG], F32)
                M3g = RC("M3g", [128, NG], F32)
                S2 = RC("S2", [128, NG], F32)
                R2 = RC("R2", [128, NG], F32)
                PGT = _alloc(PC, nc.psum_tensor("PGT", [128, NG * 8], F32))
                PGB = [_alloc(PC, nc.psum_tensor("PGB0", [8, 128], F32)),
                       _alloc(PC, nc.psum_tensor("PGB1", [8, 128], F32))]

                for g in range(NG):
                    on("tensor", (lambda g: lambda E: E.transpose(
                        PGT[:, 8 * g:8 * (g + 1)], LGF[:, 128 * g:128 * (g + 1)],
                        ID[0:8, 0:8]))(g),
                        r=[LGF, ID], w=[PGT], defer=(g < NG - 1))
                on("vector", lambda E: E.tensor_copy(CWG[:], PGT[:]), r=[PGT], w=[CWG])
                for g in range(NG):
                    on("vector", (lambda g: lambda E: E.reduce_max(
                        M1[:, g:g + 1], CWG[:, 8 * g:8 * (g + 1)], AX.X))(g),
                        r=[CWG], w=[M1])
                on("vector", lambda E: E.tensor_scalar(
                    M1N[:], M1[:], -1.0, None, ALU.mult), r=[M1], w=[M1N])
                for g in range(NG):
                    on("scalar", (lambda g: lambda E: E.activation(
                        EG[:, 8 * g:8 * (g + 1)], CWG[:, 8 * g:8 * (g + 1)], AF.Exp,
                        bias=M1N[:, g:g + 1]))(g), r=[CWG, M1N], w=[EG])
                for g in range(NG):
                    on("vector", (lambda g: lambda E: E.reduce_sum(
                        S1[:, g:g + 1], EG[:, 8 * g:8 * (g + 1)], AX.X))(g),
                        r=[EG], w=[S1])
                on("vector", lambda E: E.reciprocal(R1[:], S1[:]), r=[S1], w=[R1])
                for g in range(NG):
                    on("vector", (lambda g: lambda E: E.tensor_scalar(
                        GG[:, 8 * g:8 * (g + 1)], EG[:, 8 * g:8 * (g + 1)],
                        R1[:, g:g + 1], None, ALU.mult))(g), r=[EG, R1], w=[GG])
                for g in range(NG):
                    on("vector", (lambda g: lambda E: E.reduce_max(
                        M2[:, g:g + 1], GG[:, 8 * g:8 * (g + 1)], AX.X))(g),
                        r=[GG], w=[M2])
                for g in range(NG):
                    on("vector", (lambda g: lambda E: E.tensor_scalar(
                        K1[:, 8 * g:8 * (g + 1)], GG[:, 8 * g:8 * (g + 1)],
                        M2[:, g:g + 1], None, ALU.is_ge))(g), r=[GG, M2], w=[K1])
                on("vector", lambda E: E.scalar_tensor_tensor(
                    K1[:], K1[:], -1e9, GG[:], ALU.mult, ALU.add), r=[K1, GG], w=[K1])
                for g in range(NG):
                    on("vector", (lambda g: lambda E: E.reduce_max(
                        M3g[:, g:g + 1], K1[:, 8 * g:8 * (g + 1)], AX.X))(g),
                        r=[K1], w=[M3g])
                for g in range(NG):
                    on("vector", (lambda g: lambda E: E.tensor_scalar(
                        K1[:, 8 * g:8 * (g + 1)], GG[:, 8 * g:8 * (g + 1)],
                        M3g[:, g:g + 1], None, ALU.is_ge))(g), r=[GG, M3g], w=[K1])
                on("vector", lambda E: E.tensor_mul(GS[:], GG[:], K1[:]),
                   r=[GG, K1], w=[GS])
                for g in range(NG):
                    on("vector", (lambda g: lambda E: E.reduce_sum(
                        S2[:, g:g + 1], GS[:, 8 * g:8 * (g + 1)], AX.X))(g),
                        r=[GS], w=[S2])
                on("vector", lambda E: E.tensor_scalar(
                    S2[:], S2[:], 1e-8, None, ALU.add), r=[S2], w=[S2])
                on("vector", lambda E: E.reciprocal(R2[:], S2[:]), r=[S2], w=[R2])
                for g in range(NG):
                    on("vector", (lambda g: lambda E: E.tensor_scalar(
                        GS[:, 8 * g:8 * (g + 1)], GS[:, 8 * g:8 * (g + 1)],
                        R2[:, g:g + 1], None, ALU.mult))(g), r=[GS, R2], w=[GS])
                for g in range(NG):
                    gb = PGB[g % 2]
                    on("tensor", (lambda g, gb: lambda E: E.transpose(
                        gb[:], GS[:, 8 * g:8 * (g + 1)], ID[:]))(g, gb),
                        r=[GS, ID], w=[gb])
                    on("vector", (lambda g, gb: lambda E: E.tensor_copy(
                        CWFMB[:, 128 * g:128 * (g + 1)], gb[:]))(g, gb),
                        r=[gb], w=[CWFMB])
                dma("sync", cw_hbm[:], CWFMB[:], r=[CWFMB], w=[cw_hbm])
    
            if upto < 4:
                raise _Stop()
            with contextlib.ExitStack() as PD_:
                RD = lambda *a: _alloc(PD_, nc.sbuf_tensor(*a, side="right"))
                W1S = RD("W1S", [128, DMC * DH], BF16)
                W2S = RD("W2S", [128, DIC * DM], BF16)
                H1 = RD("H1", [128, DIC * TT], BF16)
                XGb = [RD("XGb0", [128, DMC * TT], BF16),
                       RD("XGb1", [128, DMC * TT], BF16)]
                CWB = RD("CWB", [128, tok], BF16)
                CWFM = RD("CWFM", [8, tok], BF16)
                SEL8 = RD("SEL8", [8, 1024], BF16)
                B1STG = RD("B1STG", [16, 128], F32)
                B2STG = RD("B2STG", [8, 128], F32)
                B1TA = RD("B1TA", [128, NE * 16], F32)
                B2TA = RD("B2TA", [128, NE * 8], F32)
                TE = RD("TE", [128, TT], F32)
                PH = [_alloc(PD_, nc.psum_tensor("PHa", [128, TT], F32)),
                      _alloc(PD_, nc.psum_tensor("PHb", [128, TT], F32))]
                PO = [_alloc(PD_, nc.psum_tensor("POa", [128, TT], F32)),
                      _alloc(PD_, nc.psum_tensor("POb", [128, TT], F32))]
                PCW = _alloc(PD_, nc.psum_tensor("PCW", [128, TT], F32))
                PBT = _alloc(PD_, nc.psum_tensor("PBT", [128, 16], F32))

                dma("gpsimd", SEL8[:], sel8[:], r=[sel8], w=[SEL8])
                dma("gpsimd", CWFM[:], cw_hbm[:], r=[cw_hbm], w=[CWFM])
                for e in range(NE):
                    dma("sync", B1STG[:], b1[e], r=[b1], w=[B1STG])
                    on("tensor", (lambda e: lambda E: E.transpose(
                        PBT[:, 0:16], B1STG[:], ID[0:16, 0:16]))(e),
                        r=[B1STG, ID], w=[PBT])
                    on("vector", (lambda e: lambda E: E.tensor_copy(
                        B1TA[:, 16 * e:16 * (e + 1)], PBT[:, 0:16]))(e),
                        r=[PBT], w=[B1TA])
                    dma("sync", B2STG[:], b2[e], r=[b2], w=[B2STG])
                    on("tensor", (lambda e: lambda E: E.transpose(
                        PBT[:, 0:8], B2STG[:], ID[0:8, 0:8]))(e),
                        r=[B2STG, ID], w=[PBT])
                    on("vector", (lambda e: lambda E: E.tensor_copy(
                        B2TA[:, 8 * e:8 * (e + 1)], PBT[:, 0:8]))(e),
                        r=[PBT], w=[B2TA])

                def load_w1(e):
                    wload("gpsimd", W1S, W1[e], DMC, 2, r=[W1], w=[W1S])

                def load_w2(e):
                    wload("gpsimd", W2S, W2[e], DIC, 4, r=[W2], w=[W2S])

                def load_xg(cti, buf):
                    dma("sync", cview(buf, DMC),
                        xg_hbm[:, :, cti * TT:(cti + 1) * TT].rearrange(
                            "c p t -> p c t"),
                        r=[xg_hbm], w=[buf])

                load_w1(0)
                load_w2(0)
                for i in range(min(2, NT)):
                    load_xg(i, XGb[i % 2])
                for e in range(NE):
                    for i in range(NT):
                        on("tensor", (lambda e, i: lambda E: E.matmul(
                            PCW[:], SEL8[:, 128 * e:128 * (e + 1)],
                            CWFM[:, i * TT:(i + 1) * TT], start=True, stop=True))(e, i),
                            r=[SEL8, CWFM], w=[PCW])
                        on("vector", (lambda i: lambda E: E.tensor_copy(
                            CWB[:, i * TT:(i + 1) * TT], PCW[:]))(i), r=[PCW], w=[CWB])
                    for ti in range(NT):
                        XGt = XGb[ti % 2]
                        for jh in range(DIC):
                            pb = jh % 2
                            for c in range(DMC):
                                on("tensor", (lambda jh, c, pb, XGt: lambda E: E.matmul(
                                    PH[pb][:],
                                    W1S[:, c * DH + jh * 128:c * DH + (jh + 1) * 128],
                                    XGt[:, c * TT:(c + 1) * TT],
                                    start=(c == 0), stop=(c == DMC - 1)))(jh, c, pb, XGt),
                                    r=[W1S, XGt], w=[PH[pb]], defer=(c < DMC - 1))
                            on("scalar", (lambda jh, pb, e: lambda E: E.activation(
                                H1[:, jh * TT:(jh + 1) * TT], PH[pb][:], AF.Relu,
                                bias=B1TA[:, 16 * e + jh:16 * e + jh + 1]))(jh, pb, e),
                                r=[PH[pb], B1TA], w=[H1])
                        nxt = e * NT + ti + 2
                        if nxt < NE * NT:
                            load_xg((nxt % NT) if NT > 1 else 0, XGb[(ti + 2) % 2] if NT > 1 else XGb[0])
                        if ti == NT - 1 and e + 1 < NE:
                            load_w1(e + 1)
                        for jo in range(DMC):
                            pb = jo % 2
                            for k in range(DIC):
                                on("tensor", (lambda jo, k, pb: lambda E: E.matmul(
                                    PO[pb][:],
                                    W2S[:, k * DM + jo * 128:k * DM + (jo + 1) * 128],
                                    H1[:, k * TT:(k + 1) * TT],
                                    start=(k == 0), stop=(k == DIC - 1)))(jo, k, pb),
                                    r=[W2S, H1], w=[PO[pb]], defer=(k < DIC - 1))
                            on("vector", (lambda jo, pb, e: lambda E: E.tensor_scalar(
                                TE[:], PO[pb][:], B2TA[:, 8 * e + jo:8 * e + jo + 1],
                                None, ALU.add))(jo, pb, e),
                                r=[PO[pb], B2TA], w=[TE])
                            on("vector", (lambda ti: lambda E: E.tensor_mul(
                                TE[:], TE[:], CWB[:, ti * TT:(ti + 1) * TT]))(ti),
                                r=[TE, CWB], w=[TE])
                            ak = [f"ACC.{jo}.{2 * ti}", f"ACC.{jo}.{2 * ti + 1}"]
                            on("vector", (lambda jo, ti: lambda E: E.tensor_add(
                                ACC[:, jo * tok + ti * TT:jo * tok + (ti + 1) * TT],
                                ACC[:, jo * tok + ti * TT:jo * tok + (ti + 1) * TT],
                                TE[:]))(jo, ti), r=ak + [TE], w=ak)
                    if e + 1 < NE:
                        load_w2(e + 1)
    
            if upto < 5:
                raise _Stop()
            with contextlib.ExitStack() as PE_:
                RE = lambda *a: _alloc(PE_, nc.sbuf_tensor(*a, side="right"))
                HD = RE("HD", [128, DMC * TT], BF16)
                ACB = RE("ACB", [128, DMC * TT], BF16)
                SQD = RE("SQD", [128, DMC * TT], BF16)
                MT4 = RE("MT4", [128, TT], F32)
                VT4 = RE("VT4", [128, TT], F32)
                T14 = RE("T14", [128, TT], F32)
                RT4 = RE("RT4", [128, TT], F32)
                OSB = RE("OSB", [128, DMC * TT], F32)
                PS4 = _alloc(PE_, nc.psum_tensor("PS4", [128, TT], F32))
                PQ4 = _alloc(PE_, nc.psum_tensor("PQ4", [128, TT], F32))

                for ti in range(NT):
                    for hf in range(2):
                        dma("sync",
                            HD[:].rearrange("p (c t) -> p c t", c=DMC)
                            [:, :, hf * 256:(hf + 1) * 256],
                            h_hbm[2 * ti + hf].rearrange("c p t -> p c t"),
                            r=[h_hbm], w=[HD])
                    for c in range(DMC):
                        on("scalar", (lambda c, ti: lambda E: E.activation(
                            ACB[:, c * TT:(c + 1) * TT],
                            ACC[:, c * tok + ti * TT:c * tok + (ti + 1) * TT],
                            AF.Copy))(c, ti),
                            r=[f"ACC.{c}.{2 * ti}", f"ACC.{c}.{2 * ti + 1}"], w=[ACB])
                    for c in range(DMC):
                        on("scalar", (lambda c: lambda E: E.activation(
                            SQD[:, c * TT:(c + 1) * TT], ACB[:, c * TT:(c + 1) * TT],
                            AF.Square))(c), r=[ACB], w=[SQD])
                    for c in range(DMC):
                        on("tensor", (lambda c: lambda E: E.matmul(
                            PS4[:], ONESB[:], ACB[:, c * TT:(c + 1) * TT],
                            start=(c == 0), stop=(c == DMC - 1)))(c),
                            r=[ONESB, ACB], w=[PS4], defer=(c < DMC - 1))
                    for c in range(DMC):
                        on("tensor", (lambda c: lambda E: E.matmul(
                            PQ4[:], ONESB[:], SQD[:, c * TT:(c + 1) * TT],
                            start=(c == 0), stop=(c == DMC - 1)))(c),
                            r=[ONESB, SQD], w=[PQ4], defer=(c < DMC - 1))
                    on("vector", lambda E: E.tensor_scalar(
                        MT4[:], PS4[:], 1.0 / DM, None, ALU.mult), r=[PS4], w=[MT4])
                    on("vector", lambda E: E.tensor_scalar(
                        VT4[:], PQ4[:], 1.0 / DM, None, ALU.mult), r=[PQ4], w=[VT4])
                    on("vector", lambda E: E.tensor_mul(T14[:], MT4[:], MT4[:]),
                       r=[MT4], w=[T14])
                    on("vector", lambda E: E.tensor_sub(VT4[:], VT4[:], T14[:]),
                       r=[VT4, T14], w=[VT4])
                    on("vector", lambda E: E.tensor_scalar(
                        VT4[:], VT4[:], LN_EPS, None, ALU.add), r=[VT4], w=[VT4])
                    on("vector", lambda E: E.reciprocal(T14[:], VT4[:]), r=[VT4], w=[T14])
                    on("scalar", lambda E: E.activation(RT4[:], T14[:], AF.Sqrt),
                       r=[T14], w=[RT4])
                    for c in range(DMC):
                        on("vector", (lambda c, ti: lambda E: E.tensor_sub(
                            VT4[:], ACC[:, c * tok + ti * TT:c * tok + (ti + 1) * TT],
                            MT4[:]))(c, ti),
                            r=[f"ACC.{c}.{2 * ti}", f"ACC.{c}.{2 * ti + 1}", MT4],
                            w=[VT4])
                        on("vector", lambda E: E.tensor_mul(VT4[:], VT4[:], RT4[:]),
                           r=[VT4, RT4], w=[VT4])
                        on("vector", (lambda c: lambda E: E.tensor_add(
                            OSB[:, c * TT:(c + 1) * TT], VT4[:],
                            HD[:, c * TT:(c + 1) * TT]))(c),
                            r=[VT4, HD], w=[OSB])
                    dma("sync", outT[:, ti * TT:(ti + 1) * TT].rearrange(
                        "(c p) t -> p c t", p=128), cview(OSB, DMC),
                        r=[OSB], w=[outT])

        except _Stop:
            pass
        on("sync", lambda E: E.nop(),
           waits=[("sy", now("sy")), ("gq", now("gq"))], defer=True)
        on("gpsimd", lambda E: E.nop(),
           waits=[("gq", now("gq")), ("sy", now("sy"))], defer=True)

        with nc.Block() as block:
            block.tensor(lambda E: [f(E) for f in streams["tensor"]])
            block.vector(lambda E: [f(E) for f in streams["vector"]])
            block.scalar(lambda E: [f(E) for f in streams["scalar"]])
            block.sync(lambda E: [f(E) for f in streams["sync"]])
            block.gpsimd(lambda E: [f(E) for f in streams["gpsimd"]])

    return nc


def make_aux():
    masks = []
    for c in range(NCORES):
        m = np.zeros((64, 8), np.float32)
        for k in range(4 * (c // 4), c):
            m[:, k] = 1.0
        masks.append(m)
    sel = np.zeros((8, 1024), np.float32)
    for e in range(8):
        sel[e, 128 * e:128 * (e + 1)] = 1.0
    return masks, sel.astype(BFNP), np.eye(128, dtype=np.float32)


def make_in_maps(inputs, tok=2048):
    x = np.asarray(inputs["L0_attn_x"], np.float32)
    B, S, _ = x.shape
    xf = x.reshape(B * S, DM)
    masks, sel, idm = make_aux()
    shared = {
        "W_in": np.asarray(inputs["L0_attn_W_in"], np.float32),
        "b_in": np.asarray(inputs["L0_attn_b_in"], np.float32).reshape(32, 128),
        "W_xp": np.asarray(inputs["L0_attn_W_xp"], np.float32),
        "W_dt": np.asarray(inputs["L0_attn_W_dt"], np.float32),
        "b_dt": np.asarray(inputs["L0_attn_b_dt"], np.float32).reshape(8, 128),
        "W_out": np.asarray(inputs["L0_attn_W_out"], np.float32),
        "b_out": np.asarray(inputs["L0_attn_b_out"], np.float32).reshape(8, 128),
        "bxa": np.stack([np.asarray(inputs["L0_attn_b_xp"], np.float32),
                         np.asarray(inputs["L0_attn_A_log"], np.float32)]),
        "Dv": np.asarray(inputs["L0_attn_D"], np.float32).reshape(8, 128),
        "gate_W": np.asarray(inputs["L0_ffn_gate_W"], np.float32),
        "W1": np.asarray(inputs["L0_ffn_W1"], np.float32),
        "b1": np.asarray(inputs["L0_ffn_b1"], np.float32).reshape(NE, 16, 128),
        "W2": np.asarray(inputs["L0_ffn_W2"], np.float32),
        "b2": np.asarray(inputs["L0_ffn_b2"], np.float32).reshape(NE, 8, 128),
        "idm": idm, "sel8": sel,
    }
    in_maps = []
    for c in range(NCORES):
        sl = xf[c * tok:(c + 1) * tok]
        in_maps.append({**shared,
                        "xT": np.ascontiguousarray(sl.T),
                        "maskI": masks[c]})
    return in_maps


def kernel(**inputs):
    tok = 2048
    nc = build_nc(tok)
    in_maps = make_in_maps(inputs, tok)
    res = run_bass_kernel_spmd(nc, in_maps, core_ids=list(range(NCORES)))
    x = np.asarray(inputs["L0_attn_x"])
    B, S, _ = x.shape
    parts = [np.asarray(res.results[c]["outT"]).T for c in range(NCORES)]
    out = np.concatenate(parts, axis=0).reshape(B, S, DM).astype(np.float32)
    return out


# version 06 (final)
# speedup vs baseline: 1.2281x; 1.0197x over previous
import contextlib
import numpy as np
import ml_dtypes
import concourse.bass as bass
import concourse.mybir as mybir
from concourse.bass_utils import run_bass_kernel_spmd
from concourse.replica_groups import maybe_share_collective_output_space

F32 = mybir.dt.float32
BF16 = mybir.dt.bfloat16
AF = mybir.ActivationFunctionType
ALU = mybir.AluOpType
AX = mybir.AxisListType
BFNP = ml_dtypes.bfloat16

NCORES = 8
DM = 1024
DI = 2048
DS = 64
DH = 2048
NE = 8
DMC = DM // 128
DIC = DI // 128
LN_EPS = 1e-5


def _nm(t):
    n = getattr(t, "name", None)
    if isinstance(n, str):
        return n
    try:
        return t.name()
    except Exception:
        return str(t)


class _Stop(Exception):
    pass


def build_nc(tok=2048, upto=99):
    TT = 512
    NT = tok // TT
    NG = tok // 128

    nc = bass.Bass("TRN2", target_bir_lowering=False)

    xT = nc.dram_tensor("xT", [DM, tok], F32, kind="ExternalInput")
    W_in = nc.dram_tensor("W_in", [DM, 2 * DI], F32, kind="ExternalInput")
    b_in = nc.dram_tensor("b_in", [32, 128], F32, kind="ExternalInput")
    W_xp = nc.dram_tensor("W_xp", [DI, DS], F32, kind="ExternalInput")
    W_dt = nc.dram_tensor("W_dt", [DI, DM], F32, kind="ExternalInput")
    b_dt = nc.dram_tensor("b_dt", [8, 128], F32, kind="ExternalInput")
    W_out = nc.dram_tensor("W_out", [DI, DM], F32, kind="ExternalInput")
    b_out = nc.dram_tensor("b_out", [8, 128], F32, kind="ExternalInput")
    bxa = nc.dram_tensor("bxa", [2, 64], F32, kind="ExternalInput")
    Dv = nc.dram_tensor("Dv", [8, 128], F32, kind="ExternalInput")
    gate_W = nc.dram_tensor("gate_W", [DM, NE], F32, kind="ExternalInput")
    W1 = nc.dram_tensor("W1", [NE, DM, DH], F32, kind="ExternalInput")
    b1 = nc.dram_tensor("b1", [NE, 16, 128], F32, kind="ExternalInput")
    W2 = nc.dram_tensor("W2", [NE, DH, DM], F32, kind="ExternalInput")
    b2 = nc.dram_tensor("b2", [NE, 8, 128], F32, kind="ExternalInput")
    maskI = nc.dram_tensor("maskI", [64, 8], F32, kind="ExternalInput")
    idm = nc.dram_tensor("idm", [128, 128], F32, kind="ExternalInput")
    sel8 = nc.dram_tensor("sel8", [8, 1024], BF16, kind="ExternalInput")
    outT = nc.dram_tensor("outT", [DM, tok], F32, kind="ExternalOutput")

    xi_hbm = nc.dram_tensor("xi_hbm", [NT, 128, DIC * TT], BF16, kind="Internal")
    sr_hbm = nc.dram_tensor("sr_hbm", [NT, 128, DIC * TT], BF16, kind="Internal")
    xn_hbm = nc.dram_tensor("xn_hbm", [NT, 128, DMC * TT], BF16, kind="Internal")
    NTB = tok // 256
    h_hbm = nc.dram_tensor("h_hbm", [NTB, DMC, 128, 256], BF16, kind="Internal")
    ssum_hbm = nc.dram_tensor("ssum_hbm", [128, tok], F32, kind="Internal")
    cw_hbm = nc.dram_tensor("cw_hbm", [8, tok], BF16, kind="Internal")
    xg_hbm = nc.dram_tensor("xg_hbm", [DMC, 128, tok], BF16, kind="Internal")
    rg = [list(range(NCORES))]
    space = maybe_share_collective_output_space("AllGather", rg)
    ag_in = nc.dram_tensor("ag_in", [2, 64], F32, kind="Internal")
    ag_out = nc.dram_tensor("ag_out", [16, 64], F32, kind="Internal",
                            addr_space=space)

    ENGS = ("tensor", "vector", "scalar", "sync", "gpsimd")
    ESEM = {"tensor": "t", "vector": "v", "scalar": "s", "gpsimd": "g"}
    streams = {e: [] for e in ENGS}
    cnt = {}
    wm = {e: {} for e in ENGS}
    SEM = {}
    last_w = {}
    readers = {}
    pending = {e: [] for e in ENGS}
    regions = []
    init_waits = {}

    def _alloc(stack, guard):
        h = stack.enter_context(guard)
        try:
            ml = nc.lookup_mloc(h)
            sp = str(h.space)
            sz = ml.size() if callable(ml.size) else ml.size
            lo, hi = ml.addr, ml.addr + sz
        except Exception:
            return h
        req = {}
        for sp2, lo2, hi2, nm2 in regions:
            if sp2 == sp and lo < hi2 and lo2 < hi:
                for ev in ([last_w[nm2]] if nm2 in last_w else []):
                    req[ev[0]] = max(req.get(ev[0], 0), ev[1])
                for s2, v2 in readers.get(nm2, {}).items():
                    req[s2] = max(req.get(s2, 0), v2)
        regions.append((sp, lo, hi, h.name))
        if req:
            init_waits[h.name] = req
        return h

    def _reg_read(name, ev):
        d = readers.setdefault(name, {})
        d[ev[0]] = max(d.get(ev[0], 0), ev[1])

    def _reg_write(name, ev):
        last_w[name] = ev
        readers[name] = {}

    def on(eng, f, r=(), w=(), waits=(), sem=None, amt=1, defer=False):
        rn = [_nm(t) for t in r]
        wn = [_nm(t) for t in w]
        auto = {}
        for n in rn:
            if n in last_w:
                s_, v_ = last_w[n]
                auto[s_] = max(auto.get(s_, 0), v_)
        for n in wn:
            if n in last_w:
                s_, v_ = last_w[n]
                auto[s_] = max(auto.get(s_, 0), v_)
            for s_, v_ in readers.get(n, {}).items():
                auto[s_] = max(auto.get(s_, 0), v_)
        for s_, v_ in waits:
            if v_ > 0:
                auto[s_] = max(auto.get(s_, 0), v_)
        for n in rn + wn:
            iw = init_waits.pop(n, None)
            if iw:
                for s_, v_ in iw.items():
                    auto[s_] = max(auto.get(s_, 0), v_)
        if sem is None and not defer:
            sem = ESEM.get(eng)
            if sem is None:
                raise RuntimeError(f"need explicit sem for {eng}")
        wl = sorted(auto.items())

        def op(E):
            for s_, v_ in wl:
                if wm[eng].get(s_, 0) < v_:
                    E.wait_ge(SEM[s_], v_)
                    wm[eng][s_] = v_
            ins = f(E)
            if not defer:
                ins.then_inc(SEM[sem], amt)
        streams[eng].append(op)
        if defer:
            pending[eng].extend([("r", n) for n in rn] + [("w", n) for n in wn])
            return None
        cnt[sem] = cnt.get(sem, 0) + amt
        ev = (sem, cnt[sem])
        for kind, n in pending[eng]:
            (_reg_read if kind == "r" else _reg_write)(n, ev)
        pending[eng] = []
        for n in rn:
            _reg_read(n, ev)
        for n in wn:
            _reg_write(n, ev)
        return ev

    def now(s):
        return cnt.get(s, 0)

    qidx = {"sy": 0, "gq": 0}

    def dma(eng, out_ap, in_ap, r, w, waits=()):
        q = "sy" if eng == "sync" else "gq"
        s = f"{q}{qidx[q] % 4}"
        qidx[q] += 1
        chain = [(s, cnt.get(s, 0))]
        return on(eng, lambda E: E.dma_start(out=out_ap, in_=in_ap),
                  r=r, w=w, waits=list(waits) + chain, sem=s, amt=16)

    def cview(sb, nch):
        return sb[:].rearrange("p (c j) -> p c j", c=nch)

    def wload(eng, dst_sb, src2d, nch, parts, r, w, alt=False):
        s3 = src2d.rearrange("(c p) j -> p c j", p=128)
        d3 = cview(dst_sb, nch)
        step = nch // parts
        for i in range(parts):
            e = ("sync" if i % 2 else "gpsimd") if alt else eng
            if alt and e == "sync":
                e = "gpsimd"
            dma(e, d3[:, i * step:(i + 1) * step, :],
                s3[:, i * step:(i + 1) * step, :], r=r, w=w)

    def barrier():
        on("vector", lambda E: E.memzero(FLUSH[0:1, 0:4]), w=[FLUSH])
        on("scalar", lambda E: E.memzero(FLUSH[0:1, 4:8]), w=[FLUSH])
        snap = {s: now(s) for s in SEM}
        own = {"tensor": "t", "vector": "v", "scalar": "s"}
        for eng in ENGS:
            wl = [(s, v) for s, v in snap.items() if v > 0 and own.get(eng) != s]
            on(eng, lambda E: E.nop(), waits=wl, defer=True)

    with contextlib.ExitStack() as STK:
        for s in ("t", "v", "s", "g", "cc", "sy0", "sy1", "sy2", "sy3",
                  "gq0", "gq1", "gq2", "gq3"):
            SEM[s] = STK.enter_context(nc.semaphore("sem_" + s))
            cnt[s] = 0

        L = lambda *a: _alloc(STK, nc.sbuf_tensor(*a, side="left"))
        FLUSH = L("FLUSH", [1, 8], F32)
        ID = L("ID", [128, 128], F32)
        ONESB = L("ONESB", [128, 128], BF16)
        MASK = L("MASK", [64, 8], F32)
        BINT = L("BINT", [128, 32], F32)
        BDTT = L("BDTT", [128, 8], F32)
        BDTN = L("BDTN", [128, 8], F32)
        BOUT = L("BOUT", [128, 8], F32)
        DT = L("DT", [128, 8], F32)
        BXAT = L("BXAT", [64, 2], F32)
        ASC = L("ASC", [64, 1], F32)
        GWF = L("GWF", [128, 8 * 8], F32)
        BSTG1 = L("BSTG1", [32, 128], F32)
        BSTG2 = L("BSTG2", [8, 128], F32)
        BSTG3 = L("BSTG3", [8, 128], F32)
        BSTG4 = L("BSTG4", [8, 128], F32)
        BSTG5 = L("BSTG5", [2, 64], F32)

        dma("gpsimd", ID[:], idm[:], r=[idm], w=[ID])
        dma("gpsimd", MASK[:], maskI[:], r=[maskI], w=[MASK])
        dma("gpsimd", BSTG1[:], b_in[:], r=[b_in], w=[BSTG1])
        dma("gpsimd", BSTG2[:], b_dt[:], r=[b_dt], w=[BSTG2])
        dma("gpsimd", BSTG3[:], b_out[:], r=[b_out], w=[BSTG3])
        dma("gpsimd", BSTG4[:], Dv[:], r=[Dv], w=[BSTG4])
        dma("gpsimd", BSTG5[:], bxa[:], r=[bxa], w=[BSTG5])
        dma("sync", cview(GWF, DMC),
            gate_W.ap().rearrange("(c p) e -> p c e", p=128), r=[gate_W], w=[GWF])
        on("vector", lambda E: E.memset(ONESB[:], 1.0), w=[ONESB])

        with contextlib.ExitStack() as P0:
            PT = _alloc(P0, nc.psum_tensor("PT0", [128, 32], F32))
            for (stg, dst) in ((BSTG1, BINT), (BSTG2, BDTT), (BSTG3, BOUT),
                               (BSTG4, DT), (BSTG5, BXAT)):
                p, q = stg.shape[0], stg.shape[1]
                on("tensor",
                   (lambda stg, p, q: lambda E: E.transpose(
                       PT[0:q, 0:p], stg[:], ID[0:p, 0:p]))(stg, p, q),
                   r=[stg, ID], w=[PT])
                on("vector",
                   (lambda dst, p, q: lambda E: E.tensor_copy(
                       dst[0:q, 0:p], PT[0:q, 0:p]))(dst, p, q),
                   r=[PT], w=[dst])
            on("scalar", lambda E: E.activation(ASC[:], BXAT[:, 1:2], AF.Exp),
               r=[BXAT], w=[ASC])
            on("vector", lambda E: E.tensor_scalar(
                ASC[:], ASC[:], 1.0 / DM, None, ALU.mult), r=[ASC], w=[ASC])
            on("vector", lambda E: E.tensor_scalar(
                BDTN[:], BDTT[:], -1.0, None, ALU.mult), r=[BDTT], w=[BDTN])

        try:
            if upto < 1:
                raise _Stop()
            with contextlib.ExitStack() as PA:
                RA = lambda *a: _alloc(PA, nc.sbuf_tensor(*a, side="right"))
                WIN = RA("WIN", [128, DMC * 4096], BF16)
                XTf = [RA("XTf0", [128, DMC * TT], F32), RA("XTf1", [128, DMC * TT], F32)]
                XB = RA("XB", [128, DMC * TT], BF16)
                SQ = RA("SQ", [128, DMC * TT], BF16)
                XN = RA("XN", [128, DMC * TT], BF16)
                XIT = RA("XIT", [128, DIC * TT], BF16)
                SRT = RA("SRT", [128, DIC * TT], BF16)
                MT = RA("MT", [128, TT], F32)
                VT = RA("VT", [128, TT], F32)
                T1 = RA("T1", [128, TT], F32)
                RT = RA("RT", [128, TT], F32)
                PS_S = _alloc(PA, nc.psum_tensor("PS_S", [128, TT], F32))
                PS_Q = _alloc(PA, nc.psum_tensor("PS_Q", [128, TT], F32))
                PX = [_alloc(PA, nc.psum_tensor("PXa", [128, TT], F32)),
                      _alloc(PA, nc.psum_tensor("PXb", [128, TT], F32))]

                wload("gpsimd", WIN, W_in.ap(), DMC, 2, r=[W_in], w=[WIN])

                XN2 = [XN, RA("XNq", [128, DMC * TT], BF16)]

                def a1_stage1(ti):
                    XTt = XTf[ti % 2]
                    XNt = XN2[ti % 2]
                    dma("sync", cview(XTt, DMC),
                        xT[:, ti * TT:(ti + 1) * TT].rearrange("(c p) t -> p c t", p=128),
                        r=[xT], w=[XTt])
                    for c in range(DMC):
                        on("vector", (lambda c, XTt: lambda E: E.tensor_copy(
                            XB[:, c * TT:(c + 1) * TT], XTt[:, c * TT:(c + 1) * TT]))(c, XTt),
                            r=[XTt], w=[XB])
                    for c in range(DMC):
                        on("scalar", (lambda c: lambda E: E.activation(
                            SQ[:, c * TT:(c + 1) * TT], XB[:, c * TT:(c + 1) * TT],
                            AF.Square))(c), r=[XB], w=[SQ])
                    for c in range(DMC):
                        on("tensor", (lambda c: lambda E: E.matmul(
                            PS_S[:], ONESB[:], XB[:, c * TT:(c + 1) * TT],
                            start=(c == 0), stop=(c == DMC - 1)))(c),
                            r=[ONESB, XB], w=[PS_S], defer=(c < DMC - 1))
                    for c in range(DMC):
                        on("tensor", (lambda c: lambda E: E.matmul(
                            PS_Q[:], ONESB[:], SQ[:, c * TT:(c + 1) * TT],
                            start=(c == 0), stop=(c == DMC - 1)))(c),
                            r=[ONESB, SQ], w=[PS_Q], defer=(c < DMC - 1))
                    on("vector", lambda E: E.tensor_scalar(
                        MT[:], PS_S[:], 1.0 / DM, None, ALU.mult), r=[PS_S], w=[MT])
                    on("vector", lambda E: E.tensor_scalar(
                        VT[:], PS_Q[:], 1.0 / DM, None, ALU.mult), r=[PS_Q], w=[VT])
                    on("vector", lambda E: E.tensor_mul(T1[:], MT[:], MT[:]),
                       r=[MT], w=[T1])
                    on("vector", lambda E: E.tensor_sub(VT[:], VT[:], T1[:]),
                       r=[VT, T1], w=[VT])
                    on("vector", lambda E: E.tensor_scalar(
                        VT[:], VT[:], LN_EPS, None, ALU.add), r=[VT], w=[VT])
                    on("vector", lambda E: E.reciprocal(T1[:], VT[:]), r=[VT], w=[T1])
                    on("scalar", lambda E: E.activation(RT[:], T1[:], AF.Sqrt),
                       r=[T1], w=[RT])
                    for c in range(DMC):
                        on("vector", (lambda c, XTt: lambda E: E.tensor_sub(
                            VT[:], XTt[:, c * TT:(c + 1) * TT], MT[:]))(c, XTt),
                            r=[XTt, MT], w=[VT])
                        on("vector", (lambda c, XNt: lambda E: E.tensor_mul(
                            XNt[:, c * TT:(c + 1) * TT], VT[:], RT[:]))(c, XNt),
                            r=[VT, RT], w=[XNt])

                def a1_stage2(ti):
                    XNt = XN2[ti % 2]
                    for j in range(32):
                        pb = j % 2
                        for c in range(DMC):
                            on("tensor", (lambda j, c, pb, XNt: lambda E: E.matmul(
                                PX[pb][:],
                                WIN[:, c * 4096 + j * 128:c * 4096 + (j + 1) * 128],
                                XNt[:, c * TT:(c + 1) * TT],
                                start=(c == 0), stop=(c == DMC - 1)))(j, c, pb, XNt),
                                r=[WIN, XNt], w=[PX[pb]], defer=(c < DMC - 1))
                        dst = XIT if j < 16 else SRT
                        jj = j if j < 16 else j - 16
                        on("scalar", (lambda j, jj, dst, pb: lambda E: E.activation(
                            dst[:, jj * TT:(jj + 1) * TT], PX[pb][:], AF.Silu,
                            bias=BINT[:, j:j + 1]))(j, jj, dst, pb),
                            r=[PX[pb], BINT], w=[dst])
                        if j >= 16:
                            on("vector", (lambda jj: lambda E: E.tensor_mul(
                                SRT[:, jj * TT:(jj + 1) * TT],
                                SRT[:, jj * TT:(jj + 1) * TT],
                                XIT[:, jj * TT:(jj + 1) * TT]))(jj),
                                r=[SRT, XIT], w=[SRT])
                    dma("sync", xi_hbm[ti], XIT[:], r=[XIT], w=[xi_hbm])
                    dma("gpsimd", sr_hbm[ti], SRT[:], r=[SRT], w=[sr_hbm])
                    dma("gpsimd", xn_hbm[ti], XNt[:], r=[XNt], w=[xn_hbm])

                a1_stage1(0)
                for ti in range(NT):
                    if ti + 1 < NT:
                        a1_stage1(ti + 1)
                    a1_stage2(ti)

            if upto < 2:
                raise _Stop()
            with contextlib.ExitStack() as PB:
                RB = lambda *a: _alloc(PB, nc.sbuf_tensor(*a, side="right"))
                WDT = RB("WDT", [128, DIC * 1024], BF16)
                WXP = RB("WXP", [128, DIC * 64], BF16)
                BX = RB("BX", [64, tok], F32)
                AFULL = RB("AFULL", [64, tok], F32)
                XIJ = [RB("XIJ0", [128, DIC * TT], BF16), RB("XIJ1", [128, DIC * TT], BF16)]
                SPB = RB("SPB", [128, DMC * TT], BF16)
                TA = RB("TA", [128, TT], F32)
                TB = RB("TB", [128, TT], F32)
                TC = RB("TC", [128, TT], F32)
                TD = RB("TD", [128, TT], F32)
                PD = [_alloc(PB, nc.psum_tensor("PDa", [128, TT], F32)),
                      _alloc(PB, nc.psum_tensor("PDb", [128, TT], F32))]
                PMD = _alloc(PB, nc.psum_tensor("PMD", [128, TT], F32))
                PBX = _alloc(PB, nc.psum_tensor("PBX", [64, TT], F32))
                PSS = _alloc(PB, nc.psum_tensor("PSS", [128, TT], F32))
                PSC = _alloc(PB, nc.psum_tensor("PSC", [128, 128], F32))

                A2L = (upto - 210) if 210 <= upto < 220 else 9
                wload("gpsimd", WDT, W_dt.ap(), DIC, 4, r=[W_dt], w=[WDT])
                wload("gpsimd", WXP, W_xp.ap(), DIC, 1, r=[W_xp], w=[WXP])

                for ti in range(NT):
                    XJ = XIJ[ti % 2]
                    dma("sync", XJ[:], xi_hbm[ti], r=[xi_hbm], w=[XJ])
                    for j8 in range(DMC if A2L >= 2 else 0):
                        pb = j8 % 2
                        for k in range(DIC):
                            on("tensor", (lambda j8, k, pb, XJ: lambda E: E.matmul(
                                PD[pb][:],
                                WDT[:, k * 1024 + j8 * 128:k * 1024 + (j8 + 1) * 128],
                                XJ[:, k * TT:(k + 1) * TT],
                                start=(k == 0), stop=(k == DIC - 1)))(j8, k, pb, XJ),
                                r=[WDT, XJ], w=[PD[pb]], defer=(k < DIC - 1))
                        on("scalar", (lambda j8, pb: lambda E: E.activation(
                            TA[:], PD[pb][:], AF.Sigmoid, scale=-1.0,
                            bias=BDTN[:, j8:j8 + 1]))(j8, pb),
                            r=[PD[pb], BDTN], w=[TA])
                        on("scalar", (lambda j8: lambda E: E.activation(
                            SPB[:, j8 * TT:(j8 + 1) * TT], TA[:], AF.Ln))(j8),
                            r=[TA], w=[SPB])
                        on("tensor", (lambda j8: lambda E: E.matmul(
                            PMD[:], ONESB[:], SPB[:, j8 * TT:(j8 + 1) * TT],
                            start=(j8 == 0), stop=(j8 == DMC - 1)))(j8),
                            r=[ONESB, SPB], w=[PMD], defer=(j8 < DMC - 1))
                    if A2L >= 3:
                        on("scalar", (lambda ti: lambda E: E.activation(
                            AFULL[:, ti * TT:(ti + 1) * TT], PMD[0:64, :], AF.Exp,
                            scale=ASC[:, 0:1]))(ti),
                            r=[PMD, ASC], w=[AFULL])
                    for k in range(DIC if A2L >= 4 else 0):
                        on("tensor", (lambda k, XJ: lambda E: E.matmul(
                            PBX[:], WXP[:, 64 * k:64 * (k + 1)],
                            XJ[:, k * TT:(k + 1) * TT],
                            start=(k == 0), stop=(k == DIC - 1)))(k, XJ),
                            r=[WXP, XJ], w=[PBX], defer=(k < DIC - 1))
                    if A2L >= 4:
                        on("vector", (lambda ti: lambda E: E.tensor_scalar(
                            BX[:, ti * TT:(ti + 1) * TT], PBX[:], BXAT[:, 0:1], None,
                            ALU.add))(ti),
                            r=[PBX, BXAT], w=[BX])

                if upto < 22:
                    raise _Stop()
                ZER = RB("ZER", [64, tok], F32)
                STATE = RB("STATE", [64, tok], F32)
                CP = RB("CP", [64, tok], F32)
                STCB = RB("STCB", [64, tok], BF16)
                SUM2 = RB("SUM2", [64, 2], F32)
                SUM2T = RB("SUM2T", [2, 64], F32)
                AGSB = RB("AGSB", [16, 64], F32)
                AGT = RB("AGT", [64, 16], F32)
                SIN = RB("SIN", [64, 1], F32)
                TF = RB("TF", [64, 1], F32)
                SSB = RB("SSB", [128, tok], F32)

                on("vector", lambda E: E.memset(ZER[:], 0.0), w=[ZER])
                on("vector", lambda E: E.tensor_tensor_scan(
                    STATE[:], AFULL[:], BX[:], 0.0, ALU.mult, ALU.add),
                    r=[AFULL, BX], w=[STATE])
                on("vector", lambda E: E.tensor_tensor_scan(
                    CP[:], AFULL[:], ZER[:], 1.0, ALU.mult, ALU.add),
                    r=[AFULL, ZER], w=[CP])
                on("vector", lambda E: E.tensor_copy(SUM2[:, 0:1], CP[:, tok - 1:tok]),
                   r=[CP], w=[SUM2])
                on("vector", lambda E: E.tensor_copy(SUM2[:, 1:2], STATE[:, tok - 1:tok]),
                   r=[STATE], w=[SUM2])
                on("tensor", lambda E: E.transpose(
                    PSC[0:2, 0:64], SUM2[:], ID[0:64, 0:64]), r=[SUM2, ID], w=[PSC])
                on("vector", lambda E: E.tensor_copy(SUM2T[:], PSC[0:2, 0:64]),
                   r=[PSC], w=[SUM2T])
                if upto < 23:
                    raise _Stop()
                dma("gpsimd", ag_in[:], SUM2T[:], r=[SUM2T], w=[ag_in])
                on("gpsimd", lambda E: E.collective_compute(
                    "AllGather", ALU.bypass, replica_groups=rg,
                    ins=[ag_in[:]], outs=[ag_out[:]]),
                    r=[ag_in], w=[ag_out], sem="cc", amt=1)
                dma("gpsimd", AGSB[:], ag_out[:], r=[ag_out], w=[AGSB])
                on("tensor", lambda E: E.transpose(
                    PSC[0:64, 0:16], AGSB[:], ID[0:16, 0:16]), r=[AGSB, ID], w=[PSC])
                on("vector", lambda E: E.tensor_copy(AGT[:], PSC[0:64, 0:16]),
                   r=[PSC], w=[AGT])
                if upto < 24:
                    raise _Stop()
                on("vector", lambda E: E.memset(SIN[:], 0.0), w=[SIN])
                for k in range(NCORES):
                    on("vector", (lambda k: lambda E: E.scalar_tensor_tensor(
                        TF[:], SIN[:], AGT[:, 2 * k:2 * k + 1],
                        AGT[:, 2 * k + 1:2 * k + 2], ALU.mult, ALU.add))(k),
                        r=[SIN, AGT], w=[TF])
                    on("vector", lambda E: E.tensor_sub(TF[:], TF[:], SIN[:]),
                       r=[TF, SIN], w=[TF])
                    on("vector", (lambda k: lambda E: E.tensor_mul(
                        TF[:], TF[:], MASK[:, k:k + 1]))(k), r=[TF, MASK], w=[TF])
                    on("vector", lambda E: E.tensor_add(SIN[:], SIN[:], TF[:]),
                       r=[SIN, TF], w=[SIN])
                on("vector", lambda E: E.scalar_tensor_tensor(
                    STATE[:], CP[:], SIN[:, 0:1], STATE[:], ALU.mult, ALU.add),
                    r=[CP, SIN, STATE], w=[STATE])
                on("vector", lambda E: E.tensor_copy(STCB[:], STATE[:]),
                   r=[STATE], w=[STCB])
                if upto < 25:
                    raise _Stop()
                for i in range(NT):
                    on("tensor", (lambda i: lambda E: E.matmul(
                        PSS[:], ONESB[0:64, :], STCB[:, i * TT:(i + 1) * TT],
                        start=True, stop=True))(i),
                        r=[ONESB, STCB], w=[PSS])
                    on("vector", (lambda i: lambda E: E.tensor_copy(
                        SSB[:, i * TT:(i + 1) * TT], PSS[:]))(i), r=[PSS], w=[SSB])
                dma("sync", ssum_hbm[:], SSB[:], r=[SSB], w=[ssum_hbm])
    
            if upto < 3:
                raise _Stop()
            ACC = L("ACC", [128, DMC * tok], F32)
            TTB = 256
            NTB2 = tok // TTB

            with contextlib.ExitStack() as PC:
                RC = lambda *a: _alloc(PC, nc.sbuf_tensor(*a, side="right"))
                WOUT = RC("WOUT", [128, DIC * 1024], BF16)
                SRJ = [RC("SRJ0", [128, DIC * TTB], BF16),
                       RC("SRJ1", [128, DIC * TTB], BF16)]
                XTb = [RC("XTb0", [128, DMC * TTB], BF16),
                       RC("XTb1", [128, DMC * TTB], BF16)]
                XNb = [RC("XNb0", [128, DMC * TTB], BF16),
                       RC("XNb1", [128, DMC * TTB], BF16)]
                MIX = [RC("MIX0", [128, DMC * TTB], BF16),
                       RC("MIX1", [128, DMC * TTB], BF16)]
                HTb = [RC("HTb0", [128, DMC * TTB], BF16),
                       RC("HTb1", [128, DMC * TTB], BF16)]
                SQB = [RC("SQB0", [128, DMC * TTB], BF16),
                       RC("SQB1", [128, DMC * TTB], BF16)]
                SSBb = [RC("SSBb0", [128, TTB], F32), RC("SSBb1", [128, TTB], F32)]
                LGF = RC("LGF", [8, tok], F32)
                MT2 = [RC("MT2a", [128, TTB], F32), RC("MT2b", [128, TTB], F32)]
                RT2 = [RC("RT2a", [128, TTB], F32), RC("RT2b", [128, TTB], F32)]
                M3T = [RC("M3Ta", [128, TTB], F32), RC("M3Tb", [128, TTB], F32)]
                R3T = [RC("R3Ta", [128, TTB], F32), RC("R3Tb", [128, TTB], F32)]
                VT2 = RC("VT2", [128, TTB], F32)
                VT2b = RC("VT2b", [128, TTB], F32)
                T12 = RC("T12", [128, TTB], F32)
                PY = [_alloc(PC, nc.psum_tensor("PYa", [128, TTB], F32)),
                      _alloc(PC, nc.psum_tensor("PYb", [128, TTB], F32))]
                PS2 = _alloc(PC, nc.psum_tensor("PS2", [128, TTB], F32))
                PQ2 = _alloc(PC, nc.psum_tensor("PQ2", [128, TTB], F32))
                PLG = _alloc(PC, nc.psum_tensor("PLG", [8, TTB], F32))

                wload("gpsimd", WOUT, W_out.ap(), DIC, 4, r=[W_out], w=[WOUT])

                def lnstats(src_bf, sq, mdst, rdst):
                    for c in range(DMC):
                        on("scalar", (lambda c: lambda E: E.activation(
                            sq[:, c * TTB:(c + 1) * TTB],
                            src_bf[:, c * TTB:(c + 1) * TTB],
                            AF.Square))(c), r=[src_bf], w=[sq])
                    for c in range(DMC):
                        on("tensor", (lambda c: lambda E: E.matmul(
                            PS2[:], ONESB[:], src_bf[:, c * TTB:(c + 1) * TTB],
                            start=(c == 0), stop=(c == DMC - 1)))(c),
                            r=[ONESB, src_bf], w=[PS2], defer=(c < DMC - 1))
                    for c in range(DMC):
                        on("tensor", (lambda c: lambda E: E.matmul(
                            PQ2[:], ONESB[:], sq[:, c * TTB:(c + 1) * TTB],
                            start=(c == 0), stop=(c == DMC - 1)))(c),
                            r=[ONESB, sq], w=[PQ2], defer=(c < DMC - 1))
                    on("vector", (lambda mdst: lambda E: E.tensor_scalar(
                        mdst[:], PS2[:], 1.0 / DM, None, ALU.mult))(mdst),
                        r=[PS2], w=[mdst])
                    on("vector", lambda E: E.tensor_scalar(
                        VT2[:], PQ2[:], 1.0 / DM, None, ALU.mult), r=[PQ2], w=[VT2])
                    on("vector", (lambda mdst: lambda E: E.tensor_mul(
                        T12[:], mdst[:], mdst[:]))(mdst), r=[mdst], w=[T12])
                    on("vector", lambda E: E.tensor_sub(VT2[:], VT2[:], T12[:]),
                       r=[VT2, T12], w=[VT2])
                    on("vector", lambda E: E.tensor_scalar(
                        VT2[:], VT2[:], LN_EPS, None, ALU.add), r=[VT2], w=[VT2])
                    on("vector", lambda E: E.reciprocal(T12[:], VT2[:]),
                       r=[VT2], w=[T12])
                    on("scalar", (lambda rdst: lambda E: E.activation(
                        rdst[:], T12[:], AF.Sqrt))(rdst), r=[T12], w=[rdst])

                def b_stage1(tb):
                    p = tb % 2
                    ti, hf = tb // 2, tb % 2
                    sr, xb_, xn_, mx, ss = SRJ[p], XTb[p], XNb[p], MIX[p], SSBb[p]
                    dma("sync", sr[:].rearrange("p (k t) -> p k t", k=DIC),
                        sr_hbm[ti].rearrange("p (k t) -> p k t", k=DIC)
                        [:, :, hf * TTB:(hf + 1) * TTB],
                        r=[sr_hbm], w=[sr])
                    dma("sync", xn_[:].rearrange("p (c t) -> p c t", c=DMC),
                        xn_hbm[ti].rearrange("p (c t) -> p c t", c=DMC)
                        [:, :, hf * TTB:(hf + 1) * TTB],
                        r=[xn_hbm], w=[xn_])
                    dma("gpsimd", cview(xb_, DMC),
                        xT[:, tb * TTB:(tb + 1) * TTB].rearrange(
                            "(c p) t -> p c t", p=128),
                        r=[xT], w=[xb_])
                    dma("gpsimd", ss[:], ssum_hbm[:, tb * TTB:(tb + 1) * TTB],
                        r=[ssum_hbm], w=[ss])
                    for j8 in range(DMC):
                        pb = j8 % 2
                        for k in range(DIC):
                            on("tensor", (lambda j8, k, pb, sr: lambda E: E.matmul(
                                PY[pb][:],
                                WOUT[:, k * 1024 + j8 * 128:k * 1024 + (j8 + 1) * 128],
                                sr[:, k * TTB:(k + 1) * TTB],
                                start=(k == 0), stop=(k == DIC - 1)))(j8, k, pb, sr),
                                r=[WOUT, sr], w=[PY[pb]], defer=(k < DIC - 1))
                        ssl = SSBb[tb % 2]
                        on("vector", (lambda j8, pb, ssl: lambda E: E.tensor_mul(
                            T12[:], PY[pb][:], ssl[:]))(j8, pb, ssl),
                            r=[PY[pb], ssl], w=[T12])
                        on("vector", (lambda j8: lambda E: E.tensor_scalar(
                            T12[:], T12[:], BOUT[:, j8:j8 + 1], None,
                            ALU.add))(j8), r=[T12, BOUT], w=[T12])
                        on("vector", (lambda j8, mx, xn_: lambda E: E.scalar_tensor_tensor(
                            mx[:, j8 * TTB:(j8 + 1) * TTB],
                            xn_[:, j8 * TTB:(j8 + 1) * TTB],
                            DT[:, j8:j8 + 1], T12[:], ALU.mult, ALU.add))(j8, mx, xn_),
                            r=[xn_, DT, T12], w=[mx])

                def b_stage2(tb):
                    p = tb % 2
                    mx, ht, sq, xb_ = MIX[p], HTb[p], SQB[p], XTb[p]
                    m2, r2, m3, r3 = MT2[p], RT2[p], M3T[p], R3T[p]
                    lnstats(mx, sq, m2, r2)
                    for c in range(DMC):
                        vt = VT2 if c % 2 == 0 else VT2b
                        on("vector", (lambda c, mx, m2, vt: lambda E: E.tensor_sub(
                            vt[:], mx[:, c * TTB:(c + 1) * TTB], m2[:]))(c, mx, m2, vt),
                            r=[mx, m2], w=[vt])
                        on("vector", (lambda r2, vt: lambda E: E.tensor_mul(
                            vt[:], vt[:], r2[:]))(r2, vt), r=[vt, r2], w=[vt])
                        on("gpsimd", (lambda c, ht, xb_, vt: lambda E: E.tensor_add(
                            ht[:, c * TTB:(c + 1) * TTB], vt[:],
                            xb_[:, c * TTB:(c + 1) * TTB]))(c, ht, xb_, vt),
                            r=[vt, xb_], w=[ht])
                    dma("sync", h_hbm[tb].rearrange("c p t -> p c t"), cview(ht, DMC),
                        r=[ht], w=[h_hbm])
                    lnstats(ht, sq, m3, r3)
                    for c in range(DMC):
                        vt = VT2 if c % 2 == 0 else VT2b
                        on("vector", (lambda c, ht, m3, vt: lambda E: E.tensor_sub(
                            vt[:], ht[:, c * TTB:(c + 1) * TTB], m3[:]))(c, ht, m3, vt),
                            r=[ht, m3], w=[vt])
                        on("gpsimd", (lambda c, tb, r3, vt: lambda E: E.tensor_mul(
                            ACC[:, c * tok + tb * TTB:c * tok + (tb + 1) * TTB],
                            vt[:], r3[:]))(c, tb, r3, vt),
                            r=[vt, r3], w=[f"ACC.{c}.{tb}"])
                        on("scalar", (lambda c, tb, sq: lambda E: E.activation(
                            sq[:, c * TTB:(c + 1) * TTB],
                            ACC[:, c * tok + tb * TTB:c * tok + (tb + 1) * TTB],
                            AF.Copy))(c, tb, sq),
                            r=[f"ACC.{c}.{tb}"], w=[sq])
                    dma("sync",
                        xg_hbm[:, :, tb * TTB:(tb + 1) * TTB].rearrange(
                            "c p t -> p c t"),
                        cview(sq, DMC), r=[sq], w=[xg_hbm])
                    for c in range(DMC):
                        on("tensor", (lambda c, tb: lambda E: E.matmul(
                            PLG[:], GWF[:, 8 * c:8 * (c + 1)],
                            ACC[:, c * tok + tb * TTB:c * tok + (tb + 1) * TTB],
                            start=(c == 0), stop=(c == DMC - 1)))(c, tb),
                            r=[GWF, f"ACC.{c}.{tb}"], w=[PLG], defer=(c < DMC - 1))
                    on("vector", (lambda tb: lambda E: E.tensor_copy(
                        LGF[:, tb * TTB:(tb + 1) * TTB], PLG[:]))(tb),
                        r=[PLG], w=[LGF])

                b_stage1(0)
                for tb in range(NTB2):
                    if tb + 1 < NTB2:
                        b_stage1(tb + 1)
                    b_stage2(tb)

                CWFMB = RC("CWFMB", [8, tok], BF16)
                CWG = RC("CWG", [128, NG * 8], F32)
                EG = RC("EG", [128, NG * 8], F32)
                GG = RC("GG", [128, NG * 8], F32)
                K1 = RC("K1", [128, NG * 8], F32)
                GS = RC("GS", [128, NG * 8], F32)
                M1 = RC("M1", [128, NG], F32)
                M1N = RC("M1N", [128, NG], F32)
                S1 = RC("S1", [128, NG], F32)
                R1 = RC("R1", [128, NG], F32)
                M2 = RC("M2", [128, NG], F32)
                M3g = RC("M3g", [128, NG], F32)
                S2 = RC("S2", [128, NG], F32)
                R2 = RC("R2", [128, NG], F32)
                PGT = _alloc(PC, nc.psum_tensor("PGT", [128, NG * 8], F32))
                PGB = [_alloc(PC, nc.psum_tensor("PGB0", [8, 128], F32)),
                       _alloc(PC, nc.psum_tensor("PGB1", [8, 128], F32))]

                for g in range(NG):
                    on("tensor", (lambda g: lambda E: E.transpose(
                        PGT[:, 8 * g:8 * (g + 1)], LGF[:, 128 * g:128 * (g + 1)],
                        ID[0:8, 0:8]))(g),
                        r=[LGF, ID], w=[PGT], defer=(g < NG - 1))
                on("vector", lambda E: E.tensor_copy(CWG[:], PGT[:]), r=[PGT], w=[CWG])
                for g in range(NG):
                    on("vector", (lambda g: lambda E: E.reduce_max(
                        M1[:, g:g + 1], CWG[:, 8 * g:8 * (g + 1)], AX.X))(g),
                        r=[CWG], w=[M1])
                on("vector", lambda E: E.tensor_scalar(
                    M1N[:], M1[:], -1.0, None, ALU.mult), r=[M1], w=[M1N])
                for g in range(NG):
                    on("scalar", (lambda g: lambda E: E.activation(
                        EG[:, 8 * g:8 * (g + 1)], CWG[:, 8 * g:8 * (g + 1)], AF.Exp,
                        bias=M1N[:, g:g + 1]))(g), r=[CWG, M1N], w=[EG])
                for g in range(NG):
                    on("vector", (lambda g: lambda E: E.reduce_sum(
                        S1[:, g:g + 1], EG[:, 8 * g:8 * (g + 1)], AX.X))(g),
                        r=[EG], w=[S1])
                on("vector", lambda E: E.reciprocal(R1[:], S1[:]), r=[S1], w=[R1])
                for g in range(NG):
                    on("vector", (lambda g: lambda E: E.tensor_scalar(
                        GG[:, 8 * g:8 * (g + 1)], EG[:, 8 * g:8 * (g + 1)],
                        R1[:, g:g + 1], None, ALU.mult))(g), r=[EG, R1], w=[GG])
                for g in range(NG):
                    on("vector", (lambda g: lambda E: E.reduce_max(
                        M2[:, g:g + 1], GG[:, 8 * g:8 * (g + 1)], AX.X))(g),
                        r=[GG], w=[M2])
                for g in range(NG):
                    on("vector", (lambda g: lambda E: E.tensor_scalar(
                        K1[:, 8 * g:8 * (g + 1)], GG[:, 8 * g:8 * (g + 1)],
                        M2[:, g:g + 1], None, ALU.is_ge))(g), r=[GG, M2], w=[K1])
                on("vector", lambda E: E.scalar_tensor_tensor(
                    K1[:], K1[:], -1e9, GG[:], ALU.mult, ALU.add), r=[K1, GG], w=[K1])
                for g in range(NG):
                    on("vector", (lambda g: lambda E: E.reduce_max(
                        M3g[:, g:g + 1], K1[:, 8 * g:8 * (g + 1)], AX.X))(g),
                        r=[K1], w=[M3g])
                for g in range(NG):
                    on("vector", (lambda g: lambda E: E.tensor_scalar(
                        K1[:, 8 * g:8 * (g + 1)], GG[:, 8 * g:8 * (g + 1)],
                        M3g[:, g:g + 1], None, ALU.is_ge))(g), r=[GG, M3g], w=[K1])
                on("vector", lambda E: E.tensor_mul(GS[:], GG[:], K1[:]),
                   r=[GG, K1], w=[GS])
                for g in range(NG):
                    on("vector", (lambda g: lambda E: E.reduce_sum(
                        S2[:, g:g + 1], GS[:, 8 * g:8 * (g + 1)], AX.X))(g),
                        r=[GS], w=[S2])
                on("vector", lambda E: E.tensor_scalar(
                    S2[:], S2[:], 1e-8, None, ALU.add), r=[S2], w=[S2])
                on("vector", lambda E: E.reciprocal(R2[:], S2[:]), r=[S2], w=[R2])
                for g in range(NG):
                    on("vector", (lambda g: lambda E: E.tensor_scalar(
                        GS[:, 8 * g:8 * (g + 1)], GS[:, 8 * g:8 * (g + 1)],
                        R2[:, g:g + 1], None, ALU.mult))(g), r=[GS, R2], w=[GS])
                for g in range(NG):
                    gb = PGB[g % 2]
                    on("tensor", (lambda g, gb: lambda E: E.transpose(
                        gb[:], GS[:, 8 * g:8 * (g + 1)], ID[:]))(g, gb),
                        r=[GS, ID], w=[gb])
                    on("vector", (lambda g, gb: lambda E: E.tensor_copy(
                        CWFMB[:, 128 * g:128 * (g + 1)], gb[:]))(g, gb),
                        r=[gb], w=[CWFMB])
                dma("sync", cw_hbm[:], CWFMB[:], r=[CWFMB], w=[cw_hbm])
    
            if upto < 4:
                raise _Stop()
            with contextlib.ExitStack() as PD_:
                RD = lambda *a: _alloc(PD_, nc.sbuf_tensor(*a, side="right"))
                W1S = RD("W1S", [128, DMC * DH], BF16)
                W2S = RD("W2S", [128, DIC * DM], BF16)
                H1 = RD("H1", [128, DIC * TT], BF16)
                XGb = [RD("XGb0", [128, DMC * TT], BF16),
                       RD("XGb1", [128, DMC * TT], BF16)]
                CWB = RD("CWB", [128, tok], BF16)
                CWFM = RD("CWFM", [8, tok], BF16)
                SEL8 = RD("SEL8", [8, 1024], BF16)
                B1STG = RD("B1STG", [16, 128], F32)
                B2STG = RD("B2STG", [8, 128], F32)
                B1TA = RD("B1TA", [128, NE * 16], F32)
                B2TA = RD("B2TA", [128, NE * 8], F32)
                TE = RD("TE", [128, TT], F32)
                PH = [_alloc(PD_, nc.psum_tensor("PHa", [128, TT], F32)),
                      _alloc(PD_, nc.psum_tensor("PHb", [128, TT], F32))]
                PO = [_alloc(PD_, nc.psum_tensor("POa", [128, TT], F32)),
                      _alloc(PD_, nc.psum_tensor("POb", [128, TT], F32))]
                PCW = _alloc(PD_, nc.psum_tensor("PCW", [128, TT], F32))
                PBT = _alloc(PD_, nc.psum_tensor("PBT", [128, 16], F32))

                dma("gpsimd", SEL8[:], sel8[:], r=[sel8], w=[SEL8])
                dma("gpsimd", CWFM[:], cw_hbm[:], r=[cw_hbm], w=[CWFM])
                for e in range(NE):
                    dma("sync", B1STG[:], b1[e], r=[b1], w=[B1STG])
                    on("tensor", (lambda e: lambda E: E.transpose(
                        PBT[:, 0:16], B1STG[:], ID[0:16, 0:16]))(e),
                        r=[B1STG, ID], w=[PBT])
                    on("vector", (lambda e: lambda E: E.tensor_copy(
                        B1TA[:, 16 * e:16 * (e + 1)], PBT[:, 0:16]))(e),
                        r=[PBT], w=[B1TA])
                    dma("sync", B2STG[:], b2[e], r=[b2], w=[B2STG])
                    on("tensor", (lambda e: lambda E: E.transpose(
                        PBT[:, 0:8], B2STG[:], ID[0:8, 0:8]))(e),
                        r=[B2STG, ID], w=[PBT])
                    on("vector", (lambda e: lambda E: E.tensor_copy(
                        B2TA[:, 8 * e:8 * (e + 1)], PBT[:, 0:8]))(e),
                        r=[PBT], w=[B2TA])

                def load_w1(e):
                    wload("gpsimd", W1S, W1[e], DMC, 2, r=[W1], w=[W1S])

                def load_w2(e):
                    wload("gpsimd", W2S, W2[e], DIC, 4, r=[W2], w=[W2S])

                def load_xg(cti, buf):
                    dma("sync", cview(buf, DMC),
                        xg_hbm[:, :, cti * TT:(cti + 1) * TT].rearrange(
                            "c p t -> p c t"),
                        r=[xg_hbm], w=[buf])

                load_w1(0)
                load_w2(0)
                for i in range(min(2, NT)):
                    load_xg(i, XGb[i % 2])
                for e in range(NE):
                    for i in range(NT):
                        on("tensor", (lambda e, i: lambda E: E.matmul(
                            PCW[:], SEL8[:, 128 * e:128 * (e + 1)],
                            CWFM[:, i * TT:(i + 1) * TT], start=True, stop=True))(e, i),
                            r=[SEL8, CWFM], w=[PCW])
                        on("vector", (lambda i: lambda E: E.tensor_copy(
                            CWB[:, i * TT:(i + 1) * TT], PCW[:]))(i), r=[PCW], w=[CWB])
                    for ti in range(NT):
                        XGt = XGb[ti % 2]
                        for jh in range(DIC):
                            pb = jh % 2
                            for c in range(DMC):
                                on("tensor", (lambda jh, c, pb, XGt: lambda E: E.matmul(
                                    PH[pb][:],
                                    W1S[:, c * DH + jh * 128:c * DH + (jh + 1) * 128],
                                    XGt[:, c * TT:(c + 1) * TT],
                                    start=(c == 0), stop=(c == DMC - 1)))(jh, c, pb, XGt),
                                    r=[W1S, XGt], w=[PH[pb]], defer=(c < DMC - 1))
                            on("scalar", (lambda jh, pb, e: lambda E: E.activation(
                                H1[:, jh * TT:(jh + 1) * TT], PH[pb][:], AF.Relu,
                                bias=B1TA[:, 16 * e + jh:16 * e + jh + 1]))(jh, pb, e),
                                r=[PH[pb], B1TA], w=[H1])
                        nxt = e * NT + ti + 2
                        if nxt < NE * NT:
                            load_xg((nxt % NT) if NT > 1 else 0, XGb[(ti + 2) % 2] if NT > 1 else XGb[0])
                        if ti == NT - 1 and e + 1 < NE:
                            load_w1(e + 1)
                        for jo in range(DMC):
                            pb = jo % 2
                            for k in range(DIC):
                                on("tensor", (lambda jo, k, pb: lambda E: E.matmul(
                                    PO[pb][:],
                                    W2S[:, k * DM + jo * 128:k * DM + (jo + 1) * 128],
                                    H1[:, k * TT:(k + 1) * TT],
                                    start=(k == 0), stop=(k == DIC - 1)))(jo, k, pb),
                                    r=[W2S, H1], w=[PO[pb]], defer=(k < DIC - 1))
                            on("vector", (lambda jo, pb, e: lambda E: E.tensor_scalar(
                                TE[:], PO[pb][:], B2TA[:, 8 * e + jo:8 * e + jo + 1],
                                None, ALU.add))(jo, pb, e),
                                r=[PO[pb], B2TA], w=[TE])
                            on("vector", (lambda ti: lambda E: E.tensor_mul(
                                TE[:], TE[:], CWB[:, ti * TT:(ti + 1) * TT]))(ti),
                                r=[TE, CWB], w=[TE])
                            ak = [f"ACC.{jo}.{2 * ti}", f"ACC.{jo}.{2 * ti + 1}"]
                            on("vector", (lambda jo, ti: lambda E: E.tensor_add(
                                ACC[:, jo * tok + ti * TT:jo * tok + (ti + 1) * TT],
                                ACC[:, jo * tok + ti * TT:jo * tok + (ti + 1) * TT],
                                TE[:]))(jo, ti), r=ak + [TE], w=ak)
                    if e + 1 < NE:
                        load_w2(e + 1)
    
            if upto < 5:
                raise _Stop()
            with contextlib.ExitStack() as PE_:
                RE = lambda *a: _alloc(PE_, nc.sbuf_tensor(*a, side="right"))
                HD = RE("HD", [128, DMC * TT], BF16)
                ACB = RE("ACB", [128, DMC * TT], BF16)
                SQD = RE("SQD", [128, DMC * TT], BF16)
                MT4 = RE("MT4", [128, TT], F32)
                VT4 = RE("VT4", [128, TT], F32)
                T14 = RE("T14", [128, TT], F32)
                RT4 = RE("RT4", [128, TT], F32)
                OSB = RE("OSB", [128, DMC * TT], F32)
                PS4 = _alloc(PE_, nc.psum_tensor("PS4", [128, TT], F32))
                PQ4 = _alloc(PE_, nc.psum_tensor("PQ4", [128, TT], F32))

                for ti in range(NT):
                    for hf in range(2):
                        dma("sync",
                            HD[:].rearrange("p (c t) -> p c t", c=DMC)
                            [:, :, hf * 256:(hf + 1) * 256],
                            h_hbm[2 * ti + hf].rearrange("c p t -> p c t"),
                            r=[h_hbm], w=[HD])
                    for c in range(DMC):
                        on("scalar", (lambda c, ti: lambda E: E.activation(
                            ACB[:, c * TT:(c + 1) * TT],
                            ACC[:, c * tok + ti * TT:c * tok + (ti + 1) * TT],
                            AF.Copy))(c, ti),
                            r=[f"ACC.{c}.{2 * ti}", f"ACC.{c}.{2 * ti + 1}"], w=[ACB])
                    for c in range(DMC):
                        on("scalar", (lambda c: lambda E: E.activation(
                            SQD[:, c * TT:(c + 1) * TT], ACB[:, c * TT:(c + 1) * TT],
                            AF.Square))(c), r=[ACB], w=[SQD])
                    for c in range(DMC):
                        on("tensor", (lambda c: lambda E: E.matmul(
                            PS4[:], ONESB[:], ACB[:, c * TT:(c + 1) * TT],
                            start=(c == 0), stop=(c == DMC - 1)))(c),
                            r=[ONESB, ACB], w=[PS4], defer=(c < DMC - 1))
                    for c in range(DMC):
                        on("tensor", (lambda c: lambda E: E.matmul(
                            PQ4[:], ONESB[:], SQD[:, c * TT:(c + 1) * TT],
                            start=(c == 0), stop=(c == DMC - 1)))(c),
                            r=[ONESB, SQD], w=[PQ4], defer=(c < DMC - 1))
                    on("vector", lambda E: E.tensor_scalar(
                        MT4[:], PS4[:], 1.0 / DM, None, ALU.mult), r=[PS4], w=[MT4])
                    on("vector", lambda E: E.tensor_scalar(
                        VT4[:], PQ4[:], 1.0 / DM, None, ALU.mult), r=[PQ4], w=[VT4])
                    on("vector", lambda E: E.tensor_mul(T14[:], MT4[:], MT4[:]),
                       r=[MT4], w=[T14])
                    on("vector", lambda E: E.tensor_sub(VT4[:], VT4[:], T14[:]),
                       r=[VT4, T14], w=[VT4])
                    on("vector", lambda E: E.tensor_scalar(
                        VT4[:], VT4[:], LN_EPS, None, ALU.add), r=[VT4], w=[VT4])
                    on("vector", lambda E: E.reciprocal(T14[:], VT4[:]), r=[VT4], w=[T14])
                    on("scalar", lambda E: E.activation(RT4[:], T14[:], AF.Sqrt),
                       r=[T14], w=[RT4])
                    for c in range(DMC):
                        on("vector", (lambda c, ti: lambda E: E.tensor_sub(
                            VT4[:], ACC[:, c * tok + ti * TT:c * tok + (ti + 1) * TT],
                            MT4[:]))(c, ti),
                            r=[f"ACC.{c}.{2 * ti}", f"ACC.{c}.{2 * ti + 1}", MT4],
                            w=[VT4])
                        on("vector", lambda E: E.tensor_mul(VT4[:], VT4[:], RT4[:]),
                           r=[VT4, RT4], w=[VT4])
                        on("vector", (lambda c: lambda E: E.tensor_add(
                            OSB[:, c * TT:(c + 1) * TT], VT4[:],
                            HD[:, c * TT:(c + 1) * TT]))(c),
                            r=[VT4, HD], w=[OSB])
                    dma("sync", outT[:, ti * TT:(ti + 1) * TT].rearrange(
                        "(c p) t -> p c t", p=128), cview(OSB, DMC),
                        r=[OSB], w=[outT])

        except _Stop:
            pass
        on("sync", lambda E: E.nop(),
           waits=[("sy", now("sy")), ("gq", now("gq"))], defer=True)
        on("gpsimd", lambda E: E.nop(),
           waits=[("gq", now("gq")), ("sy", now("sy"))], defer=True)

        with nc.Block() as block:
            block.tensor(lambda E: [f(E) for f in streams["tensor"]])
            block.vector(lambda E: [f(E) for f in streams["vector"]])
            block.scalar(lambda E: [f(E) for f in streams["scalar"]])
            block.sync(lambda E: [f(E) for f in streams["sync"]])
            block.gpsimd(lambda E: [f(E) for f in streams["gpsimd"]])

    return nc


def make_aux():
    masks = []
    for c in range(NCORES):
        m = np.zeros((64, 8), np.float32)
        for k in range(4 * (c // 4), c):
            m[:, k] = 1.0
        masks.append(m)
    sel = np.zeros((8, 1024), np.float32)
    for e in range(8):
        sel[e, 128 * e:128 * (e + 1)] = 1.0
    return masks, sel.astype(BFNP), np.eye(128, dtype=np.float32)


def make_in_maps(inputs, tok=2048):
    x = np.asarray(inputs["L0_attn_x"], np.float32)
    B, S, _ = x.shape
    xf = x.reshape(B * S, DM)
    masks, sel, idm = make_aux()
    shared = {
        "W_in": np.asarray(inputs["L0_attn_W_in"], np.float32),
        "b_in": np.asarray(inputs["L0_attn_b_in"], np.float32).reshape(32, 128),
        "W_xp": np.asarray(inputs["L0_attn_W_xp"], np.float32),
        "W_dt": np.asarray(inputs["L0_attn_W_dt"], np.float32),
        "b_dt": np.asarray(inputs["L0_attn_b_dt"], np.float32).reshape(8, 128),
        "W_out": np.asarray(inputs["L0_attn_W_out"], np.float32),
        "b_out": np.asarray(inputs["L0_attn_b_out"], np.float32).reshape(8, 128),
        "bxa": np.stack([np.asarray(inputs["L0_attn_b_xp"], np.float32),
                         np.asarray(inputs["L0_attn_A_log"], np.float32)]),
        "Dv": np.asarray(inputs["L0_attn_D"], np.float32).reshape(8, 128),
        "gate_W": np.asarray(inputs["L0_ffn_gate_W"], np.float32),
        "W1": np.asarray(inputs["L0_ffn_W1"], np.float32),
        "b1": np.asarray(inputs["L0_ffn_b1"], np.float32).reshape(NE, 16, 128),
        "W2": np.asarray(inputs["L0_ffn_W2"], np.float32),
        "b2": np.asarray(inputs["L0_ffn_b2"], np.float32).reshape(NE, 8, 128),
        "idm": idm, "sel8": sel,
    }
    in_maps = []
    for c in range(NCORES):
        sl = xf[c * tok:(c + 1) * tok]
        in_maps.append({**shared,
                        "xT": np.ascontiguousarray(sl.T),
                        "maskI": masks[c]})
    return in_maps


def kernel(**inputs):
    tok = 2048
    nc = build_nc(tok)
    in_maps = make_in_maps(inputs, tok)
    res = run_bass_kernel_spmd(nc, in_maps, core_ids=list(range(NCORES)))
    x = np.asarray(inputs["L0_attn_x"])
    B, S, _ = x.shape
    parts = [np.asarray(res.results[c]["outT"]).T for c in range(NCORES)]
    out = np.concatenate(parts, axis=0).reshape(B, S, DM).astype(np.float32)
    return out
